# version 04
# baseline (speedup 1.0000x reference)
import numpy as np
import ml_dtypes
from contextlib import ExitStack

import concourse.bass as bass
import concourse.mybir as mybir
from concourse.bass_utils import run_bass_kernel_spmd

F32 = mybir.dt.float32
BF16 = mybir.dt.bfloat16
AF = mybir.ActivationFunctionType
ALU = mybir.AluOpType
AX = mybir.AxisListType

S = 4096
D = 1024
NT = 32
NCH = 8
CT = 4
HA = 16
FA = 128
HB = 8
DB = 64
INNER = 512
HID = 2730
EPS = 1e-5
MAGIC = float(3 * 2.0 ** 22)
INV2PI = float(1.0 / (2 * np.pi))
NEG2PI = float(-2 * np.pi)
ZCLIP = float(1e-6 * FA)
NEG_BIG = -1e30


def _host_consts():
    c = {}
    c["ident"] = np.eye(128, dtype=np.float32).astype(ml_dtypes.bfloat16)
    st = np.zeros((128, 1), np.float32)
    st[0:64, 0] = 0.25
    c["shift_tbl"] = st
    t = np.arange(S, dtype=np.float64)
    half = np.arange(0, DB, 2, dtype=np.float64)
    inv_freq = 1.0 / (10000.0 ** (half / DB))
    freqs = t[:, None] * inv_freq[None, :]
    freqs = np.concatenate([freqs, freqs], axis=1)
    base = (half + 0.4 * DB) / (1.4 * DB)
    power = (t - S // 2) / float(512 // 2)
    scale = base[None, :] ** power[:, None]
    scale = np.concatenate([scale, scale], axis=1)
    cos, sin = np.cos(freqs), np.sin(freqs)
    sgn = np.ones((1, DB)); sgn[0, 0:32] = -1.0
    c["t1q"] = (cos * scale * 8.0).astype(np.float32)
    c["t2q"] = (sin * scale * 8.0 * sgn).astype(np.float32)
    c["t1k"] = (cos / scale).astype(np.float32)
    c["t2k"] = (sin / scale * sgn).astype(np.float32)
    ii = np.arange(128)[:, None]
    jj = np.arange(128)[None, :]
    c["m_ge"] = np.where(jj >= ii, 0.0, NEG_BIG).astype(np.float32)
    c["m_le"] = np.where(jj <= ii, 0.0, NEG_BIG).astype(np.float32)
    c["ones_bf"] = np.ones((128, 128), dtype=ml_dtypes.bfloat16)
    return c


class Ctr:
    def __init__(self, es, nc, name):
        self.sem = es.enter_context(nc.semaphore(name))
        self.n = 0

    def inc(self, instr, k=1):
        assert k > 0
        instr.then_inc(self.sem, k)
        self.n += k
        return self.n


class Env:
    pass


def _ln_stats_paranoid(E, src_ap, stt, sq_scr, nfeat, prev_guard=None):
    V, SC = E.V, E.SC
    DV, AC = E.DV, E.AC
    if prev_guard is not None:
        SC.wait_ge(prev_guard[0], prev_guard[1])
    AC.inc(SC.activation(out=sq_scr[:], in_=src_ap, func=AF.Square,
                         accum_out=stt[:, 1:2]))
    sq_ac = AC.n
    V.wait_ge(AC.sem, sq_ac)
    DV.inc(V.reduce_sum(out=stt[:, 0:1], in_=src_ap, axis=AX.X))
    V.wait_ge(DV.sem, DV.n)
    DV.inc(V.tensor_scalar_mul(stt[:, 2:3], stt[:, 0:1], 1.0 / nfeat))
    V.wait_ge(DV.sem, DV.n)
    DV.inc(V.tensor_mul(stt[:, 5:6], stt[:, 2:3], stt[:, 2:3]))
    V.wait_ge(DV.sem, DV.n)
    DV.inc(V.tensor_scalar_mul(stt[:, 6:7], stt[:, 1:2], 1.0 / nfeat))
    V.wait_ge(DV.sem, DV.n)
    DV.inc(V.tensor_scalar_add(stt[:, 6:7], stt[:, 6:7], EPS))
    V.wait_ge(DV.sem, DV.n)
    DV.inc(E.V.tensor_sub(stt[:, 6:7], stt[:, 6:7], stt[:, 5:6]))
    SC.wait_ge(DV.sem, DV.n)
    AC.inc(SC.activation(out=stt[:, 7:8], in_=stt[:, 6:7], func=AF.Sqrt))
    V.wait_ge(AC.sem, AC.n)
    DV.inc(V.reciprocal(stt[:, 3:4], stt[:, 7:8]))
    V.wait_ge(DV.sem, DV.n)
    DV.inc(V.tensor_mul(stt[:, 4:5], stt[:, 2:3], stt[:, 3:4]))
    V.wait_ge(DV.sem, DV.n)
    DV.inc(V.tensor_scalar_mul(stt[:, 4:5], stt[:, 4:5], -1.0))
    V.wait_ge(DV.sem, DV.n)
    return DV.n


def _ln_stats(E, src_ap, stt, sq_scr, nfeat, prev_guard=None):
    if True:
        return _ln_stats_paranoid(E, src_ap, stt, sq_scr, nfeat, prev_guard)
    """Emit LN stats for one [128, nfeat] tile (src may be PSUM or SBUF).
    Produces stt[:,2:3]=mu, stt[:,3:4]=rstd, stt[:,4:5]=-mu*rstd.
    prev_guard: (sem, val) the ACT Square must wait on (WAR for stt[:,1:2])."""
    V, SC = E.V, E.SC
    DV, AC = E.DV, E.AC
    if prev_guard is not None:
        SC.wait_ge(prev_guard[0], prev_guard[1])
    AC.inc(SC.activation(out=sq_scr[:], in_=src_ap, func=AF.Square,
                         accum_out=stt[:, 1:2]))
    sq_ac = AC.n
    V.wait_ge(AC.sem, sq_ac)
    V.reduce_sum(out=stt[:, 0:1], in_=src_ap, axis=AX.X)
    V.tensor_scalar_mul(stt[:, 2:3], stt[:, 0:1], 1.0 / nfeat)
    V.tensor_mul(stt[:, 5:6], stt[:, 2:3], stt[:, 2:3])
    V.tensor_scalar_mul(stt[:, 6:7], stt[:, 1:2], 1.0 / nfeat)
    V.tensor_scalar_add(stt[:, 6:7], stt[:, 6:7], EPS)
    DV.inc(E.V.tensor_sub(stt[:, 6:7], stt[:, 6:7], stt[:, 5:6]))
    SC.wait_ge(DV.sem, DV.n)
    AC.inc(SC.activation(out=stt[:, 7:8], in_=stt[:, 6:7], func=AF.Sqrt))
    V.wait_ge(AC.sem, AC.n)
    V.reciprocal(stt[:, 3:4], stt[:, 7:8])
    V.tensor_mul(stt[:, 4:5], stt[:, 2:3], stt[:, 3:4])
    DV.inc(V.tensor_scalar_mul(stt[:, 4:5], stt[:, 4:5], -1.0))
    return DV.n


def _pass_barrier(E):
    sems = [(E.PE, E.PE.n), (E.DV, E.DV.n), (E.AC, E.AC.n), (E.PO, E.PO.n),
            (E.LD, E.LD.n), (E.ST, E.ST.n)]
    for eng in (E.SY, E.GP, E.V, E.SC, E.TE):
        for ctr, val in sems:
            if val:
                eng.wait_ge(ctr.sem, val)


def _load_weights_bf16(E, pairs, stage, stage_cols=D):
    SY, GP, LD, PO = E.SY, E.GP, E.LD, E.PO
    base_po = PO.n
    for i, (src, dst) in enumerate(pairs):
        slot = i % 2
        if i >= 2:
            SY.wait_ge(PO.sem, base_po + i - 1)
        else:
            SY.wait_ge(PO.sem, base_po)
        LD.inc(SY.dma_start(out=stage[:, slot * stage_cols:(slot + 1) * stage_cols],
                            in_=src), 16)
        GP.wait_ge(LD.sem, LD.n)
        PO.inc(GP.tensor_copy(dst, stage[:, slot * stage_cols:(slot + 1) * stage_cols]))
    return PO.n


def build_nc(passes=("P0", "P1", "A0", "B0", "A1", "B1", "P6")):
    nc = bass.Bass("TRN2", target_bir_lowering=False)
    E = Env()
    E.nc = nc
    E.passes = passes

    E.x0 = nc.dram_tensor("x0", [S, D], F32, kind="ExternalInput").ap()
    E.wq_d = nc.dram_tensor("wq", [D, D], F32, kind="ExternalInput").ap()
    E.wk_d = nc.dram_tensor("wk", [D, D], F32, kind="ExternalInput").ap()
    E.wv_d = nc.dram_tensor("wv", [D, D], F32, kind="ExternalInput").ap()
    E.wo_d = nc.dram_tensor("wo_a", [D, D], F32, kind="ExternalInput").ap()
    E.rf_d = nc.dram_tensor("rf_dup", [128, FA], F32, kind="ExternalInput").ap()
    E.f_wqkv = nc.dram_tensor("f_wqkv", [2, D, 3 * INNER], F32, kind="ExternalInput").ap()
    E.f_qs = nc.dram_tensor("f_qs", [2, DB], F32, kind="ExternalInput").ap()
    E.f_ks = nc.dram_tensor("f_ks", [2, DB], F32, kind="ExternalInput").ap()
    E.f_wo = nc.dram_tensor("f_wo", [2, INNER, D], F32, kind="ExternalInput").ap()
    E.f_g = nc.dram_tensor("f_g", [2, D], F32, kind="ExternalInput").ap()
    E.f_b = nc.dram_tensor("f_b", [2, D], F32, kind="ExternalInput").ap()
    E.f_w1 = nc.dram_tensor("f_w1", [2, D, 2 * HID], F32, kind="ExternalInput").ap()
    E.f_b1 = nc.dram_tensor("f_b1", [2, 2 * HID], F32, kind="ExternalInput").ap()
    E.f_w2 = nc.dram_tensor("f_w2", [2, HID, D], F32, kind="ExternalInput").ap()
    E.f_b2 = nc.dram_tensor("f_b2", [2, D], F32, kind="ExternalInput").ap()
    E.ident_d = nc.dram_tensor("ident", [128, 128], BF16, kind="ExternalInput").ap()
    E.ones_d = nc.dram_tensor("ones_bf", [128, 128], BF16, kind="ExternalInput").ap()
    E.shift_d = nc.dram_tensor("shift_tbl", [128, 1], F32, kind="ExternalInput").ap()
    E.t1q_d = nc.dram_tensor("t1q", [S, DB], F32, kind="ExternalInput").ap()
    E.t2q_d = nc.dram_tensor("t2q", [S, DB], F32, kind="ExternalInput").ap()
    E.t1k_d = nc.dram_tensor("t1k", [S, DB], F32, kind="ExternalInput").ap()
    E.t2k_d = nc.dram_tensor("t2k", [S, DB], F32, kind="ExternalInput").ap()
    E.mge_d = nc.dram_tensor("m_ge", [128, 128], F32, kind="ExternalInput").ap()
    E.mle_d = nc.dram_tensor("m_le", [128, 128], F32, kind="ExternalInput").ap()

    E.out = nc.dram_tensor("out", [S, D], F32, kind="ExternalOutput").ap()
    E.xA = nc.dram_tensor("xA", [S, D], F32, kind="Internal").ap()
    E.x1k = nc.dram_tensor("x1k", [S, D], F32, kind="Internal").ap()

    es = ExitStack()
    E.es = es

    def sbuf(name, shape, dtype):
        return es.enter_context(nc.sbuf_tensor(name, shape, dtype))
    E.sbuf = sbuf

    E.ident_sb = sbuf("ident_sb", [128, 128], BF16)
    E.ones_sb = sbuf("ones_sb", [128, 128], BF16)
    E.shift_sb = sbuf("shift_sb", [128, 1], F32)
    E.mge_sb = sbuf("mge_sb", [128, 128], F32)
    E.mle_sb = sbuf("mle_sb", [128, 128], F32)
    E.cns = sbuf("cns", [128, 2], F32)
    nc.const_aps.register(F32, INV2PI, E.cns[:, 0:1])
    nc.const_aps.register(F32, NEG2PI, E.cns[:, 1:2])

    E.LD = Ctr(es, nc, "LD")
    E.ST = Ctr(es, nc, "ST")
    E.PE = Ctr(es, nc, "PE")
    E.DV = Ctr(es, nc, "DV")
    E.AC = Ctr(es, nc, "AC")
    E.PO = Ctr(es, nc, "PO")

    E.V, E.SC, E.TE, E.SY, E.GP = nc.vector, nc.scalar, nc.tensor, nc.sync, nc.gpsimd

    with nc.Block():
        E.LD.inc(E.SY.dma_start(out=E.ident_sb[:], in_=E.ident_d[:]), 16)
        E.LD.inc(E.SY.dma_start(out=E.ones_sb[:], in_=E.ones_d[:]), 16)
        E.LD.inc(E.SY.dma_start(out=E.shift_sb[:], in_=E.shift_d[:]), 16)
        E.LD.inc(E.SY.dma_start(out=E.mge_sb[:], in_=E.mge_d[:]), 16)
        E.LD.inc(E.SY.dma_start(out=E.mle_sb[:], in_=E.mle_d[:]), 16)
        E.V.memset(E.cns[:, 0:1], INV2PI)
        E.DV.inc(E.V.memset(E.cns[:, 1:2], NEG2PI))
        E.cns_dv = E.DV.n
        E.consts_ld = E.LD.n

        E.stageA_es = ExitStack()
        E.ln1_fm = E.stageA_es.enter_context(nc.sbuf_tensor("ln1_fm", [128, 8 * S], BF16))
        E.kv_sb = E.stageA_es.enter_context(nc.sbuf_tensor("kv_sb", [128, HA * 65], BF16))
        E.kv_done = 0

        if "P0" in passes:
            _emit_P0(E)
        p1lev = [p for p in passes if p.startswith("P1")]
        if p1lev:
            E.p1_level = int(p1lev[0][3:]) if "@" in p1lev[0] else 99
            _emit_P1(E)
        if any(("A%d" % li) in passes or ("B%d" % li) in passes for li in (0, 1)):
            E.stageA_es.close()
            E.stageA_closed = True
        for li in (0, 1):
            if ("A%d" % li) in passes:
                _emit_A(E, li)
            if ("B%d" % li) in passes:
                _emit_B(E, li)
        if "P6" in passes:
            _emit_P6(E)
        if "DBGKV" in passes:
            V, GP, SY, LD, ST, PO, DV = E.V, E.GP, E.SY, E.LD, E.ST, E.PO, E.DV
            tmp = E.stageA_es.enter_context(nc.sbuf_tensor("dbgkv_tmp", [128, 1040], F32))
            V.wait_ge(DV.sem, E.kv_done)
            DV.inc(V.tensor_copy(tmp[:], E.kv_sb[:]))
            GP.wait_ge(DV.sem, E.DV.n)
            ST.inc(GP.dma_start(out=E.out[0:128, 0:1024], in_=tmp[:, 0:1024]), 16)
            ST.inc(GP.dma_start(out=E.out[128:256, 0:16], in_=tmp[:, 1024:1040]), 16)
            GP.wait_ge(ST.sem, ST.n)
        if "DBGX1" in passes or "DBGXA" in passes:
            srcT = E.x1k if "DBGX1" in passes else E.xA
            V, GP, SY, LD, ST, PO = E.V, E.GP, E.SY, E.LD, E.ST, E.PO
            SY.wait_ge(ST.sem, ST.n)
            tmp = (sbuf("dbg_tmp", [128, D], F32)
                   if getattr(E, "stageA_closed", False)
                   else E.stageA_es.enter_context(nc.sbuf_tensor("dbg_tmp", [128, D], F32)))
            for i in range(NT):
                LD.inc(SY.dma_start(out=tmp[:], in_=srcT[i * 128:(i + 1) * 128, :]), 16)
                GP.wait_ge(LD.sem, LD.n)
                ST.inc(GP.dma_start(out=E.out[i * 128:(i + 1) * 128, :], in_=tmp[:]), 16)
                SY.wait_ge(ST.sem, ST.n)
            GP.wait_ge(ST.sem, ST.n)

    if not getattr(E, "stageA_closed", False):
        E.stageA_es.close()
    es.close()
    return nc


def _emit_P0(E):
    nc = E.nc
    V, SC, TE, SY, GP = E.V, E.SC, E.TE, E.SY, E.GP
    LD, ST, PE, DV, AC, PO = E.LD, E.ST, E.PE, E.DV, E.AC, E.PO

    p0 = ExitStack()

    def sb(name, shape, dtype):
        return p0.enter_context(nc.sbuf_tensor(name, shape, dtype))

    def ps(name, shape, dtype=F32):
        return p0.enter_context(nc.psum_tensor(name, shape, dtype))

    wk_bf = sb("wk_bf", [128, 8 * D], BF16)
    wv_bf = sb("wv_bf", [128, 8 * D], BF16)
    wstage = sb("p0_wstage", [128, 2 * D], F32)
    rf_sb = sb("rf_sb", [128, FA], F32)
    rf_bf = sb("rf_bf", [128, FA], BF16)

    x_in = sb("p0_x_in", [128, 2 * D], F32)
    ln1_tm = sb("p0_ln1_tm", [128, D], BF16)
    sq_scr = sb("p0_sq_scr", [128, D], BF16)
    stt = sb("p0_stt", [128, 8], F32)
    kfm = sb("p0_kfm", [128, 8 * 512], BF16)
    kp_fm = sb("p0_kp_fm", [128, 512], BF16)
    kp_tm = sb("p0_kp_tm", [128, HA * 512], BF16)
    v_tm = sb("p0_v_tm", [128, 4 * 1040], BF16)
    u0b = sb("p0_u0", [128, 512], F32)
    tb = sb("p0_t", [128, 512], F32)
    rb = sb("p0_r", [128, 512], F32)

    ps_a0 = ps("ps_a0", [128, 1024], F32)
    ps_a1 = ps("ps_a1", [128, 1024], F32)
    ps_b = ps("ps_b", [128, 1024], BF16)
    ps_kv = [ps("ps_kv0", [128, 455], F32), ps("ps_kv1", [128, 455], F32),
             ps("ps_kv2", [128, 130], F32)]

    def kv_slice(h):
        return ps_kv[h // 7][:, (h % 7) * 65:(h % 7) * 65 + 65]

    if not getattr(E, "p0_skip_weights", False):
        LD.inc(SY.dma_start(out=rf_sb[:], in_=rf_d_ap(E)), 16)
        GP.wait_ge(LD.sem, LD.n)
        PO.inc(GP.tensor_copy(rf_bf[:], rf_sb[:]))
        pairs = [(E.wk_d[i * 128:(i + 1) * 128, :], wk_bf[:, i * D:(i + 1) * D]) for i in range(8)]
        pairs += [(E.wv_d[i * 128:(i + 1) * 128, :], wv_bf[:, i * D:(i + 1) * D]) for i in range(8)]
        w_ready = _load_weights_bf16(E, pairs, wstage)
    else:
        w_ready = PO.n

    if not getattr(E, "p0_skip_memset", False):
        for tt in range(4):
            V.memset(v_tm[:, tt * 1040:(tt + 1) * 1040]
                     .rearrange("p (h s) -> p h s", s=65)[:, :, 64:65], 1.0)

    x_free_dv = {}
    evac = {}
    kv_started = [False, False, False]
    prev_norm_guard = 0
    prev_sin_guard = 0
    prev_sq_guard = 0

    p0_nch = getattr(E, "p0_nch", NCH)
    p0_tiles = getattr(E, "p0_tiles", CT)
    for c in range(p0_nch):
        c0 = c * 512
        for j in range(p0_tiles):
            i = c * CT + j
            slot = i % 2
            if i >= 2:
                SY.wait_ge(DV.sem, x_free_dv[i - 2])
            LD.inc(SY.dma_start(out=x_in[:, slot * D:slot * D + D],
                                in_=E.x0[i * 128:(i + 1) * 128, :]), 16)
            x_ld = LD.n

            src = x_in[:, slot * D:slot * D + D]
            SC.wait_ge(LD.sem, x_ld)
            V.wait_ge(LD.sem, x_ld)
            _ln_stats(E, src, stt, sq_scr, D,
                      prev_guard=(DV.sem, prev_sq_guard) if prev_sq_guard else None)
            if prev_norm_guard:
                V.wait_ge(PE.sem, prev_norm_guard)
            DV.inc(V.tensor_scalar(out=ln1_tm[:], in0=src,
                                   scalar1=stt[:, 2:3], scalar2=stt[:, 3:4],
                                   op0=ALU.subtract, op1=ALU.mult))
            x_free_dv[i] = DV.n
            prev_sq_guard = DV.n
            norm_dv = DV.n

            TE.wait_ge(DV.sem, norm_dv)
            TE.wait_ge(LD.sem, E.consts_ld)
            if "ps_b" in evac:
                TE.wait_ge(DV.sem, evac["ps_b"])
            for ft in range(8):
                mm = TE.transpose(ps_b[:, ft * 128:(ft + 1) * 128],
                                  ln1_tm[:, ft * 128:(ft + 1) * 128], E.ident_sb[:])
                if ft == 7:
                    PE.inc(mm)
            prev_norm_guard = PE.n
            V.wait_ge(PE.sem, PE.n)
            DV.inc(V.tensor_copy(
                E.ln1_fm[:].rearrange("p (f t) -> p f t", t=S)[:, :, i * 128:(i + 1) * 128],
                ps_b[:].rearrange("p (f t) -> p f t", t=128)))
            evac["ps_b"] = DV.n

        ln1_ready = DV.n
        if getattr(E, "p0_skip_rest", False):
            continue

        TE.wait_ge(DV.sem, ln1_ready)
        if c == 0:
            TE.wait_ge(PO.sem, w_ready)
        for ft in range(8):
            half = [ps_a0, ps_a1][ft % 2][:, 0:512]
            key = ("ak", ft % 2)
            if key in evac:
                TE.wait_ge(DV.sem, evac[key])
            if ft % 2 == 0 and ("v",) in evac:
                TE.wait_ge(DV.sem, evac[("v",)])
            for kt in range(8):
                mm = TE.matmul(half,
                               wk_bf[:, kt * D + ft * 128: kt * D + (ft + 1) * 128],
                               E.ln1_fm[:, kt * S + c0: kt * S + c0 + 512],
                               start=(kt == 0), stop=(kt == 7))
                if kt == 7:
                    PE.inc(mm)
            V.wait_ge(PE.sem, PE.n)
            DV.inc(V.tensor_copy(kfm[:, ft * 512:(ft + 1) * 512], half))
            evac[key] = DV.n
        kfm_ready = DV.n

        for tt in range(CT):
            TE.wait_ge(DV.sem, evac[("ak", 0)])
            if ("v",) in evac:
                TE.wait_ge(DV.sem, evac[("v",)])
            for kt in range(8):
                for nh in range(2):
                    mm = TE.matmul(ps_a0[:, nh * 512:(nh + 1) * 512],
                                   E.ln1_fm[:, kt * S + c0 + tt * 128: kt * S + c0 + (tt + 1) * 128],
                                   wv_bf[:, kt * D + nh * 512: kt * D + (nh + 1) * 512],
                                   start=(kt == 0), stop=(kt == 7))
                    if kt == 7 and nh == 1:
                        PE.inc(mm)
            V.wait_ge(PE.sem, PE.n)
            DV.inc(V.tensor_copy(
                v_tm[:, tt * 1040:(tt + 1) * 1040]
                .rearrange("p (h s) -> p h s", s=65)[:, :, 0:64],
                ps_a0[:].rearrange("p (h d) -> p h d", d=64)))
            evac[("v",)] = DV.n

        for h in range(HA):
            base = (h % 2) * 64
            proj = ps_a1[:, 512:1024]
            TE.wait_ge(DV.sem, kfm_ready)
            if ("projk",) in evac:
                TE.wait_ge(AC.sem, evac[("projk",)])
            PE.inc(TE.matmul(proj, rf_bf[base:base + 64, :],
                             kfm[base:base + 64, (h // 2) * 512:(h // 2) * 512 + 512],
                             start=True, stop=True))
            SC.wait_ge(PE.sem, PE.n)
            SC.wait_ge(DV.sem, max(E.cns_dv, 0))
            SC.wait_ge(LD.sem, E.consts_ld)
            AC.inc(SC.activation(out=u0b[:], in_=proj, func=AF.Identity,
                                 scale=INV2PI, bias=E.shift_sb[:, 0:1]))
            evac[("projk",)] = AC.n
            V.wait_ge(AC.sem, AC.n)
            DV.inc(V.tensor_scalar(tb[:], u0b[:], MAGIC, None, ALU.add))
            V.wait_ge(DV.sem, DV.n)
            DV.inc(V.scalar_tensor_tensor(out=rb[:], in0=tb[:], scalar=MAGIC,
                                          in1=u0b[:], op0=ALU.subtract,
                                          op1=ALU.subtract))
            SC.wait_ge(DV.sem, DV.n)
            if prev_sin_guard:
                SC.wait_ge(PE.sem, prev_sin_guard)
            AC.inc(SC.activation(out=kp_fm[:], in_=rb[:], func=AF.Sin, scale=NEG2PI))
            kp_ac = AC.n

            TE.wait_ge(AC.sem, kp_ac)
            TE.wait_ge(DV.sem, evac["ps_b"])
            for tt in range(CT):
                mm = TE.transpose(ps_b[:, tt * 128:(tt + 1) * 128],
                                  kp_fm[:, tt * 128:(tt + 1) * 128], E.ident_sb[:])
                if tt == 3:
                    PE.inc(mm)
            V.wait_ge(PE.sem, PE.n)
            DV.inc(V.tensor_copy(kp_tm[:, h * 512:(h + 1) * 512], ps_b[:, 0:512]))
            evac["ps_b"] = DV.n
            kpt_dv = DV.n

            TE.wait_ge(DV.sem, kpt_dv)
            bank = h // 7
            for tt in range(CT):
                st_flag = not kv_started[bank]
                kv_started[bank] = True
                mm = TE.matmul(kv_slice(h),
                               kp_tm[:, h * 512 + tt * 128: h * 512 + (tt + 1) * 128],
                               v_tm[:, tt * 1040 + h * 65: tt * 1040 + h * 65 + 65],
                               start=st_flag,
                               stop=(c == NCH - 1 and tt == CT - 1))
                if tt == CT - 1:
                    PE.inc(mm)
            prev_sin_guard = PE.n

    V.wait_ge(PE.sem, PE.n)
    DV.inc(V.tensor_copy(E.kv_sb[:, 0:455], ps_kv[0][:]))
    DV.inc(V.tensor_copy(E.kv_sb[:, 455:910], ps_kv[1][:]))
    DV.inc(V.tensor_copy(E.kv_sb[:, 910:1040], ps_kv[2][:]))
    E.kv_done = DV.n
    dbg = getattr(E, "p0_dbg", None)
    if dbg:
        tmp = sb("p0dbg_tmp", [128, 1024], F32)
        row = 0
        for name in dbg:
            buf = {"ln1": E.ln1_fm, "kfm": kfm, "kp": kp_tm, "v": v_tm,
                   "lntm": ln1_tm, "stt": stt, "xin": x_in}[name]
            ncols = {"ln1": 4096, "kfm": 4096, "kp": 4096, "v": 4160,
                     "lntm": 1024, "stt": 8, "xin": 2048}[name]
            for blk in range(0, min(ncols, 4096), 1024):
                w = min(1024, ncols - blk)
                V.wait_ge(DV.sem, DV.n)
                V.wait_ge(ST.sem, ST.n)
                V.memset(tmp[:], 0.0)
                DV.inc(V.tensor_copy(tmp[:, 0:w], buf[:, blk:blk + w]))
                GP.wait_ge(DV.sem, DV.n)
                ST.inc(GP.dma_start(out=E.out[row * 128:(row + 1) * 128, :], in_=tmp[:]), 16)
                GP.wait_ge(ST.sem, ST.n)
                row += 1
    p0.close()


def rf_d_ap(E):
    return E.rf_d[:]


def _emit_P1(E):
    nc = E.nc
    V, SC, TE, SY, GP = E.V, E.SC, E.TE, E.SY, E.GP
    LD, ST, PE, DV, AC, PO = E.LD, E.ST, E.PE, E.DV, E.AC, E.PO

    p1 = ExitStack()

    def sb(name, shape, dtype):
        return p1.enter_context(nc.sbuf_tensor(name, shape, dtype))

    def ps(name, shape, dtype=F32):
        return p1.enter_context(nc.psum_tensor(name, shape, dtype))

    _pass_barrier(E)
    wq_bf = sb("wq_bf", [128, 8 * D], BF16)
    wo_bf = sb("wo_bf", [128, 8 * D], BF16)
    wstage = sb("p1_wstage", [128, 2 * D], F32)
    rf_sb = sb("p1_rf_sb", [128, FA], F32)
    rf_bf = sb("p1_rf_bf", [128, FA], BF16)

    qfm = sb("p1_qfm", [128, 8 * 512], BF16)
    qp_all = sb("p1_qp", [128, HA * 512], BF16)
    u0b = sb("p1_u0", [128, 512], F32)
    tb = sb("p1_t", [128, 512], F32)
    rb = sb("p1_r", [128, 512], F32)
    zinv = sb("p1_zinv", [128, 16], F32)
    attn_tm = sb("p1_attn_tm", [128, D], BF16)
    attn_fm = sb("p1_attn_fm", [128, D], BF16)
    x0_in = sb("p1_x0_in", [128, 2 * D], F32)
    ln2_sb = sb("p1_ln2", [128, D], F32)
    x1_sb = sb("p1_x1", [128, D], F32)
    xa_sb = sb("p1_xa", [128, D], F32)
    sq_scr = sb("p1_sq_scr", [128, D], BF16)
    stt = sb("p1_stt", [128, 8], F32)

    ps_q = [ps("ps_q0", [128, 512], F32), ps("ps_q1", [128, 512], F32)]
    ps_qkv = ps("ps_qkv", [128, 1024], F32)
    ps_z = ps("ps_z", [128, 16], F32)
    ps_b = ps("p1_ps_b", [128, 1024], BF16)
    ps_wo = ps("ps_wo", [128, 1024], F32)

    LD.inc(SY.dma_start(out=rf_sb[:], in_=E.rf_d[:]), 16)
    GP.wait_ge(LD.sem, LD.n)
    PO.inc(GP.tensor_copy(rf_bf[:], rf_sb[:]))
    pairs = [(E.wq_d[i * 128:(i + 1) * 128, :], wq_bf[:, i * D:(i + 1) * D]) for i in range(8)]
    pairs += [(E.wo_d[i * 128:(i + 1) * 128, :], wo_bf[:, i * D:(i + 1) * D]) for i in range(8)]
    w_ready = _load_weights_bf16(E, pairs, wstage)

    evac = {}
    x0_free = {}
    prev_qkv_pe = 0
    prev_sq_guard = 0
    prev_store_guard = 0

    lev = getattr(E, "p1_level", 99)
    for c in range(NCH):
        c0 = c * 512
        if lev < 10:
            break
        if c == 0:
            TE.wait_ge(PO.sem, w_ready)
            if E.kv_done:
                TE.wait_ge(DV.sem, E.kv_done)
        for ft in range(8):
            half = ps_q[ft % 2][:]
            key = ("q", ft % 2)
            if key in evac:
                TE.wait_ge(DV.sem, evac[key])
            if ft == 0 and ("proj",) in evac:
                TE.wait_ge(AC.sem, evac[("proj",)])
            for kt in range(8):
                mm = TE.matmul(half,
                               wq_bf[:, kt * D + ft * 128: kt * D + (ft + 1) * 128],
                               E.ln1_fm[:, kt * S + c0: kt * S + c0 + 512],
                               start=(kt == 0), stop=(kt == 7))
                if kt == 7:
                    PE.inc(mm)
            V.wait_ge(PE.sem, PE.n)
            DV.inc(V.tensor_copy(qfm[:, ft * 512:(ft + 1) * 512], half))
            evac[key] = DV.n
        qfm_ready = DV.n

        for h in range(HA if lev >= 20 else 0):
            base = (h % 2) * 64
            TE.wait_ge(DV.sem, qfm_ready if h == 0 else DV.n)
            if h == 0:
                TE.wait_ge(DV.sem, evac[("q", 0)])
            if ("proj",) in evac:
                TE.wait_ge(AC.sem, evac[("proj",)])
            PE.inc(TE.matmul(ps_q[0][:], rf_bf[base:base + 64, :],
                             qfm[base:base + 64, (h // 2) * 512:(h // 2) * 512 + 512],
                             start=True, stop=True))
            SC.wait_ge(PE.sem, PE.n)
            if prev_qkv_pe and h == 0:
                SC.wait_ge(PE.sem, prev_qkv_pe)
            SC.wait_ge(DV.sem, max(E.cns_dv, 0))
            SC.wait_ge(LD.sem, E.consts_ld)
            AC.inc(SC.activation(out=u0b[:], in_=ps_q[0][:], func=AF.Identity,
                                 scale=INV2PI, bias=E.shift_sb[:, 0:1]))
            evac[("proj",)] = AC.n
            V.wait_ge(AC.sem, AC.n)
            DV.inc(V.tensor_scalar(tb[:], u0b[:], MAGIC, None, ALU.add))
            V.wait_ge(DV.sem, DV.n)
            DV.inc(V.scalar_tensor_tensor(out=rb[:], in0=tb[:], scalar=MAGIC,
                                          in1=u0b[:], op0=ALU.subtract,
                                          op1=ALU.subtract))
            SC.wait_ge(DV.sem, DV.n)
            AC.inc(SC.activation(out=qp_all[:, h * 512:(h + 1) * 512], in_=rb[:],
                                 func=AF.Sin, scale=NEG2PI))
        qp_ready = AC.n

        for tt in range(CT if lev >= 30 else 0):
            i = c * CT + tt
            slot = i % 2
            if i >= 2 and (i - 2) in x0_free:
                SY.wait_ge(PO.sem, x0_free[i - 2])
            LD.inc(SY.dma_start(out=x0_in[:, slot * D:slot * D + D],
                                in_=E.x0[i * 128:(i + 1) * 128, :]), 16)
            x_ld = LD.n

            TE.wait_ge(AC.sem, qp_ready)
            if ("qkv",) in evac:
                TE.wait_ge(DV.sem, evac[("qkv",)])
            for h in range(HA):
                mm = None
                if lev != 32:
                    mm = TE.matmul(ps_qkv[:, h * 64:(h + 1) * 64],
                                   qp_all[:, h * 512 + tt * 128: h * 512 + (tt + 1) * 128],
                                   E.kv_sb[:, h * 65: h * 65 + 64],
                                   start=True, stop=True)
                if lev != 31:
                    mm = TE.matmul(ps_z[:, h:h + 1],
                                   qp_all[:, h * 512 + tt * 128: h * 512 + (tt + 1) * 128],
                                   E.kv_sb[:, h * 65 + 64: h * 65 + 65],
                                   start=True, stop=True)
                if h == HA - 1:
                    PE.inc(mm)
            qkv_pe = PE.n
            prev_qkv_pe = qkv_pe
            V.wait_ge(PE.sem, qkv_pe)
            if lev >= 35:
                DV.inc(V.tensor_scalar_max(zinv[:], ps_z[:], ZCLIP))
                V.wait_ge(DV.sem, DV.n)
                DV.inc(V.reciprocal(zinv[:], zinv[:]))
                V.wait_ge(DV.sem, DV.n)
                DV.inc(V.tensor_tensor(
                    out=attn_tm[:].rearrange("p (h d) -> p h d", d=64),
                    in0=ps_qkv[:].rearrange("p (h d) -> p h d", d=64),
                    in1=zinv[:, :, None].broadcast_to([128, 16, 64]),
                    op=ALU.mult))
            else:
                DV.inc(V.tensor_copy(attn_tm[:], ps_qkv[:]))
            evac[("qkv",)] = DV.n
            attn_dv = DV.n

            if lev < 40:
                continue
            TE.wait_ge(DV.sem, attn_dv)
            TE.wait_ge(LD.sem, E.consts_ld)
            if ("ps_b",) in evac:
                TE.wait_ge(DV.sem, evac[("ps_b",)])
            for ft in range(8):
                mm = TE.transpose(ps_b[:, ft * 128:(ft + 1) * 128],
                                  attn_tm[:, ft * 128:(ft + 1) * 128], E.ident_sb[:])
                if ft == 7:
                    PE.inc(mm)
            V.wait_ge(PE.sem, PE.n)
            DV.inc(V.tensor_copy(attn_fm[:], ps_b[:]))
            evac[("ps_b",)] = DV.n
            afm_dv = DV.n

            if lev < 50:
                continue
            TE.wait_ge(DV.sem, afm_dv)
            if ("wo",) in evac:
                TE.wait_ge(AC.sem, evac[("wo",)])
            for kt in range(8):
                for nh in range(2):
                    mm = TE.matmul(ps_wo[:, nh * 512:(nh + 1) * 512],
                                   attn_fm[:, kt * 128:(kt + 1) * 128],
                                   wo_bf[:, kt * D + nh * 512: kt * D + (nh + 1) * 512],
                                   start=(kt == 0), stop=(kt == 7))
                    if kt == 7 and nh == 1:
                        PE.inc(mm)
            wo_pe = PE.n

            if lev < 61:
                continue
            SC.wait_ge(PE.sem, wo_pe)
            V.wait_ge(PE.sem, wo_pe)
            _ln_stats(E, ps_wo[:], stt, sq_scr, D,
                      prev_guard=(DV.sem, prev_sq_guard) if prev_sq_guard else None)
            stat_dv = DV.n
            prev_sq_guard = DV.n
            if lev < 62:
                evac[("wo",)] = 0
                continue
            SC.wait_ge(DV.sem, stat_dv)
            AC.inc(SC.activation(out=ln2_sb[:], in_=ps_wo[:], func=AF.Identity,
                                 scale=stt[:, 3:4], bias=stt[:, 4:5]))
            evac[("wo",)] = AC.n
            ln2_ac = AC.n
            if lev < 63:
                continue

            GP.wait_ge(AC.sem, ln2_ac)
            GP.wait_ge(LD.sem, x_ld)
            if prev_store_guard:
                GP.wait_ge(ST.sem, prev_store_guard)
            PO.inc(GP.tensor_add(x1_sb[:], x0_in[:, slot * D:slot * D + D], ln2_sb[:]))
            x1_po = PO.n
            x0_free[i] = PO.n
            ST.inc(GP.dma_start(out=E.x1k[i * 128:(i + 1) * 128, :], in_=x1_sb[:]), 16)
            prev_store_guard = ST.n
            if lev < 64:
                continue

            V.wait_ge(PO.sem, x1_po)
            SC.wait_ge(PO.sem, x1_po)
            _ln_stats(E, x1_sb[:], stt, sq_scr, D, prev_guard=(DV.sem, stat_dv))
            prev_sq_guard = DV.n
            GP.wait_ge(DV.sem, DV.n)
            PO.inc(GP.tensor_scalar(out=xa_sb[:], in0=x1_sb[:],
                                    scalar1=stt[:, 2:3], scalar2=stt[:, 3:4],
                                    op0=ALU.subtract, op1=ALU.mult))
            ST.inc(GP.dma_start(out=E.xA[i * 128:(i + 1) * 128, :], in_=xa_sb[:]), 16)
            prev_store_guard = ST.n

    dbg = getattr(E, "p1_dbg", None)
    if dbg:
        tmp = sb("p1dbg_tmp", [128, 1024], F32)
        row = 0
        for name in dbg:
            buf = {"qfm": qfm, "qp": qp_all, "attn_tm": attn_tm, "attn_fm": attn_fm,
                   "zinv": zinv, "stt": stt, "ln2": ln2_sb, "x1": x1_sb, "xa": xa_sb}[name]
            ncols = {"qfm": 4096, "qp": 4096, "attn_tm": 1024, "attn_fm": 1024,
                     "zinv": 16, "stt": 8, "ln2": 1024, "x1": 1024, "xa": 1024}[name]
            for blk in range(0, min(ncols, 4096), 1024):
                w = min(1024, ncols - blk)
                V.wait_ge(DV.sem, DV.n)
                V.wait_ge(ST.sem, ST.n)
                V.wait_ge(PO.sem, PO.n)
                V.wait_ge(AC.sem, AC.n)
                V.memset(tmp[:], 0.0)
                DV.inc(V.tensor_copy(tmp[:, 0:w], buf[:, blk:blk + w]))
                GP.wait_ge(DV.sem, DV.n)
                ST.inc(GP.dma_start(out=E.out[row * 128:(row + 1) * 128, :], in_=tmp[:]), 16)
                GP.wait_ge(ST.sem, ST.n)
                row += 1
    p1.close()


def _emit_A(E, li):
    nc = E.nc
    V, SC, TE, SY, GP = E.V, E.SC, E.TE, E.SY, E.GP
    LD, ST, PE, DV, AC, PO = E.LD, E.ST, E.PE, E.DV, E.AC, E.PO

    pa = ExitStack()

    def sb(name, shape, dtype):
        return pa.enter_context(nc.sbuf_tensor("a%d_%s" % (li, name), shape, dtype))

    def ps(name, shape, dtype=F32):
        return pa.enter_context(nc.psum_tensor("a%d_%s" % (li, name), shape, dtype))

    _pass_barrier(E)

    wqkv_bf = sb("wqkv", [128, 8 * 1536], BF16)
    wqkv_lo = sb("wqkv_lo", [128, 8 * 1536], BF16)
    wo_bf = sb("wo", [128, 4 * 1024], BF16)
    wstage = sb("wstage", [128, 2 * 1536], F32)
    tq1 = sb("tq1", [128, 32 * 64], F32)
    tq2 = sb("tq2", [128, 32 * 64], F32)
    tk1 = sb("tk1", [128, 32 * 64], F32)
    tk2 = sb("tk2", [128, 32 * 64], F32)
    qs_sb = sb("qs", [1, 64], F32)
    ones_f = sb("ones_f", [1, 128], F32)
    ks_sb = sb("ks", [1, 64], F32)
    sc_mat = sb("sc_mat", [128, 4 * 64], F32)
    kfm_roll = sb("kroll", [128, 5 * 512], BF16)
    v_roll = sb("vroll", [128, 5 * 512], BF16)
    x_tm = sb("x_tm", [128, 2 * D], F32)
    x_bf = sb("x_bf", [128, 2 * D], BF16)
    x_fm = sb("x_fm", [128, 2 * 8 * 129], BF16)
    q1b = sb("q1", [128, 512], F32)
    k1b = sb("k1", [128, 512], F32)
    rta = sb("rta", [128, 512], F32)
    rtb = sb("rtb", [128, 512], F32)
    q2b = sb("q2", [128, 512], BF16)
    k2b = sb("k2", [128, 512], BF16)
    q_fmc = sb("qfmc", [128, 512], BF16)
    attn_f = sb("attnf", [128, 640], F32)
    attn_sb = sb("attn", [128, 640], BF16)
    attn_fm = sb("attnfm", [128, 640], BF16)
    o_fm = sb("ofm", [128, 512], BF16)
    x_out = sb("x_out", [128, 2 * D], F32)
    stt = sb("stt", [128, 24], F32)
    ident_f = sb("ident_f", [128, 128], F32)
    q2f = sb("q2f", [128, 512], F32)
    k2f = sb("k2f", [128, 512], F32)
    qfm_f = sb("qfm_f", [128, 512], F32)
    kf_roll = sb("kf_roll", [128, 2 * 512], F32)
    vf_roll = sb("vf_roll", [128, 2 * 512], F32)
    attn_nf = sb("attn_nf", [128, 256], F32)
    afm_f = sb("afm_f", [128, 256], F32)
    xlo_bf = sb("xlo_bf", [128, D], BF16)
    xlo_f = sb("xlo_f", [128, D], F32)
    x_fm_lo = sb("x_fm_lo", [128, 2 * 8 * 129], BF16)

    ps_big = ps("big", [128, 1536], F32)
    ps_b = ps("trp", [128, 1024], BF16)
    ps_o = [ps("o0", [64, 512], F32), ps("o1", [64, 512], F32)]
    ps_qm = ps("qm", [128, 64], F32)
    ps_tf = ps("tf", [128, 512], F32)

    base_po = PO.n
    for i in range(8):
        slot = i % 2
        if i >= 2:
            SY.wait_ge(PO.sem, base_po + 2 * (i - 1))
        else:
            SY.wait_ge(PO.sem, base_po)
        LD.inc(SY.dma_start(out=wstage[:, slot * 1536:(slot + 1) * 1536],
                            in_=E.f_wqkv[li, i * 128:(i + 1) * 128, :]), 16)
        GP.wait_ge(LD.sem, LD.n)
        PO.inc(GP.tensor_copy(wqkv_bf[:, i * 1536:(i + 1) * 1536],
                              wstage[:, slot * 1536:(slot + 1) * 1536]))
        GP.wait_ge(PO.sem, PO.n)
        PO.inc(GP.tensor_tensor(out=wqkv_lo[:, i * 1536:(i + 1) * 1536],
                                in0=wstage[:, slot * 1536:(slot + 1) * 1536],
                                in1=wqkv_bf[:, i * 1536:(i + 1) * 1536],
                                op=ALU.subtract))
    w_ready = PO.n
    pairs = [(E.f_wo[li, i * 128:(i + 1) * 128, :], wo_bf[:, i * 1024:(i + 1) * 1024])
             for i in range(4)]
    w_ready = _load_weights_bf16(E, pairs, wstage, stage_cols=1024)

    for tbl_d, tbl in ((E.t1q_d, tq1), (E.t2q_d, tq2), (E.t1k_d, tk1), (E.t2k_d, tk2)):
        LD.inc(SY.dma_start(out=tbl[:].rearrange("p (c d) -> p c d", d=64),
                            in_=tbl_d.rearrange("(c p) d -> p c d", p=128)), 16)
    LD.inc(SY.dma_start(out=qs_sb[:], in_=E.f_qs[li:li + 1, :]), 16)
    LD.inc(SY.dma_start(out=ks_sb[:], in_=E.f_ks[li:li + 1, :]), 16)
    tbl_ld = LD.n
    V.wait_ge(LD.sem, E.consts_ld)
    V.memset(ones_f[:], 1.0)
    DV.inc(V.tensor_copy(ident_f[:], E.ident_sb[:]))
    ones_dv = DV.n
    TE.wait_ge(LD.sem, tbl_ld)
    TE.wait_ge(DV.sem, ones_dv)
    for idx, (row, swap) in enumerate(((qs_sb, False), (qs_sb, True),
                                      (ks_sb, False), (ks_sb, True))):
        rhs = row[0:1, :]
        if swap:
            rhs = row[0:1, :].rearrange("p (two d) -> p two d", two=2)[:, ::-1, :]
        if idx > 0:
            TE.wait_ge(DV.sem, DV.n)
        PE.inc(TE.matmul(ps_qm[:], ones_f[0:1, :], rhs, start=True, stop=True))
        V.wait_ge(PE.sem, PE.n)
        DV.inc(V.tensor_copy(sc_mat[:, idx * 64:(idx + 1) * 64], ps_qm[:]))
    V.wait_ge(DV.sem, DV.n)
    for tbl, mi in ((tq1, 0), (tq2, 1), (tk1, 2), (tk2, 3)):
        DV.inc(V.tensor_tensor(
            out=tbl[:].rearrange("p (c d) -> p c d", d=64),
            in0=tbl[:].rearrange("p (c d) -> p c d", d=64),
            in1=sc_mat[:, mi * 64:(mi + 1) * 64][:, None, :].broadcast_to([128, 32, 64]),
            op=ALU.mult))
    tbl_dv = DV.n

    evac = {}
    rel = {}
    first_in_bank = {"o0": True, "o1": True}

    for qt in range(NT):
        slot = qt % 2
        kts = list(range(max(0, qt - 4), qt + 1))
        nkt = len(kts)
        w = nkt * 128

        if ("xtm", slot) in rel:
            SY.wait_ge(DV.sem, rel[("xtm", slot)])
        LD.inc(SY.dma_start(out=x_tm[:, slot * D:slot * D + D],
                            in_=E.xA[qt * 128:(qt + 1) * 128, :]), 16)
        x_ld = LD.n

        V.wait_ge(LD.sem, x_ld)
        if ("xbf", slot) in rel:
            V.wait_ge(PE.sem, rel[("xbf", slot)])
        DV.inc(V.tensor_copy(x_bf[:, slot * D:slot * D + D],
                             x_tm[:, slot * D:slot * D + D]))
        xbf_dv = DV.n

        if qt < 2:
            V.wait_ge(DV.sem, xbf_dv)
            DV.inc(V.tensor_sub(xlo_f[:], x_tm[:, slot * D:slot * D + D],
                                x_bf[:, slot * D:slot * D + D]))
            V.wait_ge(DV.sem, DV.n)
            DV.inc(V.tensor_copy(xlo_bf[:], xlo_f[:]))
            xbf_dv = DV.n
        TE.wait_ge(DV.sem, xbf_dv)
        if qt == 0:
            TE.wait_ge(PO.sem, w_ready)
            TE.wait_ge(DV.sem, tbl_dv)
        if "ps_b" in evac:
            TE.wait_ge(DV.sem, evac["ps_b"])
        for ft in range(8):
            mm = TE.transpose(ps_b[:, ft * 128:(ft + 1) * 128],
                              x_bf[:, slot * D + ft * 128: slot * D + (ft + 1) * 128],
                              E.ident_sb[:])
            if ft == 7:
                PE.inc(mm)
        rel[("xbf", slot)] = PE.n
        V.wait_ge(PE.sem, PE.n)
        if ("xfm", slot) in rel:
            V.wait_ge(PE.sem, rel[("xfm", slot)])
        base = slot * 1032
        pbase = (1 - slot) * 1032
        DV.inc(V.tensor_copy(
            x_fm[:, base:base + 1032].rearrange("p (f t) -> p f t", t=129)[:, :, 1:129],
            ps_b[:].rearrange("p (f t) -> p f t", t=128)))
        if qt == 0:
            V.memset(x_fm[:, base:base + 1032]
                     .rearrange("p (f t) -> p f t", t=129)[:, :, 0:1], 0.0)
            DV.inc(V.tensor_copy(stt[:, 23:24], stt[:, 23:24]))
        else:
            DV.inc(V.tensor_copy(
                x_fm[:, base:base + 1032].rearrange("p (f t) -> p f t", t=129)[:, :, 0:1],
                x_fm[:, pbase:pbase + 1032].rearrange("p (f t) -> p f t", t=129)[:, :, 128:129]))
        if qt < 2:
            TE.wait_ge(DV.sem, DV.n)
            for ft in range(8):
                mm = TE.transpose(ps_b[:, ft * 128:(ft + 1) * 128],
                                  xlo_bf[:, ft * 128:(ft + 1) * 128], E.ident_sb[:])
                if ft == 7:
                    PE.inc(mm)
            V.wait_ge(PE.sem, PE.n)
            DV.inc(V.tensor_copy(
                x_fm_lo[:, base:base + 1032].rearrange("p (f t) -> p f t", t=129)[:, :, 1:129],
                ps_b[:].rearrange("p (f t) -> p f t", t=128)))
            evac["ps_b"] = DV.n
            if qt == 0:
                DV.inc(V.memset(x_fm_lo[:, base:base + 1032]
                                .rearrange("p (f t) -> p f t", t=129)[:, :, 0:1], 0.0))
            else:
                DV.inc(V.tensor_copy(
                    x_fm_lo[:, base:base + 1032].rearrange("p (f t) -> p f t", t=129)[:, :, 0:1],
                    x_fm_lo[:, pbase:pbase + 1032].rearrange("p (f t) -> p f t", t=129)[:, :, 128:129]))
        xfm_dv = DV.n

        TE.wait_ge(DV.sem, xfm_dv)
        if ("big",) in evac:
            TE.wait_ge(DV.sem, evac[("big",)])
        if ("big_ac",) in evac:
            TE.wait_ge(AC.sem, evac[("big_ac",)])
        import os
        hl = int(os.environ.get("HILO", "3"))
        qkv_terms = [(x_fm, wqkv_bf)]
        if qt < 2:
            if hl & 1:
                qkv_terms += [(x_fm, wqkv_lo)]
            if hl & 2:
                qkv_terms += [(x_fm_lo, wqkv_bf)]
        nterm = len(qkv_terms)
        for ti, (xsrc, wsrc) in enumerate(qkv_terms):
            for kt in range(8):
                if kt < 4:
                    lhsT = xsrc[:, base + kt * 129 + 1: base + kt * 129 + 129]
                else:
                    lhsT = xsrc[:, base + kt * 129 + 0: base + kt * 129 + 128]
                for nh in range(3):
                    mm = TE.matmul(ps_big[:, nh * 512:(nh + 1) * 512],
                                   lhsT,
                                   wsrc[:, kt * 1536 + nh * 512: kt * 1536 + (nh + 1) * 512],
                                   start=(ti == 0 and kt == 0),
                                   stop=(ti == nterm - 1 and kt == 7))
                    if ti == nterm - 1 and kt == 7 and nh == 2:
                        PE.inc(mm)
        rel[("xfm", slot)] = PE.n
        qkv_pe = PE.n

        SC.wait_ge(PE.sem, qkv_pe)
        AC.inc(SC.activation(out=rta[:], in_=ps_big[:, 0:512], func=AF.Square))
        AC.inc(SC.activation(out=rtb[:], in_=ps_big[:, 512:1024], func=AF.Square))
        sq_ac = AC.n
        V.wait_ge(AC.sem, sq_ac)
        DV.inc(V.reduce_sum(out=stt[:, 0:8],
                            in_=rta[:].rearrange("p (h d) -> p h d", d=64), axis=AX.X))
        DV.inc(V.reduce_sum(out=stt[:, 8:16],
                            in_=rtb[:].rearrange("p (h d) -> p h d", d=64), axis=AX.X))
        V.wait_ge(DV.sem, DV.n)
        SC.wait_ge(DV.sem, DV.n)
        AC.inc(SC.activation(out=rta[:, 0:16], in_=stt[:, 0:16], func=AF.Sqrt))
        V.wait_ge(AC.sem, AC.n)
        DV.inc(V.tensor_scalar_max(rta[:, 0:16], rta[:, 0:16], 1e-12))
        V.wait_ge(DV.sem, DV.n)
        DV.inc(V.reciprocal(rta[:, 16:32], rta[:, 0:16]))
        V.wait_ge(DV.sem, DV.n)
        DV.inc(V.tensor_mul(rta[:, 32:48], rta[:, 16:32], rta[:, 16:32]))
        V.wait_ge(DV.sem, DV.n)
        DV.inc(V.tensor_mul(rta[:, 32:48], rta[:, 32:48], stt[:, 0:16]))
        V.wait_ge(DV.sem, DV.n)
        DV.inc(V.tensor_scalar_mul(rta[:, 32:48], rta[:, 32:48], -0.5))
        V.wait_ge(DV.sem, DV.n)
        DV.inc(V.tensor_scalar_add(rta[:, 32:48], rta[:, 32:48], 1.5))
        V.wait_ge(DV.sem, DV.n)
        DV.inc(V.tensor_mul(stt[:, 0:16], rta[:, 16:32], rta[:, 32:48]))
        V.wait_ge(DV.sem, DV.n)
        DV.inc(V.tensor_tensor(
            out=q1b[:].rearrange("p (h d) -> p h d", d=64),
            in0=ps_big[:, 0:512].rearrange("p (h d) -> p h d", d=64),
            in1=stt[:, 0:8][:, :, None].broadcast_to([128, 8, 64]), op=ALU.mult))
        DV.inc(V.tensor_tensor(
            out=k1b[:].rearrange("p (h d) -> p h d", d=64),
            in0=ps_big[:, 512:1024].rearrange("p (h d) -> p h d", d=64),
            in1=stt[:, 8:16][:, :, None].broadcast_to([128, 8, 64]), op=ALU.mult))
        vslot = qt % 5
        if ("vroll", vslot) in rel:
            V.wait_ge(PE.sem, rel[("vroll", vslot)])
        DV.inc(V.tensor_copy(v_roll[:, vslot * 512:(vslot + 1) * 512],
                             ps_big[:, 1024:1536]))
        V.wait_ge(DV.sem, DV.n)
        ttile_q1 = tq1[:, qt * 64:(qt + 1) * 64][:, None, :].broadcast_to([128, 8, 64])
        ttile_q2 = tq2[:, qt * 64:(qt + 1) * 64][:, None, :].broadcast_to([128, 8, 64])
        ttile_k1 = tk1[:, qt * 64:(qt + 1) * 64][:, None, :].broadcast_to([128, 8, 64])
        ttile_k2 = tk2[:, qt * 64:(qt + 1) * 64][:, None, :].broadcast_to([128, 8, 64])
        DV.inc(V.tensor_tensor(out=rta[:].rearrange("p (h d) -> p h d", d=64),
                               in0=q1b[:].rearrange("p (h d) -> p h d", d=64),
                               in1=ttile_q1, op=ALU.mult))
        DV.inc(V.tensor_tensor(
            out=rtb[:].rearrange("p (h d) -> p h d", d=64),
            in0=q1b[:].rearrange("p (h two s) -> p h two s", two=2, s=32)[:, :, ::-1, :]
                .rearrange("p h two s -> p h (two s)"),
            in1=ttile_q2, op=ALU.mult)) if False else DV.inc(V.tensor_tensor(
            out=rtb[:].rearrange("p (h two s) -> p h two s", two=2, s=32),
            in0=q1b[:].rearrange("p (h two s) -> p h two s", two=2, s=32)[:, :, ::-1, :],
            in1=ttile_q2.rearrange("p h (two s) -> p h two s", two=2, s=32), op=ALU.mult))
        V.wait_ge(DV.sem, DV.n)
        DV.inc(V.tensor_add(q2b[:], rta[:], rtb[:]))
        if qt < 2:
            DV.inc(V.tensor_add(q2f[:], rta[:], rtb[:]))
        DV.inc(V.tensor_tensor(out=rta[:].rearrange("p (h d) -> p h d", d=64),
                               in0=k1b[:].rearrange("p (h d) -> p h d", d=64),
                               in1=ttile_k1, op=ALU.mult))
        DV.inc(V.tensor_tensor(
            out=rtb[:].rearrange("p (h two s) -> p h two s", two=2, s=32),
            in0=k1b[:].rearrange("p (h two s) -> p h two s", two=2, s=32)[:, :, ::-1, :],
            in1=ttile_k2.rearrange("p h (two s) -> p h two s", two=2, s=32), op=ALU.mult))
        V.wait_ge(DV.sem, DV.n)
        DV.inc(V.tensor_add(k2b[:], rta[:], rtb[:]))
        if qt < 2:
            DV.inc(V.tensor_add(k2f[:], rta[:], rtb[:]))
            DV.inc(V.tensor_copy(vf_roll[:, vslot * 512:(vslot + 1) * 512],
                                 ps_big[:, 1024:1536]))
        qk_dv = DV.n

        if qt == 0 and getattr(E, "a_dbg", False):
            V.wait_ge(DV.sem, qk_dv)
            V.wait_ge(ST.sem, ST.n)
            DV.inc(V.tensor_copy(x_out[:, 0:256], sc_mat[:]))
            DV.inc(V.tensor_copy(x_out[0:1, 256:320], qs_sb[:]))
            DV.inc(V.tensor_copy(x_out[:, 320:384], tq1[:, 0:64]))
            GP.wait_ge(DV.sem, DV.n)
            ST.inc(GP.dma_start(out=E.out[15 * 128:16 * 128, 0:384], in_=x_out[:, 0:384]), 16)
            V.wait_ge(ST.sem, ST.n)
            for bi, buf in enumerate((q1b, k1b, q2b, k2b)):
                DV.inc(V.tensor_copy(x_out[:, 0:512], buf[:]))
                GP.wait_ge(DV.sem, DV.n)
                ST.inc(GP.dma_start(out=E.out[(16 + bi) * 128:(17 + bi) * 128, 0:512],
                                    in_=x_out[:, 0:512]), 16)
                V.wait_ge(ST.sem, ST.n)
        TE.wait_ge(DV.sem, qk_dv)
        TE.wait_ge(DV.sem, evac["ps_b"]) if "ps_b" in evac else None
        for ft in range(4):
            TE.transpose(ps_b[:, ft * 128:(ft + 1) * 128],
                         q2b[:, ft * 128:(ft + 1) * 128], E.ident_sb[:])
        for ft in range(4):
            mm = TE.transpose(ps_b[:, 512 + ft * 128: 512 + (ft + 1) * 128],
                              k2b[:, ft * 128:(ft + 1) * 128], E.ident_sb[:])
            if ft == 3:
                PE.inc(mm)
        qkT_pe = PE.n
        V.wait_ge(PE.sem, qkT_pe)
        if ("kroll", vslot) in rel:
            V.wait_ge(PE.sem, rel[("kroll", vslot)])
        DV.inc(V.tensor_copy(q_fmc[:], ps_b[:, 0:512]))
        DV.inc(V.tensor_copy(kfm_roll[:, vslot * 512:(vslot + 1) * 512],
                             ps_b[:, 512:1024]))
        evac["ps_b"] = DV.n
        if qt < 2:
            if ("tf",) in evac:
                TE.wait_ge(DV.sem, evac[("tf",)])
            for ft in range(4):
                mm = TE.transpose(ps_tf[:, ft * 128:(ft + 1) * 128],
                                  q2f[:, ft * 128:(ft + 1) * 128], ident_f[:])
                if ft == 3:
                    PE.inc(mm)
            V.wait_ge(PE.sem, PE.n)
            DV.inc(V.tensor_copy(qfm_f[:], ps_tf[:]))
            TE.wait_ge(DV.sem, DV.n)
            for ft in range(4):
                mm = TE.transpose(ps_tf[:, ft * 128:(ft + 1) * 128],
                                  k2f[:, ft * 128:(ft + 1) * 128], ident_f[:])
                if ft == 3:
                    PE.inc(mm)
            V.wait_ge(PE.sem, PE.n)
            DV.inc(V.tensor_copy(kf_roll[:, qt * 512:(qt + 1) * 512], ps_tf[:]))
            evac[("tf",)] = DV.n
        qkT_dv = DV.n

        dv_after_stats = qkT_dv
        for h in range(HB):
            hb = (h % 2) * 64
            hf = (h // 2) * 128
            TE.wait_ge(DV.sem, dv_after_stats)
            if h == 0:
                pass
            else:
                if ("sim_ac",) in evac:
                    TE.wait_ge(AC.sem, evac[("sim_ac",)])
            fp32_tile = qt < 2
            for ki, ktile in enumerate(kts):
                ks_ = ktile % 5
                if fp32_tile:
                    mm = TE.matmul(ps_big[:, ki * 128:(ki + 1) * 128],
                                   qfm_f[hb:hb + 64, hf:hf + 128],
                                   kf_roll[hb:hb + 64, ktile * 512 + hf: ktile * 512 + hf + 128],
                                   start=True, stop=True)
                else:
                    mm = TE.matmul(ps_big[:, ki * 128:(ki + 1) * 128],
                                   q_fmc[hb:hb + 64, hf:hf + 128],
                                   kfm_roll[hb:hb + 64, ks_ * 512 + hf: ks_ * 512 + hf + 128],
                                   start=True, stop=True)
                if ki == nkt - 1:
                    PE.inc(mm)
            sim_pe = PE.n
            V.wait_ge(PE.sem, sim_pe)
            DV.inc(V.tensor_add(ps_big[:, (nkt - 1) * 128: nkt * 128],
                                ps_big[:, (nkt - 1) * 128: nkt * 128], E.mle_sb[:]))
            if qt >= 4:
                DV.inc(V.tensor_add(ps_big[:, 0:128], ps_big[:, 0:128], E.mge_sb[:]))
            V.wait_ge(DV.sem, DV.n)
            DV.inc(V.reduce_max(out=stt[:, 16:17], in_=ps_big[:, 0:w], axis=AX.X,
                                negate=True))
            mask_dv = DV.n
            SC.wait_ge(DV.sem, mask_dv)
            AC.inc(SC.activation(out=attn_f[:, 0:w], in_=ps_big[:, 0:w], func=AF.Exp,
                                 bias=stt[:, 16:17], accum_out=stt[:, 17:18]))
            evac[("sim_ac",)] = AC.n
            exp_ac = AC.n
            V.wait_ge(AC.sem, exp_ac)
            DV.inc(V.reciprocal(stt[:, 18:19], stt[:, 17:18]))
            V.wait_ge(DV.sem, DV.n)
            if fp32_tile:
                DV.inc(V.tensor_scalar_mul(attn_nf[:, 0:w], attn_f[:, 0:w], stt[:, 18:19]))
                attn_dv = DV.n
                TE.wait_ge(DV.sem, attn_dv)
                if ("tf",) in evac:
                    TE.wait_ge(DV.sem, evac[("tf",)])
                for ki in range(nkt):
                    mm = TE.transpose(ps_tf[:, ki * 128:(ki + 1) * 128],
                                      attn_nf[:, ki * 128:(ki + 1) * 128], ident_f[:])
                    if ki == nkt - 1:
                        PE.inc(mm)
                V.wait_ge(PE.sem, PE.n)
                DV.inc(V.tensor_copy(afm_f[:, 0:w], ps_tf[:, 0:w]))
                evac[("tf",)] = DV.n
                afm_dv = DV.n
            else:
                DV.inc(V.tensor_scalar_mul(attn_sb[:, 0:w], attn_f[:, 0:w], stt[:, 18:19]))
                attn_dv = DV.n
                TE.wait_ge(DV.sem, attn_dv)
                if ("ps_b2",) in evac:
                    TE.wait_ge(DV.sem, evac[("ps_b2",)])
                for ki in range(nkt):
                    mm = TE.transpose(ps_b[:, ki * 128:(ki + 1) * 128],
                                      attn_sb[:, ki * 128:(ki + 1) * 128], E.ident_sb[:])
                    if ki == nkt - 1:
                        PE.inc(mm)
                V.wait_ge(PE.sem, PE.n)
                DV.inc(V.tensor_copy(attn_fm[:, 0:w], ps_b[:, 0:w]))
                evac[("ps_b2",)] = DV.n
                evac["ps_b"] = DV.n
                afm_dv = DV.n
            og = ["o0", "o1"][h // 4]
            pso = ps_o[h // 4]
            ocol = (h % 4) * 128
            TE.wait_ge(DV.sem, afm_dv)
            if h < 2 and ("oev", h // 4) in evac:
                TE.wait_ge(DV.sem, evac[("oev", h // 4)])
            for ki, ktile in enumerate(kts):
                ks_ = ktile % 5
                st_flag = first_in_bank[og]
                first_in_bank[og] = False
                if fp32_tile:
                    mm = TE.matmul(pso[0:64, ocol:ocol + 128],
                                   vf_roll[:, ktile * 512 + h * 64: ktile * 512 + h * 64 + 64],
                                   afm_f[:, ki * 128:(ki + 1) * 128],
                                   start=st_flag, stop=(ki == nkt - 1))
                else:
                    mm = TE.matmul(pso[0:64, ocol:ocol + 128],
                                   v_roll[:, ks_ * 512 + h * 64: ks_ * 512 + h * 64 + 64],
                                   attn_fm[:, ki * 128:(ki + 1) * 128],
                                   start=st_flag, stop=(ki == nkt - 1))
                if ki == nkt - 1:
                    PE.inc(mm)
            if h % 4 == 3:
                V.wait_ge(PE.sem, PE.n)
                for hh in range(h - 3, h + 1):
                    DV.inc(V.tensor_copy(
                        o_fm[(hh % 2) * 64:(hh % 2) * 64 + 64,
                             (hh // 2) * 128:(hh // 2) * 128 + 128],
                        pso[0:64, (hh % 4) * 128:(hh % 4) * 128 + 128]))
                evac[("oev", h // 4)] = DV.n
                first_in_bank[og] = True
        o_dv = DV.n

        TE.wait_ge(DV.sem, o_dv)
        TE.wait_ge(AC.sem, evac[("sim_ac",)])
        for ot in range(4):
            for nh in range(2):
                mm = TE.matmul(ps_big[:, nh * 512:(nh + 1) * 512],
                               o_fm[:, ot * 128:(ot + 1) * 128],
                               wo_bf[:, ot * 1024 + nh * 512: ot * 1024 + (nh + 1) * 512],
                               start=(ot == 0), stop=(ot == 3))
                if ot == 3 and nh == 1:
                    PE.inc(mm)
        wo_pe = PE.n
        V.wait_ge(PE.sem, wo_pe)
        if ("xout", slot) in rel:
            V.wait_ge(ST.sem, rel[("xout", slot)])
        DV.inc(V.tensor_add(x_out[:, slot * D:slot * D + D],
                            x_tm[:, slot * D:slot * D + D], ps_big[:, 0:1024]))
        evac[("big",)] = DV.n
        rel[("xtm", slot)] = DV.n
        xo_dv = DV.n
        rel[("vroll", vslot)] = qkv_pe
        rel[("kroll", vslot)] = qkv_pe
        GP.wait_ge(DV.sem, xo_dv)
        ST.inc(GP.dma_start(out=E.xA[qt * 128:(qt + 1) * 128, :],
                            in_=x_out[:, slot * D:slot * D + D]), 16)
        rel[("xout", slot)] = ST.n
    pa.close()


def _emit_B(E, li):
    nc = E.nc
    V, SC, TE, SY, GP = E.V, E.SC, E.TE, E.SY, E.GP
    LD, ST, PE, DV, AC, PO = E.LD, E.ST, E.PE, E.DV, E.AC, E.PO
    H2 = 2 * HID

    pb = ExitStack()

    def sb(name, shape, dtype):
        return pb.enter_context(nc.sbuf_tensor("b%d_%s" % (li, name), shape, dtype))

    def ps(name, shape, dtype=F32):
        return pb.enter_context(nc.psum_tensor("b%d_%s" % (li, name), shape, dtype))

    _pass_barrier(E)

    w1g = sb("w1g", [128, 8 * H2], BF16)
    w2b = sb("w2b", [128, 22 * D], BF16)
    wstage = sb("wstage", [128, 2 * 1024], F32)
    g_cols = sb("gcols", [128, 8], F32)
    b_cols = sb("bcols", [128, 8], F32)
    b_cols_bf = sb("bcolsbf", [128, 8], BF16)
    bias_a = sb("biasa", [128, 22], F32)
    bias_g = sb("biasg", [128, 22], F32)
    b1_cols = sb("b1cols", [128, 44], F32)
    ones_f = sb("ones_f", [1, 128], F32)
    x_tm = sb("x_tm", [128, 2 * D], F32)
    x_bf = sb("x_bf", [128, D], BF16)
    x_fm = sb("x_fm", [128, 8 * 513], BF16)
    prevcol = sb("prevcol", [128, 8], BF16)
    h_ln = sb("h_ln", [128, 8 * 512], BF16)
    sqscr = sb("sqscr", [128, 512], BF16)
    rowbuf = sb("rowbuf", [1, 3 * 512], F32)
    mu_b = sb("mu_b", [128, 512], F32)
    r_b = sb("r_b", [128, 512], F32)
    t1buf = sb("t1", [128, 512], F32)
    gelu_sb = sb("gelu", [128, 512], BF16)
    hc = sb("hc", [128, 22 * 512], BF16)
    x_out = sb("x_out", [128, D], F32)

    ps_za = ps("za", [128, 512], F32)
    ps_zg = ps("zg", [128, 512], F32)
    ps_y = ps("y", [128, 1024], F32)
    ps_b = ps("trp", [128, 1024], BF16)
    ps_row = ps("row", [1, 1024], F32)
    ps_bc = ps("bc", [128, 512], F32)

    DV.inc(V.memset(ones_f[:], 1.0))
    V.memset(hc[32:64, 21 * 512:22 * 512], 0.0)
    V.memset(hc[64:128, 21 * 512:22 * 512], 0.0)
    DV.inc(V.memset(hc[64:65, 21 * 512:22 * 512], 1.0))
    hcinit_dv = DV.n

    for kt in range(8):
        LD.inc(SY.dma_start(out=g_cols[:, kt:kt + 1],
                            in_=E.f_g[li, kt * 128:(kt + 1) * 128][:, None]), 16)
        LD.inc(SY.dma_start(out=b_cols[:, kt:kt + 1],
                            in_=E.f_b[li, kt * 128:(kt + 1) * 128][:, None]), 16)
    for jc in range(22):
        wa = 128 if jc < 21 else HID - 21 * 128
        LD.inc(SY.dma_start(out=b1_cols[0:wa, jc:jc + 1],
                            in_=E.f_b1[li, jc * 128: jc * 128 + wa][:, None]), 16)
        LD.inc(SY.dma_start(out=b1_cols[0:wa, 22 + jc:23 + jc],
                            in_=E.f_b1[li, HID + jc * 128: HID + jc * 128 + wa][:, None]), 16)
    gcol_ld = LD.n

    base_po = PO.n
    chunks1 = [(q * 1024, min(1024, H2 - q * 1024)) for q in range(6)]
    i = 0
    for kt in range(8):
        for (j0, wd) in chunks1:
            slot = i % 2
            if i >= 2:
                SY.wait_ge(PO.sem, base_po + i - 1)
            elif i < 2:
                SY.wait_ge(PO.sem, base_po)
            LD.inc(SY.dma_start(out=wstage[:, slot * 1024: slot * 1024 + wd],
                                in_=E.f_w1[li, kt * 128:(kt + 1) * 128, j0:j0 + wd]), 16)
            GP.wait_ge(LD.sem, LD.n)
            if i == 0:
                GP.wait_ge(LD.sem, gcol_ld)
            PO.inc(GP.tensor_copy(w1g[:, kt * H2 + j0: kt * H2 + j0 + wd],
                                   wstage[:, slot * 1024: slot * 1024 + wd]))
            i += 1
    base_po2 = PO.n
    for ct in range(22):
        slot = ct % 2
        wr = 128 if ct < 21 else HID - 21 * 128
        if ct >= 2:
            SY.wait_ge(PO.sem, base_po2 + ct - 1)
        else:
            SY.wait_ge(PO.sem, base_po2)
        if ct == 21:
            V.wait_ge(PO.sem, PO.n)
            V.memset(wstage[32:64, slot * 1024: slot * 1024 + D], 0.0)
            DV.inc(V.memset(wstage[64:128, slot * 1024: slot * 1024 + D], 0.0))
            SY.wait_ge(DV.sem, DV.n)
            LD.inc(SY.dma_start(out=wstage[64:65, slot * 1024: slot * 1024 + D],
                                in_=E.f_b2[li][None, :]), 16)
        LD.inc(SY.dma_start(out=wstage[0:wr, slot * 1024: slot * 1024 + D],
                            in_=E.f_w2[li, ct * 128: ct * 128 + wr, :]), 16)
        GP.wait_ge(LD.sem, LD.n)
        PO.inc(GP.tensor_copy(w2b[:, ct * D:(ct + 1) * D],
                              wstage[:, slot * 1024: slot * 1024 + D]))
    w_ready = PO.n

    V.wait_ge(LD.sem, gcol_ld)
    DV.inc(V.tensor_copy(bias_a[:, 0:22], b1_cols[:, 0:22]))
    DV.inc(V.tensor_copy(bias_g[:, 0:22], b1_cols[:, 22:44]))
    bias_done = DV.n

    evac = {}
    rel = {}

    for c in range(NCH):
        slot = 0
        base = 0
        for tt in range(CT):
            i = c * CT + tt
            if ("xtm", i % 2) in rel:
                SY.wait_ge(DV.sem, rel[("xtm", i % 2)])
            LD.inc(SY.dma_start(out=x_tm[:, (i % 2) * D:(i % 2 + 1) * D],
                                in_=E.xA[i * 128:(i + 1) * 128, :]), 16)
            x_ld = LD.n
            V.wait_ge(LD.sem, x_ld)
            if ("xbf",) in rel:
                V.wait_ge(PE.sem, rel[("xbf",)])
            DV.inc(V.tensor_copy(x_bf[:], x_tm[:, (i % 2) * D:(i % 2 + 1) * D]))
            rel[("xtm", i % 2)] = DV.n
            xbf_dv = DV.n
            TE.wait_ge(DV.sem, xbf_dv)
            if c == 0 and tt == 0:
                TE.wait_ge(PO.sem, w_ready)
                TE.wait_ge(DV.sem, bias_done)
            if "ps_b" in evac:
                TE.wait_ge(DV.sem, evac["ps_b"])
            for ft in range(8):
                mm = TE.transpose(ps_b[:, ft * 128:(ft + 1) * 128],
                                  x_bf[:, ft * 128:(ft + 1) * 128], E.ident_sb[:])
                if ft == 7:
                    PE.inc(mm)
            rel[("xbf",)] = PE.n
            V.wait_ge(PE.sem, PE.n)
            if tt == 0 and ("xfm",) in rel:
                V.wait_ge(PE.sem, rel[("xfm",)])
                V.wait_ge(AC.sem, rel[("xfm_ac",)])
            DV.inc(V.tensor_copy(
                x_fm[:].rearrange("p (f t) -> p f t", t=513)
                [:, :, 1 + tt * 128: 1 + (tt + 1) * 128],
                ps_b[:].rearrange("p (f t) -> p f t", t=128)))
            evac["ps_b"] = DV.n
        if c == 0:
            DV.inc(V.memset(
                x_fm[:].rearrange("p (f t) -> p f t", t=513)[:, :, 0:1], 0.0))
        else:
            DV.inc(V.tensor_copy(
                x_fm[:].rearrange("p (f t) -> p f t", t=513)[:, :, 0:1],
                prevcol[:, :, None]))
        xfm_dv = DV.n

        def hview(kt):
            if kt < 4:
                return x_fm[:, kt * 513 + 1: kt * 513 + 513]
            return x_fm[:, kt * 513 + 0: kt * 513 + 512]

        TE.wait_ge(DV.sem, xfm_dv)
        if ("row",) in evac:
            TE.wait_ge(DV.sem, evac[("row",)])
        for kt in range(8):
            mm = TE.matmul(ps_row[0:1, 0:512], E.ones_sb[:, 0:1], hview(kt),
                           start=(kt == 0), stop=(kt == 7))
            if kt == 7:
                PE.inc(mm)
        sum_pe = PE.n
        for kt in range(8):
            SC.wait_ge(DV.sem, xfm_dv)
            if kt >= 1:
                SC.wait_ge(PE.sem, PE.n)
            AC.inc(SC.activation(out=sqscr[:], in_=hview(kt), func=AF.Square))
            TE.wait_ge(AC.sem, AC.n)
            PE.inc(TE.matmul(ps_row[0:1, 512:1024], E.ones_sb[:, 0:1],
                             sqscr[:],
                             start=(kt == 0), stop=(kt == 7)))
        sq_pe = PE.n
        sq_ac_last = AC.n
        V.wait_ge(PE.sem, sq_pe)
        DV.inc(V.tensor_scalar_mul(rowbuf[0:1, 0:512], ps_row[0:1, 0:512], 1.0 / D))
        V.wait_ge(DV.sem, DV.n)
        DV.inc(V.tensor_scalar_mul(rowbuf[0:1, 512:1024], ps_row[0:1, 512:1024], 1.0 / D))
        evac[("row",)] = DV.n
        V.wait_ge(DV.sem, DV.n)
        DV.inc(V.tensor_mul(rowbuf[0:1, 1024:1536], rowbuf[0:1, 0:512], rowbuf[0:1, 0:512]))
        V.wait_ge(DV.sem, DV.n)
        DV.inc(V.tensor_sub(rowbuf[0:1, 512:1024], rowbuf[0:1, 512:1024],
                            rowbuf[0:1, 1024:1536]))
        V.wait_ge(DV.sem, DV.n)
        DV.inc(V.tensor_scalar_add(rowbuf[0:1, 512:1024], rowbuf[0:1, 512:1024], EPS))
        SC.wait_ge(DV.sem, DV.n)
        AC.inc(SC.activation(out=rowbuf[0:1, 1024:1536], in_=rowbuf[0:1, 512:1024],
                             func=AF.Sqrt))
        V.wait_ge(AC.sem, AC.n)
        DV.inc(V.reciprocal(rowbuf[0:1, 512:1024], rowbuf[0:1, 1024:1536]))
        V.wait_ge(DV.sem, DV.n)
        row_dv = DV.n
        TE.wait_ge(DV.sem, row_dv)
        if ("bc",) in evac:
            TE.wait_ge(DV.sem, evac[("bc",)])
        PE.inc(TE.matmul(ps_bc[:], ones_f[0:1, :], rowbuf[0:1, 0:512],
                         start=True, stop=True))
        V.wait_ge(PE.sem, PE.n)
        DV.inc(V.tensor_copy(mu_b[:], ps_bc[:]))
        TE.wait_ge(DV.sem, DV.n)
        PE.inc(TE.matmul(ps_bc[:], ones_f[0:1, :], rowbuf[0:1, 512:1024],
                         start=True, stop=True))
        V.wait_ge(PE.sem, PE.n)
        DV.inc(V.tensor_copy(r_b[:], ps_bc[:]))
        evac[("bc",)] = DV.n
        V.wait_ge(DV.sem, DV.n)
        if ("hln",) in rel:
            V.wait_ge(PE.sem, rel[("hln",)])
        for kt in range(8):
            DV.inc(V.tensor_sub(t1buf[:], hview(kt), mu_b[:]))
            V.wait_ge(DV.sem, DV.n)
            DV.inc(V.tensor_mul(t1buf[:], t1buf[:], r_b[:]))
            V.wait_ge(DV.sem, DV.n)
            DV.inc(V.tensor_scalar(out=h_ln[:, kt * 512:(kt + 1) * 512],
                                   in0=t1buf[:],
                                   scalar1=g_cols[:, kt:kt + 1],
                                   scalar2=b_cols[:, kt:kt + 1],
                                   op0=ALU.mult, op1=ALU.add))
        DV.inc(V.tensor_copy(prevcol[:, :, None],
                             x_fm[:].rearrange("p (f t) -> p f t", t=513)[:, :, 512:513]))
        hln_dv = DV.n

        for jc in range(22):
            wa = 128 if jc < 21 else HID - 21 * 128
            TE.wait_ge(DV.sem, hln_dv)
            if ("za",) in evac:
                TE.wait_ge(DV.sem, evac[("za",)])
            if ("zg",) in evac:
                TE.wait_ge(AC.sem, evac[("zg",)])
            for kt in range(8):
                TE.matmul(ps_za[0:wa, :],
                          w1g[:, kt * H2 + jc * 128: kt * H2 + jc * 128 + wa],
                          h_ln[:, kt * 512:(kt + 1) * 512],
                          start=(kt == 0), stop=(kt == 7))
                mm = TE.matmul(ps_zg[0:wa, :],
                               w1g[:, kt * H2 + HID + jc * 128: kt * H2 + HID + jc * 128 + wa],
                               h_ln[:, kt * 512:(kt + 1) * 512],
                               start=(kt == 0), stop=(kt == 7))
                if kt == 7:
                    PE.inc(mm)
            z_pe = PE.n
            SC.wait_ge(PE.sem, z_pe)
            if jc >= 1:
                SC.wait_ge(DV.sem, DV.n)
            AC.inc(SC.activation(out=gelu_sb[0:wa, :],
                                 in_=ps_zg[0:wa, :], func=AF.Gelu,
                                 bias=bias_g[0:wa, jc:jc + 1]))
            evac[("zg",)] = AC.n
            V.wait_ge(AC.sem, AC.n)
            if ("hc", jc) in rel:
                V.wait_ge(PE.sem, rel[("hc", jc)])
            DV.inc(V.scalar_tensor_tensor(
                out=hc[0:wa, jc * 512:(jc + 1) * 512],
                in0=ps_za[0:wa, :], scalar=bias_a[0:wa, jc:jc + 1],
                in1=gelu_sb[0:wa, :],
                op0=ALU.add, op1=ALU.mult))
            evac[("za",)] = DV.n
        hc_dv = DV.n

        for tt in range(CT):
            i = c * CT + tt
            if ("xtm", i % 2) in rel:
                SY.wait_ge(DV.sem, rel[("xtm", i % 2)])
            LD.inc(SY.dma_start(out=x_tm[:, (i % 2) * D:(i % 2 + 1) * D],
                                in_=E.xA[i * 128:(i + 1) * 128, :]), 16)
            xr_ld = LD.n
            TE.wait_ge(DV.sem, hc_dv)
            if ("y",) in evac:
                TE.wait_ge(DV.sem, evac[("y",)])
            for jc in range(22):
                kr = 128 if jc < 21 else 65
                for nh in range(2):
                    mm = TE.matmul(ps_y[:, nh * 512:(nh + 1) * 512],
                                   hc[0:kr, jc * 512 + tt * 128: jc * 512 + (tt + 1) * 128],
                                   w2b[0:kr, jc * D + nh * 512: jc * D + (nh + 1) * 512],
                                   start=(jc == 0), stop=(jc == 21))
                    if jc == 21 and nh == 1:
                        PE.inc(mm)
            y_pe = PE.n
            if tt == CT - 1:
                rel[("hc", 0)] = PE.n
                for jc in range(22):
                    rel[("hc", jc)] = PE.n
            V.wait_ge(PE.sem, y_pe)
            V.wait_ge(LD.sem, xr_ld)
            if ("xout",) in rel:
                V.wait_ge(ST.sem, rel[("xout",)])
            DV.inc(V.tensor_add(x_out[:], x_tm[:, (i % 2) * D:(i % 2 + 1) * D], ps_y[:]))
            rel[("xtm", i % 2)] = DV.n
            evac[("y",)] = DV.n
            GP.wait_ge(DV.sem, DV.n)
            ST.inc(GP.dma_start(out=E.xA[i * 128:(i + 1) * 128, :], in_=x_out[:]), 16)
            rel[("xout",)] = ST.n
        rel[("hln",)] = PE.n
        rel[("xfm",)] = sq_pe
        rel[("xfm_ac",)] = sq_ac_last

    dbg = getattr(E, "b_dbg", None)
    if dbg:
        tmp = x_tm
        row = 0
        for name in dbg:
            buf = {"hln": h_ln, "rowbuf": rowbuf, "mub": mu_b, "rb": r_b,
                   "hc": hc, "xfm": x_fm, "hc2": hc, "hc3": hc,
                   "w2b": w2b, "w2b2": w2b}[name]
            ncols = {"hln": 4096, "rowbuf": 1536, "mub": 512, "rb": 512,
                     "hc": 4096, "xfm": 4104, "hc2": 4096, "hc3": 4096,
                     "w2b": 4096, "w2b2": 4096}[name]
            off = {"hc2": 4096, "hc3": 8192, "w2b2": 18432}.get(name, 0)
            for blk in range(0, min(ncols, 4096), 1024):
                w_ = min(1024, ncols - blk)
                V.wait_ge(DV.sem, DV.n)
                V.wait_ge(ST.sem, ST.n)
                V.wait_ge(PO.sem, PO.n)
                V.wait_ge(AC.sem, AC.n)
                V.memset(tmp[:, 0:1024], 0.0)
                blk2 = blk + off
                w_ = min(1024, buf.shape[1] - blk2) if name in ("hc3",) else w_
                if w_ <= 0:
                    row += 1
                    continue
                DV.inc(V.tensor_copy(tmp[0:1, 0:w_] if name == "rowbuf" else tmp[:, 0:w_],
                                     buf[0:1, blk2:blk2 + w_] if name == "rowbuf" else buf[:, blk2:blk2 + w_]))
                GP.wait_ge(DV.sem, DV.n)
                ST.inc(GP.dma_start(out=E.out[row * 128:(row + 1) * 128, :], in_=tmp[:, 0:1024]), 16)
                GP.wait_ge(ST.sem, ST.n)
                row += 1
    pb.close()


def _emit_P6(E):
    nc = E.nc
    V, SC, TE, SY, GP = E.V, E.SC, E.TE, E.SY, E.GP
    LD, ST, PE, DV, AC, PO = E.LD, E.ST, E.PE, E.DV, E.AC, E.PO

    p6 = ExitStack()

    def sb(name, shape, dtype):
        return p6.enter_context(nc.sbuf_tensor(name, shape, dtype))

    _pass_barrier(E)
    xa_in = sb("p6_xa", [128, 2 * D], F32)
    x1_in = sb("p6_x1", [128, 2 * D], F32)
    o_sb = sb("p6_o", [128, 2 * D], F32)
    sq_scr = sb("p6_sq", [128, D], BF16)
    stt = sb("p6_stt", [128, 8], F32)

    SY.wait_ge(ST.sem, ST.n)
    free_po = {}
    st_after = {}
    prev_sq_guard = 0
    for i in range(NT):
        slot = i % 2
        if i >= 2:
            SY.wait_ge(PO.sem, free_po[i - 2])
        LD.inc(SY.dma_start(out=xa_in[:, slot * D:slot * D + D],
                            in_=E.xA[i * 128:(i + 1) * 128, :]), 16)
        LD.inc(SY.dma_start(out=x1_in[:, slot * D:slot * D + D],
                            in_=E.x1k[i * 128:(i + 1) * 128, :]), 16)
        x_ld = LD.n
        SC.wait_ge(LD.sem, x_ld)
        V.wait_ge(LD.sem, x_ld)
        _ln_stats(E, xa_in[:, slot * D:slot * D + D], stt, sq_scr, D,
                  prev_guard=(DV.sem, prev_sq_guard) if prev_sq_guard else None)
        prev_sq_guard = DV.n
        if i >= 2:
            V.wait_ge(ST.sem, st_after[i - 2])
        DV.inc(V.tensor_scalar(out=o_sb[:, slot * D:slot * D + D],
                               in0=xa_in[:, slot * D:slot * D + D],
                               scalar1=stt[:, 2:3], scalar2=stt[:, 3:4],
                               op0=ALU.subtract, op1=ALU.mult))
        GP.wait_ge(DV.sem, DV.n)
        GP.wait_ge(LD.sem, x_ld)
        PO.inc(GP.tensor_add(o_sb[:, slot * D:slot * D + D],
                             o_sb[:, slot * D:slot * D + D],
                             x1_in[:, slot * D:slot * D + D]))
        free_po[i] = PO.n
        ST.inc(GP.dma_start(out=E.out[i * 128:(i + 1) * 128, :],
                            in_=o_sb[:, slot * D:slot * D + D]), 16)
        st_after[i] = ST.n
    GP.wait_ge(ST.sem, ST.n)
    p6.close()


def kernel(L0_attn_x, L0_attn_Wq, L0_attn_Wk, L0_attn_Wv, L0_attn_Wo, L0_attn_rf,
           L0_ffn_wqkv, L0_ffn_q_scale, L0_ffn_k_scale, L0_ffn_wo, L0_ffn_ln_g,
           L0_ffn_ln_b, L0_ffn_w1, L0_ffn_b1, L0_ffn_w2, L0_ffn_b2,
           _passes=("P0", "P1", "A0", "B0", "A1", "B1", "P6"), _cores=8):
    consts = _host_consts()
    rf_dup = np.concatenate([np.asarray(L0_attn_rf), np.asarray(L0_attn_rf)], axis=0)
    rf_dup = np.ascontiguousarray(rf_dup, dtype=np.float32)

    def a(x):
        return np.ascontiguousarray(np.asarray(x), dtype=np.float32)

    common = dict(
        wq=a(L0_attn_Wq), wk=a(L0_attn_Wk), wv=a(L0_attn_Wv), wo_a=a(L0_attn_Wo),
        rf_dup=rf_dup,
        f_wqkv=a(L0_ffn_wqkv), f_qs=a(L0_ffn_q_scale), f_ks=a(L0_ffn_k_scale),
        f_wo=a(L0_ffn_wo), f_g=a(L0_ffn_ln_g), f_b=a(L0_ffn_ln_b),
        f_w1=a(L0_ffn_w1), f_b1=a(L0_ffn_b1), f_w2=a(L0_ffn_w2), f_b2=a(L0_ffn_b2),
        ident=consts["ident"], ones_bf=consts["ones_bf"], shift_tbl=consts["shift_tbl"],
        t1q=consts["t1q"], t2q=consts["t2q"], t1k=consts["t1k"], t2k=consts["t2k"],
        m_ge=consts["m_ge"], m_le=consts["m_le"],
    )
    xfull = np.asarray(L0_attn_x)
    in_maps = []
    for c in range(_cores):
        m = dict(common)
        m["x0"] = np.ascontiguousarray(xfull[c], dtype=np.float32)
        in_maps.append(m)

    nc = build_nc(passes=_passes)
    res = run_bass_kernel_spmd(nc, in_maps, core_ids=list(range(_cores)))
    return np.stack([r["out"] for r in res.results], axis=0)


# version 08
# speedup vs baseline: 1.2856x; 1.2856x over previous
import numpy as np
import ml_dtypes
from contextlib import ExitStack

import concourse.bass as bass
import concourse.mybir as mybir
from concourse.bass_utils import run_bass_kernel_spmd

F32 = mybir.dt.float32
BF16 = mybir.dt.bfloat16
AF = mybir.ActivationFunctionType
ALU = mybir.AluOpType
AX = mybir.AxisListType

S = 4096
D = 1024
NT = 32
NCH = 8
CT = 4
HA = 16
FA = 128
HB = 8
DB = 64
INNER = 512
HID = 2730
EPS = 1e-5
MAGIC = float(3 * 2.0 ** 22)
INV2PI = float(1.0 / (2 * np.pi))
NEG2PI = float(-2 * np.pi)
ZCLIP = float(1e-6 * FA)
NEG_BIG = -1e30
LNEPS = 1e-30


def _host_consts():
    c = {}
    c["ident"] = np.eye(128, dtype=np.float32).astype(ml_dtypes.bfloat16)
    st = np.zeros((128, 1), np.float32)
    st[0:64, 0] = 0.25
    c["shift_tbl"] = st
    t = np.arange(S, dtype=np.float64)
    half = np.arange(0, DB, 2, dtype=np.float64)
    inv_freq = 1.0 / (10000.0 ** (half / DB))
    freqs = t[:, None] * inv_freq[None, :]
    freqs = np.concatenate([freqs, freqs], axis=1)
    base = (half + 0.4 * DB) / (1.4 * DB)
    power = (t - S // 2) / float(512 // 2)
    scale = base[None, :] ** power[:, None]
    scale = np.concatenate([scale, scale], axis=1)
    cos, sin = np.cos(freqs), np.sin(freqs)
    sgn = np.ones((1, DB)); sgn[0, 0:32] = -1.0
    c["t1q"] = (cos * scale * 8.0).astype(np.float32)
    c["t2q"] = (sin * scale * 8.0 * sgn).astype(np.float32)
    c["t1k"] = (cos / scale).astype(np.float32)
    c["t2k"] = (sin / scale * sgn).astype(np.float32)
    ii = np.arange(128)[:, None]
    jj = np.arange(128)[None, :]
    c["m_ge"] = np.where(jj >= ii, 0.0, NEG_BIG).astype(np.float32)
    c["m_le"] = np.where(jj <= ii, 0.0, NEG_BIG).astype(np.float32)
    c["ones_bf"] = np.ones((128, 128), dtype=ml_dtypes.bfloat16)
    return c


class Ctr:
    def __init__(self, es, nc, name):
        self.sem = es.enter_context(nc.semaphore(name))
        self.n = 0

    def inc(self, instr, k=1):
        assert k > 0
        instr.then_inc(self.sem, k)
        self.n += k
        return self.n


class Env:
    pass


def _ln_stats_paranoid(E, src_ap, stt, sq_scr, nfeat, prev_guard=None):
    V, SC = E.V, E.SC
    DV, AC = E.DV, E.AC
    if prev_guard is not None:
        SC.wait_ge(prev_guard[0], prev_guard[1])
    AC.inc(SC.activation(out=sq_scr[:], in_=src_ap, func=AF.Square,
                         accum_out=stt[:, 1:2]))
    sq_ac = AC.n
    V.wait_ge(AC.sem, sq_ac)
    DV.inc(V.reduce_sum(out=stt[:, 0:1], in_=src_ap, axis=AX.X))
    V.wait_ge(DV.sem, DV.n)
    DV.inc(V.tensor_scalar_mul(stt[:, 2:3], stt[:, 0:1], 1.0 / nfeat))
    V.wait_ge(DV.sem, DV.n)
    DV.inc(V.tensor_mul(stt[:, 5:6], stt[:, 2:3], stt[:, 2:3]))
    V.wait_ge(DV.sem, DV.n)
    DV.inc(V.tensor_scalar_mul(stt[:, 6:7], stt[:, 1:2], 1.0 / nfeat))
    V.wait_ge(DV.sem, DV.n)
    DV.inc(V.tensor_scalar_add(stt[:, 6:7], stt[:, 6:7], EPS))
    V.wait_ge(DV.sem, DV.n)
    DV.inc(E.V.tensor_sub(stt[:, 6:7], stt[:, 6:7], stt[:, 5:6]))
    SC.wait_ge(DV.sem, DV.n)
    AC.inc(SC.activation(out=stt[:, 7:8], in_=stt[:, 6:7], func=AF.Sqrt))
    V.wait_ge(AC.sem, AC.n)
    DV.inc(V.reciprocal(stt[:, 3:4], stt[:, 7:8]))
    V.wait_ge(DV.sem, DV.n)
    DV.inc(V.tensor_mul(stt[:, 4:5], stt[:, 2:3], stt[:, 3:4]))
    V.wait_ge(DV.sem, DV.n)
    DV.inc(V.tensor_scalar_mul(stt[:, 4:5], stt[:, 4:5], -1.0))
    V.wait_ge(DV.sem, DV.n)
    return DV.n


def _ln_stats(E, src_ap, stt, sq_scr, nfeat, prev_guard=None):
    if True:
        return _ln_stats_paranoid(E, src_ap, stt, sq_scr, nfeat, prev_guard)
    """Emit LN stats for one [128, nfeat] tile (src may be PSUM or SBUF).
    Produces stt[:,2:3]=mu, stt[:,3:4]=rstd, stt[:,4:5]=-mu*rstd.
    prev_guard: (sem, val) the ACT Square must wait on (WAR for stt[:,1:2])."""
    V, SC = E.V, E.SC
    DV, AC = E.DV, E.AC
    if prev_guard is not None:
        SC.wait_ge(prev_guard[0], prev_guard[1])
    AC.inc(SC.activation(out=sq_scr[:], in_=src_ap, func=AF.Square,
                         accum_out=stt[:, 1:2]))
    sq_ac = AC.n
    V.wait_ge(AC.sem, sq_ac)
    V.reduce_sum(out=stt[:, 0:1], in_=src_ap, axis=AX.X)
    V.tensor_scalar_mul(stt[:, 2:3], stt[:, 0:1], 1.0 / nfeat)
    V.tensor_mul(stt[:, 5:6], stt[:, 2:3], stt[:, 2:3])
    V.tensor_scalar_mul(stt[:, 6:7], stt[:, 1:2], 1.0 / nfeat)
    V.tensor_scalar_add(stt[:, 6:7], stt[:, 6:7], EPS)
    DV.inc(E.V.tensor_sub(stt[:, 6:7], stt[:, 6:7], stt[:, 5:6]))
    SC.wait_ge(DV.sem, DV.n)
    AC.inc(SC.activation(out=stt[:, 7:8], in_=stt[:, 6:7], func=AF.Sqrt))
    V.wait_ge(AC.sem, AC.n)
    V.reciprocal(stt[:, 3:4], stt[:, 7:8])
    V.tensor_mul(stt[:, 4:5], stt[:, 2:3], stt[:, 3:4])
    DV.inc(V.tensor_scalar_mul(stt[:, 4:5], stt[:, 4:5], -1.0))
    return DV.n


def _pass_barrier(E):
    sems = [(E.PE, E.PE.n), (E.DV, E.DV.n), (E.AC, E.AC.n), (E.PO, E.PO.n),
            (E.LD, E.LD.n), (E.ST, E.ST.n)]
    for eng in (E.SY, E.GP, E.V, E.SC, E.TE):
        for ctr, val in sems:
            if val:
                eng.wait_ge(ctr.sem, val)


def _load_weights_bf16(E, pairs, stage, stage_cols=D):
    SY, GP, LD, PO = E.SY, E.GP, E.LD, E.PO
    base_po = PO.n
    for i, (src, dst) in enumerate(pairs):
        slot = i % 2
        if i >= 2:
            SY.wait_ge(PO.sem, base_po + i - 1)
        else:
            SY.wait_ge(PO.sem, base_po)
        LD.inc(SY.dma_start(out=stage[:, slot * stage_cols:(slot + 1) * stage_cols],
                            in_=src), 16)
        GP.wait_ge(LD.sem, LD.n)
        PO.inc(GP.tensor_copy(dst, stage[:, slot * stage_cols:(slot + 1) * stage_cols]))
    return PO.n


def build_nc(passes=("P0", "P1", "A0", "B0", "A1", "B1")):
    nc = bass.Bass("TRN2", target_bir_lowering=False)
    E = Env()
    E.nc = nc
    E.passes = passes
    E.fold_p6 = ("P6" not in passes) and ("B1" in passes)

    E.x0 = nc.dram_tensor("x0", [S, D], F32, kind="ExternalInput").ap()
    E.wq_d = nc.dram_tensor("wq", [D, D], F32, kind="ExternalInput").ap()
    E.wk_d = nc.dram_tensor("wk", [D, D], F32, kind="ExternalInput").ap()
    E.wv_d = nc.dram_tensor("wv", [D, D], F32, kind="ExternalInput").ap()
    E.wo_d = nc.dram_tensor("wo_a", [D, D], F32, kind="ExternalInput").ap()
    E.rf_d = nc.dram_tensor("rf_dup", [128, FA], F32, kind="ExternalInput").ap()
    E.f_wqkv = nc.dram_tensor("f_wqkv", [2, D, 3 * INNER], F32, kind="ExternalInput").ap()
    E.f_qs = nc.dram_tensor("f_qs", [2, DB], F32, kind="ExternalInput").ap()
    E.f_ks = nc.dram_tensor("f_ks", [2, DB], F32, kind="ExternalInput").ap()
    E.f_wo = nc.dram_tensor("f_wo", [2, INNER, D], F32, kind="ExternalInput").ap()
    E.f_g = nc.dram_tensor("f_g", [2, D], F32, kind="ExternalInput").ap()
    E.f_b = nc.dram_tensor("f_b", [2, D], F32, kind="ExternalInput").ap()
    E.f_w1 = nc.dram_tensor("f_w1", [2, D, 2 * HID], F32, kind="ExternalInput").ap()
    E.f_b1 = nc.dram_tensor("f_b1", [2, 2 * HID], F32, kind="ExternalInput").ap()
    E.f_w2 = nc.dram_tensor("f_w2", [2, HID, D], F32, kind="ExternalInput").ap()
    E.f_b2 = nc.dram_tensor("f_b2", [2, D], F32, kind="ExternalInput").ap()
    E.ident_d = nc.dram_tensor("ident", [128, 128], BF16, kind="ExternalInput").ap()
    E.ones_d = nc.dram_tensor("ones_bf", [128, 128], BF16, kind="ExternalInput").ap()
    E.shift_d = nc.dram_tensor("shift_tbl", [128, 1], F32, kind="ExternalInput").ap()
    E.t1q_d = nc.dram_tensor("t1q", [S, DB], F32, kind="ExternalInput").ap()
    E.t2q_d = nc.dram_tensor("t2q", [S, DB], F32, kind="ExternalInput").ap()
    E.t1k_d = nc.dram_tensor("t1k", [S, DB], F32, kind="ExternalInput").ap()
    E.t2k_d = nc.dram_tensor("t2k", [S, DB], F32, kind="ExternalInput").ap()
    E.mge_d = nc.dram_tensor("m_ge", [128, 128], F32, kind="ExternalInput").ap()
    E.mle_d = nc.dram_tensor("m_le", [128, 128], F32, kind="ExternalInput").ap()

    E.out = nc.dram_tensor("out", [S, D], F32, kind="ExternalOutput").ap()
    E.xA = nc.dram_tensor("xA", [S, D], F32, kind="Internal").ap()
    E.x1k = nc.dram_tensor("x1k", [S, D], F32, kind="Internal").ap()

    es = ExitStack()
    E.es = es

    def sbuf(name, shape, dtype):
        return es.enter_context(nc.sbuf_tensor(name, shape, dtype))
    E.sbuf = sbuf

    E.ident_sb = sbuf("ident_sb", [128, 128], BF16)
    E.ones_sb = sbuf("ones_sb", [128, 128], BF16)
    E.shift_sb = sbuf("shift_sb", [128, 1], F32)
    E.mge_sb = sbuf("mge_sb", [128, 128], F32)
    E.mle_sb = sbuf("mle_sb", [128, 128], F32)
    E.cns = sbuf("cns", [128, 4], F32)
    nc.const_aps.register(F32, INV2PI, E.cns[:, 0:1])
    nc.const_aps.register(F32, NEG2PI, E.cns[:, 1:2])
    nc.const_aps.register(F32, -0.5, E.cns[:, 2:3])
    nc.const_aps.register(F32, LNEPS, E.cns[:, 3:4])

    E.LD = Ctr(es, nc, "LD")
    E.ST = Ctr(es, nc, "ST")
    E.PE = Ctr(es, nc, "PE")
    E.DV = Ctr(es, nc, "DV")
    E.AC = Ctr(es, nc, "AC")
    E.PO = Ctr(es, nc, "PO")

    E.V, E.SC, E.TE, E.SY, E.GP = nc.vector, nc.scalar, nc.tensor, nc.sync, nc.gpsimd

    with nc.Block():
        E.LD.inc(E.SY.dma_start(out=E.ident_sb[:], in_=E.ident_d[:]), 16)
        E.LD.inc(E.SY.dma_start(out=E.ones_sb[:], in_=E.ones_d[:]), 16)
        E.LD.inc(E.SY.dma_start(out=E.shift_sb[:], in_=E.shift_d[:]), 16)
        E.LD.inc(E.SY.dma_start(out=E.mge_sb[:], in_=E.mge_d[:]), 16)
        E.LD.inc(E.SY.dma_start(out=E.mle_sb[:], in_=E.mle_d[:]), 16)
        E.V.memset(E.cns[:, 0:1], INV2PI)
        E.V.memset(E.cns[:, 2:3], -0.5)
        E.V.memset(E.cns[:, 3:4], LNEPS)
        E.DV.inc(E.V.memset(E.cns[:, 1:2], NEG2PI))
        E.cns_dv = E.DV.n
        E.consts_ld = E.LD.n

        E.stageA_es = ExitStack()
        E.ln1_fm = E.stageA_es.enter_context(nc.sbuf_tensor("ln1_fm", [128, 8 * S], BF16))
        E.kv_sb = E.stageA_es.enter_context(nc.sbuf_tensor("kv_sb", [128, HA * 65], BF16))
        E.kv_done = 0

        if "P0" in passes:
            with nc.named_scope("P0"):
                _emit_P0(E)
        p1lev = [p for p in passes if p.startswith("P1")]
        if p1lev:
            E.p1_level = int(p1lev[0][3:]) if "@" in p1lev[0] else 99
            with nc.named_scope("P1"):
                _emit_P1(E)
        if any(("A%d" % li) in passes or ("B%d" % li) in passes for li in (0, 1)):
            E.stageA_es.close()
            E.stageA_closed = True
        for li in (0, 1):
            if ("A%d" % li) in passes:
                with nc.named_scope("A%d" % li):
                    _emit_A(E, li)
            if ("B%d" % li) in passes:
                with nc.named_scope("B%d" % li):
                    _emit_B(E, li)
        if "P6" in passes:
            with nc.named_scope("P6"):
                _emit_P6(E)
        if "DBGKV" in passes:
            V, GP, SY, LD, ST, PO, DV = E.V, E.GP, E.SY, E.LD, E.ST, E.PO, E.DV
            tmp = E.stageA_es.enter_context(nc.sbuf_tensor("dbgkv_tmp", [128, 1040], F32))
            V.wait_ge(DV.sem, E.kv_done)
            DV.inc(V.tensor_copy(tmp[:], E.kv_sb[:]))
            GP.wait_ge(DV.sem, E.DV.n)
            ST.inc(GP.dma_start(out=E.out[0:128, 0:1024], in_=tmp[:, 0:1024]), 16)
            ST.inc(GP.dma_start(out=E.out[128:256, 0:16], in_=tmp[:, 1024:1040]), 16)
            GP.wait_ge(ST.sem, ST.n)
        if "DBGX1" in passes or "DBGXA" in passes:
            srcT = E.x1k if "DBGX1" in passes else E.xA
            V, GP, SY, LD, ST, PO = E.V, E.GP, E.SY, E.LD, E.ST, E.PO
            SY.wait_ge(ST.sem, ST.n)
            tmp = (sbuf("dbg_tmp", [128, D], F32)
                   if getattr(E, "stageA_closed", False)
                   else E.stageA_es.enter_context(nc.sbuf_tensor("dbg_tmp", [128, D], F32)))
            for i in range(NT):
                LD.inc(SY.dma_start(out=tmp[:], in_=srcT[i * 128:(i + 1) * 128, :]), 16)
                GP.wait_ge(LD.sem, LD.n)
                ST.inc(GP.dma_start(out=E.out[i * 128:(i + 1) * 128, :], in_=tmp[:]), 16)
                SY.wait_ge(ST.sem, ST.n)
            GP.wait_ge(ST.sem, ST.n)

    if not getattr(E, "stageA_closed", False):
        E.stageA_es.close()
    es.close()
    return nc


def _emit_P0(E):
    nc = E.nc
    V, SC, TE, SY, GP = E.V, E.SC, E.TE, E.SY, E.GP
    LD, ST, PE, DV, AC, PO = E.LD, E.ST, E.PE, E.DV, E.AC, E.PO

    p0 = ExitStack()

    def sb(name, shape, dtype):
        return p0.enter_context(nc.sbuf_tensor(name, shape, dtype))

    def ps(name, shape, dtype=F32):
        return p0.enter_context(nc.psum_tensor(name, shape, dtype))

    wk_bf = sb("wk_bf", [128, 8 * D], BF16)
    wv_bf = sb("wv_bf", [128, 8 * D], BF16)
    wstage = sb("p0_wstage", [128, 2 * D], F32)
    rf_sb = sb("rf_sb", [128, FA], F32)
    rf_bf = sb("rf_bf", [128, FA], BF16)

    x_in = sb("p0_x_in", [128, 2 * D], F32)
    ln1_tm = sb("p0_ln1_tm", [128, D], BF16)
    sq_scr = sb("p0_sq_scr", [128, D], BF16)
    stt = sb("p0_stt", [128, 8], F32)
    kfm = sb("p0_kfm", [128, 8 * 512], BF16)
    kp_fm = sb("p0_kp_fm", [128, 512], BF16)
    kp_tm = sb("p0_kp_tm", [128, HA * 512], BF16)
    v_tm = sb("p0_v_tm", [128, 4 * 1040], BF16)
    u0b = sb("p0_u0", [128, 512], F32)
    tb = sb("p0_t", [128, 512], F32)
    rb = sb("p0_r", [128, 512], F32)

    ps_a0 = ps("ps_a0", [128, 1024], F32)
    ps_a1 = ps("ps_a1", [128, 1024], F32)
    ps_b = ps("ps_b", [128, 1024], BF16)
    ps_kv = [ps("ps_kv0", [128, 455], F32), ps("ps_kv1", [128, 455], F32),
             ps("ps_kv2", [128, 130], F32)]

    def kv_slice(h):
        return ps_kv[h // 7][:, (h % 7) * 65:(h % 7) * 65 + 65]

    if not getattr(E, "p0_skip_weights", False):
        LD.inc(SY.dma_start(out=rf_sb[:], in_=rf_d_ap(E)), 16)
        GP.wait_ge(LD.sem, LD.n)
        PO.inc(GP.tensor_copy(rf_bf[:], rf_sb[:]))
        pairs = [(E.wk_d[i * 128:(i + 1) * 128, :], wk_bf[:, i * D:(i + 1) * D]) for i in range(8)]
        pairs += [(E.wv_d[i * 128:(i + 1) * 128, :], wv_bf[:, i * D:(i + 1) * D]) for i in range(8)]
        w_ready = _load_weights_bf16(E, pairs, wstage)
    else:
        w_ready = PO.n

    if not getattr(E, "p0_skip_memset", False):
        for tt in range(4):
            V.memset(v_tm[:, tt * 1040:(tt + 1) * 1040]
                     .rearrange("p (h s) -> p h s", s=65)[:, :, 64:65], 1.0)

    x_free_dv = {}
    evac = {}
    kv_started = [False, False, False]
    prev_norm_guard = 0
    prev_sin_guard = 0
    prev_sq_guard = 0

    p0_nch = getattr(E, "p0_nch", NCH)
    p0_tiles = getattr(E, "p0_tiles", CT)
    for c in range(p0_nch):
        c0 = c * 512
        for j in range(p0_tiles):
            i = c * CT + j
            slot = i % 2
            if i >= 2:
                SY.wait_ge(DV.sem, x_free_dv[i - 2])
            LD.inc(SY.dma_start(out=x_in[:, slot * D:slot * D + D],
                                in_=E.x0[i * 128:(i + 1) * 128, :]), 16)
            x_ld = LD.n

            src = x_in[:, slot * D:slot * D + D]
            SC.wait_ge(LD.sem, x_ld)
            V.wait_ge(LD.sem, x_ld)
            _ln_stats(E, src, stt, sq_scr, D,
                      prev_guard=(DV.sem, prev_sq_guard) if prev_sq_guard else None)
            if prev_norm_guard:
                V.wait_ge(PE.sem, prev_norm_guard)
            DV.inc(V.tensor_scalar(out=ln1_tm[:], in0=src,
                                   scalar1=stt[:, 2:3], scalar2=stt[:, 3:4],
                                   op0=ALU.subtract, op1=ALU.mult))
            x_free_dv[i] = DV.n
            prev_sq_guard = DV.n
            norm_dv = DV.n

            TE.wait_ge(DV.sem, norm_dv)
            TE.wait_ge(LD.sem, E.consts_ld)
            if "ps_b" in evac:
                TE.wait_ge(DV.sem, evac["ps_b"])
            for ft in range(8):
                mm = TE.transpose(ps_b[:, ft * 128:(ft + 1) * 128],
                                  ln1_tm[:, ft * 128:(ft + 1) * 128], E.ident_sb[:])
                if ft == 7:
                    PE.inc(mm)
            prev_norm_guard = PE.n
            V.wait_ge(PE.sem, PE.n)
            DV.inc(V.tensor_copy(
                E.ln1_fm[:].rearrange("p (f t) -> p f t", t=S)[:, :, i * 128:(i + 1) * 128],
                ps_b[:].rearrange("p (f t) -> p f t", t=128)))
            evac["ps_b"] = DV.n

        ln1_ready = DV.n
        if getattr(E, "p0_skip_rest", False):
            continue

        TE.wait_ge(DV.sem, ln1_ready)
        if c == 0:
            TE.wait_ge(PO.sem, w_ready)
        for ft in range(8):
            half = [ps_a0, ps_a1][ft % 2][:, 0:512]
            key = ("ak", ft % 2)
            if key in evac:
                TE.wait_ge(DV.sem, evac[key])
            if ft % 2 == 0 and ("v",) in evac:
                TE.wait_ge(DV.sem, evac[("v",)])
            for kt in range(8):
                mm = TE.matmul(half,
                               wk_bf[:, kt * D + ft * 128: kt * D + (ft + 1) * 128],
                               E.ln1_fm[:, kt * S + c0: kt * S + c0 + 512],
                               start=(kt == 0), stop=(kt == 7))
                if kt == 7:
                    PE.inc(mm)
            V.wait_ge(PE.sem, PE.n)
            DV.inc(V.tensor_copy(kfm[:, ft * 512:(ft + 1) * 512], half))
            evac[key] = DV.n
        kfm_ready = DV.n

        for tt in range(CT):
            TE.wait_ge(DV.sem, evac[("ak", 0)])
            if ("v",) in evac:
                TE.wait_ge(DV.sem, evac[("v",)])
            for kt in range(8):
                for nh in range(2):
                    mm = TE.matmul(ps_a0[:, nh * 512:(nh + 1) * 512],
                                   E.ln1_fm[:, kt * S + c0 + tt * 128: kt * S + c0 + (tt + 1) * 128],
                                   wv_bf[:, kt * D + nh * 512: kt * D + (nh + 1) * 512],
                                   start=(kt == 0), stop=(kt == 7))
                    if kt == 7 and nh == 1:
                        PE.inc(mm)
            V.wait_ge(PE.sem, PE.n)
            DV.inc(V.tensor_copy(
                v_tm[:, tt * 1040:(tt + 1) * 1040]
                .rearrange("p (h s) -> p h s", s=65)[:, :, 0:64],
                ps_a0[:].rearrange("p (h d) -> p h d", d=64)))
            evac[("v",)] = DV.n

        for h in range(HA):
            base = (h % 2) * 64
            proj = ps_a1[:, 512:1024]
            TE.wait_ge(DV.sem, kfm_ready)
            if ("projk",) in evac:
                TE.wait_ge(AC.sem, evac[("projk",)])
            PE.inc(TE.matmul(proj, rf_bf[base:base + 64, :],
                             kfm[base:base + 64, (h // 2) * 512:(h // 2) * 512 + 512],
                             start=True, stop=True))
            SC.wait_ge(PE.sem, PE.n)
            SC.wait_ge(DV.sem, max(E.cns_dv, 0))
            SC.wait_ge(LD.sem, E.consts_ld)
            AC.inc(SC.activation(out=u0b[:], in_=proj, func=AF.Identity,
                                 scale=INV2PI, bias=E.shift_sb[:, 0:1]))
            evac[("projk",)] = AC.n
            V.wait_ge(AC.sem, AC.n)
            DV.inc(V.tensor_scalar(tb[:], u0b[:], MAGIC, None, ALU.add))
            V.wait_ge(DV.sem, DV.n)
            DV.inc(V.scalar_tensor_tensor(out=rb[:], in0=tb[:], scalar=MAGIC,
                                          in1=u0b[:], op0=ALU.subtract,
                                          op1=ALU.subtract))
            SC.wait_ge(DV.sem, DV.n)
            if prev_sin_guard:
                SC.wait_ge(PE.sem, prev_sin_guard)
            AC.inc(SC.activation(out=kp_fm[:], in_=rb[:], func=AF.Sin, scale=NEG2PI))
            kp_ac = AC.n

            TE.wait_ge(AC.sem, kp_ac)
            TE.wait_ge(DV.sem, evac["ps_b"])
            for tt in range(CT):
                mm = TE.transpose(ps_b[:, tt * 128:(tt + 1) * 128],
                                  kp_fm[:, tt * 128:(tt + 1) * 128], E.ident_sb[:])
                if tt == 3:
                    PE.inc(mm)
            V.wait_ge(PE.sem, PE.n)
            DV.inc(V.tensor_copy(kp_tm[:, h * 512:(h + 1) * 512], ps_b[:, 0:512]))
            evac["ps_b"] = DV.n
            kpt_dv = DV.n

            TE.wait_ge(DV.sem, kpt_dv)
            bank = h // 7
            for tt in range(CT):
                st_flag = not kv_started[bank]
                kv_started[bank] = True
                mm = TE.matmul(kv_slice(h),
                               kp_tm[:, h * 512 + tt * 128: h * 512 + (tt + 1) * 128],
                               v_tm[:, tt * 1040 + h * 65: tt * 1040 + h * 65 + 65],
                               start=st_flag,
                               stop=(c == NCH - 1 and tt == CT - 1))
                if tt == CT - 1:
                    PE.inc(mm)
            prev_sin_guard = PE.n

    V.wait_ge(PE.sem, PE.n)
    DV.inc(V.tensor_copy(E.kv_sb[:, 0:455], ps_kv[0][:]))
    DV.inc(V.tensor_copy(E.kv_sb[:, 455:910], ps_kv[1][:]))
    DV.inc(V.tensor_copy(E.kv_sb[:, 910:1040], ps_kv[2][:]))
    E.kv_done = DV.n
    dbg = getattr(E, "p0_dbg", None)
    if dbg:
        tmp = sb("p0dbg_tmp", [128, 1024], F32)
        row = 0
        for name in dbg:
            buf = {"ln1": E.ln1_fm, "kfm": kfm, "kp": kp_tm, "v": v_tm,
                   "lntm": ln1_tm, "stt": stt, "xin": x_in}[name]
            ncols = {"ln1": 4096, "kfm": 4096, "kp": 4096, "v": 4160,
                     "lntm": 1024, "stt": 8, "xin": 2048}[name]
            for blk in range(0, min(ncols, 4096), 1024):
                w = min(1024, ncols - blk)
                V.wait_ge(DV.sem, DV.n)
                V.wait_ge(ST.sem, ST.n)
                V.memset(tmp[:], 0.0)
                DV.inc(V.tensor_copy(tmp[:, 0:w], buf[:, blk:blk + w]))
                GP.wait_ge(DV.sem, DV.n)
                ST.inc(GP.dma_start(out=E.out[row * 128:(row + 1) * 128, :], in_=tmp[:]), 16)
                GP.wait_ge(ST.sem, ST.n)
                row += 1
    p0.close()


def rf_d_ap(E):
    return E.rf_d[:]


def _emit_P1(E):
    nc = E.nc
    V, SC, TE, SY, GP = E.V, E.SC, E.TE, E.SY, E.GP
    LD, ST, PE, DV, AC, PO = E.LD, E.ST, E.PE, E.DV, E.AC, E.PO

    p1 = ExitStack()

    def sb(name, shape, dtype):
        return p1.enter_context(nc.sbuf_tensor(name, shape, dtype))

    def ps(name, shape, dtype=F32):
        return p1.enter_context(nc.psum_tensor(name, shape, dtype))

    _pass_barrier(E)
    wq_bf = sb("wq_bf", [128, 8 * D], BF16)
    wo_bf = sb("wo_bf", [128, 8 * D], BF16)
    wstage = sb("p1_wstage", [128, 2 * D], F32)
    rf_sb = sb("p1_rf_sb", [128, FA], F32)
    rf_bf = sb("p1_rf_bf", [128, FA], BF16)

    qfm = sb("p1_qfm", [128, 8 * 512], BF16)
    qp_all = sb("p1_qp", [128, HA * 512], BF16)
    u0b = sb("p1_u0", [128, 512], F32)
    tb = sb("p1_t", [128, 512], F32)
    rb = sb("p1_r", [128, 512], F32)
    zinv = sb("p1_zinv", [128, 16], F32)
    attn_tm = sb("p1_attn_tm", [128, D], BF16)
    attn_fm = sb("p1_attn_fm", [128, D], BF16)
    x0_in = sb("p1_x0_in", [128, 2 * D], F32)
    ln2_sb = sb("p1_ln2", [128, D], F32)
    x1_sb = sb("p1_x1", [128, D], F32)
    xa_sb = sb("p1_xa", [128, D], F32)
    sq_scr = sb("p1_sq_scr", [128, D], BF16)
    stt = sb("p1_stt", [128, 8], F32)

    ps_q = [ps("ps_q0", [128, 512], F32), ps("ps_q1", [128, 512], F32)]
    ps_qkv = ps("ps_qkv", [128, 1024], F32)
    ps_z = ps("ps_z", [128, 16], F32)
    ps_b = ps("p1_ps_b", [128, 1024], BF16)
    ps_wo = ps("ps_wo", [128, 1024], F32)

    LD.inc(SY.dma_start(out=rf_sb[:], in_=E.rf_d[:]), 16)
    GP.wait_ge(LD.sem, LD.n)
    PO.inc(GP.tensor_copy(rf_bf[:], rf_sb[:]))
    pairs = [(E.wq_d[i * 128:(i + 1) * 128, :], wq_bf[:, i * D:(i + 1) * D]) for i in range(8)]
    pairs += [(E.wo_d[i * 128:(i + 1) * 128, :], wo_bf[:, i * D:(i + 1) * D]) for i in range(8)]
    w_ready = _load_weights_bf16(E, pairs, wstage)

    evac = {}
    x0_free = {}
    prev_qkv_pe = 0
    prev_sq_guard = 0
    prev_store_guard = 0
    prev_xa_ac = 0
    prev_xa_st = 0

    lev = getattr(E, "p1_level", 99)
    for c in range(NCH):
        c0 = c * 512
        if lev < 10:
            break
        if c == 0:
            TE.wait_ge(PO.sem, w_ready)
            if E.kv_done:
                TE.wait_ge(DV.sem, E.kv_done)
        for ft in range(8):
            half = ps_q[ft % 2][:]
            key = ("q", ft % 2)
            if key in evac:
                TE.wait_ge(DV.sem, evac[key])
            if ft == 0 and ("proj",) in evac:
                TE.wait_ge(AC.sem, evac[("proj",)])
            for kt in range(8):
                mm = TE.matmul(half,
                               wq_bf[:, kt * D + ft * 128: kt * D + (ft + 1) * 128],
                               E.ln1_fm[:, kt * S + c0: kt * S + c0 + 512],
                               start=(kt == 0), stop=(kt == 7))
                if kt == 7:
                    PE.inc(mm)
            V.wait_ge(PE.sem, PE.n)
            DV.inc(V.tensor_copy(qfm[:, ft * 512:(ft + 1) * 512], half))
            evac[key] = DV.n
        qfm_ready = DV.n

        for h in range(HA if lev >= 20 else 0):
            base = (h % 2) * 64
            TE.wait_ge(DV.sem, qfm_ready if h == 0 else DV.n)
            if h == 0:
                TE.wait_ge(DV.sem, evac[("q", 0)])
            if ("proj",) in evac:
                TE.wait_ge(AC.sem, evac[("proj",)])
            PE.inc(TE.matmul(ps_q[0][:], rf_bf[base:base + 64, :],
                             qfm[base:base + 64, (h // 2) * 512:(h // 2) * 512 + 512],
                             start=True, stop=True))
            SC.wait_ge(PE.sem, PE.n)
            if prev_qkv_pe and h == 0:
                SC.wait_ge(PE.sem, prev_qkv_pe)
            SC.wait_ge(DV.sem, max(E.cns_dv, 0))
            SC.wait_ge(LD.sem, E.consts_ld)
            AC.inc(SC.activation(out=u0b[:], in_=ps_q[0][:], func=AF.Identity,
                                 scale=INV2PI, bias=E.shift_sb[:, 0:1]))
            evac[("proj",)] = AC.n
            V.wait_ge(AC.sem, AC.n)
            DV.inc(V.tensor_scalar(tb[:], u0b[:], MAGIC, None, ALU.add))
            V.wait_ge(DV.sem, DV.n)
            DV.inc(V.scalar_tensor_tensor(out=rb[:], in0=tb[:], scalar=MAGIC,
                                          in1=u0b[:], op0=ALU.subtract,
                                          op1=ALU.subtract))
            SC.wait_ge(DV.sem, DV.n)
            AC.inc(SC.activation(out=qp_all[:, h * 512:(h + 1) * 512], in_=rb[:],
                                 func=AF.Sin, scale=NEG2PI))
        qp_ready = AC.n

        for tt in range(CT if lev >= 30 else 0):
            i = c * CT + tt
            slot = i % 2
            if i >= 2 and (i - 2) in x0_free:
                SY.wait_ge(DV.sem, x0_free[i - 2])
            LD.inc(SY.dma_start(out=x0_in[:, slot * D:slot * D + D],
                                in_=E.x0[i * 128:(i + 1) * 128, :]), 16)
            x_ld = LD.n

            TE.wait_ge(AC.sem, qp_ready)
            if ("qkv",) in evac:
                TE.wait_ge(DV.sem, evac[("qkv",)])
            for h in range(HA):
                mm = None
                if lev != 32:
                    mm = TE.matmul(ps_qkv[:, h * 64:(h + 1) * 64],
                                   qp_all[:, h * 512 + tt * 128: h * 512 + (tt + 1) * 128],
                                   E.kv_sb[:, h * 65: h * 65 + 64],
                                   start=True, stop=True)
                if lev != 31:
                    mm = TE.matmul(ps_z[:, h:h + 1],
                                   qp_all[:, h * 512 + tt * 128: h * 512 + (tt + 1) * 128],
                                   E.kv_sb[:, h * 65 + 64: h * 65 + 65],
                                   start=True, stop=True)
                if h == HA - 1:
                    PE.inc(mm)
            qkv_pe = PE.n
            prev_qkv_pe = qkv_pe
            V.wait_ge(PE.sem, qkv_pe)
            if lev >= 35:
                DV.inc(V.tensor_scalar_max(zinv[:], ps_z[:], ZCLIP))
                V.wait_ge(DV.sem, DV.n)
                DV.inc(V.reciprocal(zinv[:], zinv[:]))
                V.wait_ge(DV.sem, DV.n)
                DV.inc(V.tensor_tensor(
                    out=attn_tm[:].rearrange("p (h d) -> p h d", d=64),
                    in0=ps_qkv[:].rearrange("p (h d) -> p h d", d=64),
                    in1=zinv[:, :, None].broadcast_to([128, 16, 64]),
                    op=ALU.mult))
            else:
                DV.inc(V.tensor_copy(attn_tm[:], ps_qkv[:]))
            evac[("qkv",)] = DV.n
            attn_dv = DV.n

            if lev < 40:
                continue
            TE.wait_ge(DV.sem, attn_dv)
            TE.wait_ge(LD.sem, E.consts_ld)
            if ("ps_b",) in evac:
                TE.wait_ge(DV.sem, evac[("ps_b",)])
            for ft in range(8):
                mm = TE.transpose(ps_b[:, ft * 128:(ft + 1) * 128],
                                  attn_tm[:, ft * 128:(ft + 1) * 128], E.ident_sb[:])
                if ft == 7:
                    PE.inc(mm)
            V.wait_ge(PE.sem, PE.n)
            DV.inc(V.tensor_copy(attn_fm[:], ps_b[:]))
            evac[("ps_b",)] = DV.n
            afm_dv = DV.n

            if lev < 50:
                continue
            TE.wait_ge(DV.sem, afm_dv)
            if ("wo",) in evac:
                TE.wait_ge(AC.sem, evac[("wo",)])
            for kt in range(8):
                for nh in range(2):
                    mm = TE.matmul(ps_wo[:, nh * 512:(nh + 1) * 512],
                                   attn_fm[:, kt * 128:(kt + 1) * 128],
                                   wo_bf[:, kt * D + nh * 512: kt * D + (nh + 1) * 512],
                                   start=(kt == 0), stop=(kt == 7))
                    if kt == 7 and nh == 1:
                        PE.inc(mm)
            wo_pe = PE.n

            if lev < 61:
                continue
            SC.wait_ge(PE.sem, wo_pe)
            V.wait_ge(PE.sem, wo_pe)
            _ln_stats(E, ps_wo[:], stt, sq_scr, D,
                      prev_guard=(DV.sem, prev_sq_guard) if prev_sq_guard else None)
            stat_dv = DV.n
            prev_sq_guard = DV.n
            if lev < 62:
                evac[("wo",)] = 0
                continue
            SC.wait_ge(DV.sem, stat_dv)
            AC.inc(SC.activation(out=ln2_sb[:], in_=ps_wo[:], func=AF.Identity,
                                 scale=stt[:, 3:4], bias=stt[:, 4:5]))
            evac[("wo",)] = AC.n
            ln2_ac = AC.n
            if lev < 63:
                continue

            V.wait_ge(AC.sem, ln2_ac)
            V.wait_ge(LD.sem, x_ld)
            if prev_store_guard:
                V.wait_ge(ST.sem, prev_store_guard)
            if prev_xa_ac:
                V.wait_ge(AC.sem, prev_xa_ac)
            DV.inc(V.tensor_add(x1_sb[:], x0_in[:, slot * D:slot * D + D], ln2_sb[:]))
            x1_dv = DV.n
            x0_free[i] = DV.n
            GP.wait_ge(DV.sem, x1_dv)
            ST.inc(GP.dma_start(out=E.x1k[i * 128:(i + 1) * 128, :], in_=x1_sb[:]), 16)
            prev_store_guard = ST.n
            if lev < 64:
                continue

            SC.wait_ge(DV.sem, x1_dv)
            _ln_stats(E, x1_sb[:], stt, sq_scr, D, prev_guard=(DV.sem, stat_dv))
            prev_sq_guard = DV.n
            stat2_dv = DV.n
            SC.wait_ge(DV.sem, stat2_dv)
            if prev_xa_st:
                SC.wait_ge(ST.sem, prev_xa_st)
            AC.inc(SC.activation(out=xa_sb[:], in_=x1_sb[:], func=AF.Identity,
                                 scale=stt[:, 3:4], bias=stt[:, 4:5]))
            prev_xa_ac = AC.n
            GP.wait_ge(AC.sem, prev_xa_ac)
            ST.inc(GP.dma_start(out=E.xA[i * 128:(i + 1) * 128, :], in_=xa_sb[:]), 16)
            prev_store_guard = ST.n
            prev_xa_st = ST.n

    dbg = getattr(E, "p1_dbg", None)
    if dbg:
        tmp = sb("p1dbg_tmp", [128, 1024], F32)
        row = 0
        for name in dbg:
            buf = {"qfm": qfm, "qp": qp_all, "attn_tm": attn_tm, "attn_fm": attn_fm,
                   "zinv": zinv, "stt": stt, "ln2": ln2_sb, "x1": x1_sb, "xa": xa_sb}[name]
            ncols = {"qfm": 4096, "qp": 4096, "attn_tm": 1024, "attn_fm": 1024,
                     "zinv": 16, "stt": 8, "ln2": 1024, "x1": 1024, "xa": 1024}[name]
            for blk in range(0, min(ncols, 4096), 1024):
                w = min(1024, ncols - blk)
                V.wait_ge(DV.sem, DV.n)
                V.wait_ge(ST.sem, ST.n)
                V.wait_ge(PO.sem, PO.n)
                V.wait_ge(AC.sem, AC.n)
                V.memset(tmp[:], 0.0)
                DV.inc(V.tensor_copy(tmp[:, 0:w], buf[:, blk:blk + w]))
                GP.wait_ge(DV.sem, DV.n)
                ST.inc(GP.dma_start(out=E.out[row * 128:(row + 1) * 128, :], in_=tmp[:]), 16)
                GP.wait_ge(ST.sem, ST.n)
                row += 1
    p1.close()


def _emit_A(E, li):
    nc = E.nc
    V, SC, TE, SY, GP = E.V, E.SC, E.TE, E.SY, E.GP
    LD, ST, PE, DV, AC, PO = E.LD, E.ST, E.PE, E.DV, E.AC, E.PO

    pa = ExitStack()

    def sb(name, shape, dtype):
        return pa.enter_context(nc.sbuf_tensor("a%d_%s" % (li, name), shape, dtype))

    def ps(name, shape, dtype=F32):
        return pa.enter_context(nc.psum_tensor("a%d_%s" % (li, name), shape, dtype))

    _pass_barrier(E)

    wqkv_bf = sb("wqkv", [128, 8 * 1536], BF16)
    wqkv_lo = sb("wqkv_lo", [128, 8 * 1536], BF16)
    wo_bf = sb("wo", [128, 4 * 1024], BF16)
    wstage = sb("wstage", [128, 2 * 1536], F32)
    tq1 = sb("tq1", [128, 32 * 64], F32)
    tq2 = sb("tq2", [128, 32 * 64], F32)
    tk1 = sb("tk1", [128, 32 * 64], F32)
    tk2 = sb("tk2", [128, 32 * 64], F32)
    qs_sb = sb("qs", [1, 64], F32)
    ones_f = sb("ones_f", [1, 128], F32)
    ks_sb = sb("ks", [1, 64], F32)
    sc_mat = sb("sc_mat", [128, 4 * 64], F32)
    kfm_roll = sb("kroll", [128, 5 * 512], BF16)
    v_roll = sb("vroll", [128, 5 * 512], BF16)
    x_tm = sb("x_tm", [128, 2 * D], F32)
    x_bf = sb("x_bf", [128, 2 * D], BF16)
    x_fm = sb("x_fm", [128, 2 * 8 * 129], BF16)
    q1b = sb("q1", [128, 512], F32)
    k1b = sb("k1", [128, 512], F32)
    rta = sb("rta", [128, 512], F32)
    rtb = sb("rtb", [128, 512], F32)
    q2b = sb("q2", [128, 512], BF16)
    k2b = sb("k2", [128, 512], BF16)
    q_fmc = sb("qfmc", [128, 512], BF16)
    attn_f = sb("attnf", [128, 2 * 640], F32)
    attn_sb = sb("attn", [128, 2 * 640], BF16)
    attn_fm = sb("attnfm", [128, 2 * 640], BF16)
    mcat = sb("mcat", [128, 256], F32)
    o_fm = sb("ofm", [128, 512], BF16)
    x_out = sb("x_out", [128, 2 * D], F32)
    stt = sb("stt", [128, 24], F32)
    ident_f = sb("ident_f", [128, 128], F32)
    q2f = sb("q2f", [128, 512], F32)
    k2f = sb("k2f", [128, 512], F32)
    qfm_f = sb("qfm_f", [128, 512], F32)
    kf_roll = sb("kf_roll", [128, 2 * 512], F32)
    vf_roll = sb("vf_roll", [128, 2 * 512], F32)
    attn_nf = sb("attn_nf", [128, 256], F32)
    afm_f = sb("afm_f", [128, 256], F32)
    xlo_bf = sb("xlo_bf", [128, D], BF16)
    xlo_f = sb("xlo_f", [128, D], F32)
    x_fm_lo = sb("x_fm_lo", [128, 2 * 8 * 129], BF16)

    ps_pool = ps("pool", [128, 2048], F32)
    ps_big = ps_pool[:, 0:1536]
    ps_b = ps("trp", [128, 1024], BF16)
    ps_o = ps("o", [64, 1024], F32)
    ps_f = ps("f32aux", [128, 512], F32)
    ps_qm = ps_f[:, 0:64]
    ps_tf = ps_f

    base_po = PO.n
    for i in range(8):
        slot = i % 2
        if i >= 2:
            SY.wait_ge(PO.sem, base_po + 2 * (i - 1))
        else:
            SY.wait_ge(PO.sem, base_po)
        LD.inc(SY.dma_start(out=wstage[:, slot * 1536:(slot + 1) * 1536],
                            in_=E.f_wqkv[li, i * 128:(i + 1) * 128, :]), 16)
        GP.wait_ge(LD.sem, LD.n)
        PO.inc(GP.tensor_copy(wqkv_bf[:, i * 1536:(i + 1) * 1536],
                              wstage[:, slot * 1536:(slot + 1) * 1536]))
        GP.wait_ge(PO.sem, PO.n)
        PO.inc(GP.tensor_tensor(out=wqkv_lo[:, i * 1536:(i + 1) * 1536],
                                in0=wstage[:, slot * 1536:(slot + 1) * 1536],
                                in1=wqkv_bf[:, i * 1536:(i + 1) * 1536],
                                op=ALU.subtract))
    w_ready = PO.n
    pairs = [(E.f_wo[li, i * 128:(i + 1) * 128, :], wo_bf[:, i * 1024:(i + 1) * 1024])
             for i in range(4)]
    w_ready = _load_weights_bf16(E, pairs, wstage, stage_cols=1024)

    for tbl_d, tbl in ((E.t1q_d, tq1), (E.t2q_d, tq2), (E.t1k_d, tk1), (E.t2k_d, tk2)):
        LD.inc(SY.dma_start(out=tbl[:].rearrange("p (c d) -> p c d", d=64),
                            in_=tbl_d.rearrange("(c p) d -> p c d", p=128)), 16)
    LD.inc(SY.dma_start(out=qs_sb[:], in_=E.f_qs[li:li + 1, :]), 16)
    LD.inc(SY.dma_start(out=ks_sb[:], in_=E.f_ks[li:li + 1, :]), 16)
    tbl_ld = LD.n
    V.wait_ge(LD.sem, E.consts_ld)
    V.memset(ones_f[:], 1.0)
    V.tensor_copy(mcat[:, 0:128], E.mge_sb[:])
    V.tensor_copy(mcat[:, 128:256], E.mle_sb[:])
    DV.inc(V.tensor_copy(ident_f[:], E.ident_sb[:]))
    ones_dv = DV.n
    TE.wait_ge(LD.sem, tbl_ld)
    TE.wait_ge(DV.sem, ones_dv)
    for idx, (row, swap) in enumerate(((qs_sb, False), (qs_sb, True),
                                      (ks_sb, False), (ks_sb, True))):
        rhs = row[0:1, :]
        if swap:
            rhs = row[0:1, :].rearrange("p (two d) -> p two d", two=2)[:, ::-1, :]
        if idx > 0:
            TE.wait_ge(DV.sem, DV.n)
        PE.inc(TE.matmul(ps_qm[:], ones_f[0:1, :], rhs, start=True, stop=True))
        V.wait_ge(PE.sem, PE.n)
        DV.inc(V.tensor_copy(sc_mat[:, idx * 64:(idx + 1) * 64], ps_qm[:]))
    V.wait_ge(DV.sem, DV.n)
    for tbl, mi in ((tq1, 0), (tq2, 1), (tk1, 2), (tk2, 3)):
        DV.inc(V.tensor_tensor(
            out=tbl[:].rearrange("p (c d) -> p c d", d=64),
            in0=tbl[:].rearrange("p (c d) -> p c d", d=64),
            in1=sc_mat[:, mi * 64:(mi + 1) * 64][:, None, :].broadcast_to([128, 32, 64]),
            op=ALU.mult))
    tbl_dv = DV.n

    evac = {}
    rel = {}
    first_in_bank = {"o0": True, "o1": True}
    prev_heads_pe = 0

    for qt in range(NT):
        slot = qt % 2
        kts = list(range(max(0, qt - 4), qt + 1))
        nkt = len(kts)
        w = nkt * 128

        if ("xtm", slot) in rel:
            SY.wait_ge(DV.sem, rel[("xtm", slot)])
        LD.inc(SY.dma_start(out=x_tm[:, slot * D:slot * D + D],
                            in_=E.xA[qt * 128:(qt + 1) * 128, :]), 16)
        x_ld = LD.n

        V.wait_ge(LD.sem, x_ld)
        if ("xbf", slot) in rel:
            V.wait_ge(PE.sem, rel[("xbf", slot)])
        DV.inc(V.tensor_copy(x_bf[:, slot * D:slot * D + D],
                             x_tm[:, slot * D:slot * D + D]))
        xbf_dv = DV.n

        if qt < 2:
            V.wait_ge(DV.sem, xbf_dv)
            DV.inc(V.tensor_sub(xlo_f[:], x_tm[:, slot * D:slot * D + D],
                                x_bf[:, slot * D:slot * D + D]))
            V.wait_ge(DV.sem, DV.n)
            DV.inc(V.tensor_copy(xlo_bf[:], xlo_f[:]))
            xbf_dv = DV.n
        TE.wait_ge(DV.sem, xbf_dv)
        if qt == 0:
            TE.wait_ge(PO.sem, w_ready)
            TE.wait_ge(DV.sem, tbl_dv)
        if "ps_b" in evac:
            TE.wait_ge(DV.sem, evac["ps_b"])
        for ft in range(8):
            mm = TE.transpose(ps_b[:, ft * 128:(ft + 1) * 128],
                              x_bf[:, slot * D + ft * 128: slot * D + (ft + 1) * 128],
                              E.ident_sb[:])
            if ft == 7:
                PE.inc(mm)
        rel[("xbf", slot)] = PE.n
        V.wait_ge(PE.sem, PE.n)
        if ("xfm", slot) in rel:
            V.wait_ge(PE.sem, rel[("xfm", slot)])
        base = slot * 1032
        pbase = (1 - slot) * 1032
        DV.inc(V.tensor_copy(
            x_fm[:, base:base + 1032].rearrange("p (f t) -> p f t", t=129)[:, :, 1:129],
            ps_b[:].rearrange("p (f t) -> p f t", t=128)))
        if qt == 0:
            V.memset(x_fm[:, base:base + 1032]
                     .rearrange("p (f t) -> p f t", t=129)[:, :, 0:1], 0.0)
            DV.inc(V.tensor_copy(stt[:, 23:24], stt[:, 23:24]))
        else:
            DV.inc(V.tensor_copy(
                x_fm[:, base:base + 1032].rearrange("p (f t) -> p f t", t=129)[:, :, 0:1],
                x_fm[:, pbase:pbase + 1032].rearrange("p (f t) -> p f t", t=129)[:, :, 128:129]))
        if qt < 2:
            TE.wait_ge(DV.sem, DV.n)
            for ft in range(8):
                mm = TE.transpose(ps_b[:, ft * 128:(ft + 1) * 128],
                                  xlo_bf[:, ft * 128:(ft + 1) * 128], E.ident_sb[:])
                if ft == 7:
                    PE.inc(mm)
            V.wait_ge(PE.sem, PE.n)
            DV.inc(V.tensor_copy(
                x_fm_lo[:, base:base + 1032].rearrange("p (f t) -> p f t", t=129)[:, :, 1:129],
                ps_b[:].rearrange("p (f t) -> p f t", t=128)))
            evac["ps_b"] = DV.n
            if qt == 0:
                DV.inc(V.memset(x_fm_lo[:, base:base + 1032]
                                .rearrange("p (f t) -> p f t", t=129)[:, :, 0:1], 0.0))
            else:
                DV.inc(V.tensor_copy(
                    x_fm_lo[:, base:base + 1032].rearrange("p (f t) -> p f t", t=129)[:, :, 0:1],
                    x_fm_lo[:, pbase:pbase + 1032].rearrange("p (f t) -> p f t", t=129)[:, :, 128:129]))
        xfm_dv = DV.n

        TE.wait_ge(DV.sem, xfm_dv)
        if ("big",) in evac:
            TE.wait_ge(DV.sem, evac[("big",)])
        if ("exp_last",) in evac:
            TE.wait_ge(AC.sem, evac[("exp_last",)])
        import os
        hl = int(os.environ.get("HILO", "3"))
        qkv_terms = [(x_fm, wqkv_bf)]
        if qt < 2:
            if hl & 1:
                qkv_terms += [(x_fm, wqkv_lo)]
            if hl & 2:
                qkv_terms += [(x_fm_lo, wqkv_bf)]
        nterm = len(qkv_terms)
        for ti, (xsrc, wsrc) in enumerate(qkv_terms):
            for kt in range(8):
                if kt < 4:
                    lhsT = xsrc[:, base + kt * 129 + 1: base + kt * 129 + 129]
                else:
                    lhsT = xsrc[:, base + kt * 129 + 0: base + kt * 129 + 128]
                for nh in range(3):
                    mm = TE.matmul(ps_big[:, nh * 512:(nh + 1) * 512],
                                   lhsT,
                                   wsrc[:, kt * 1536 + nh * 512: kt * 1536 + (nh + 1) * 512],
                                   start=(ti == 0 and kt == 0),
                                   stop=(ti == nterm - 1 and kt == 7))
                    if ti == nterm - 1 and kt == 7 and nh == 2:
                        PE.inc(mm)
        rel[("xfm", slot)] = PE.n
        qkv_pe = PE.n

        SC.wait_ge(PE.sem, qkv_pe)
        AC.inc(SC.activation(out=rta[:], in_=ps_big[:, 0:512], func=AF.Square))
        AC.inc(SC.activation(out=rtb[:], in_=ps_big[:, 512:1024], func=AF.Square))
        sq_ac = AC.n
        V.wait_ge(AC.sem, sq_ac)
        DV.inc(V.reduce_sum(out=stt[:, 0:8],
                            in_=rta[:].rearrange("p (h d) -> p h d", d=64), axis=AX.X))
        DV.inc(V.reduce_sum(out=stt[:, 8:16],
                            in_=rtb[:].rearrange("p (h d) -> p h d", d=64), axis=AX.X))
        V.wait_ge(DV.sem, DV.n)
        SC.wait_ge(DV.sem, DV.n)
        AC.inc(SC.activation(out=rta[:, 48:64], in_=stt[:, 0:16], func=AF.Ln,
                             bias=LNEPS))
        SC.wait_ge(AC.sem, AC.n)
        AC.inc(SC.activation(out=rta[:, 16:32], in_=rta[:, 48:64], func=AF.Exp,
                             scale=-0.5))
        V.wait_ge(AC.sem, AC.n)
        DV.inc(V.tensor_mul(rta[:, 32:48], rta[:, 16:32], rta[:, 16:32]))
        V.wait_ge(DV.sem, DV.n)
        DV.inc(V.tensor_mul(rta[:, 32:48], rta[:, 32:48], stt[:, 0:16]))
        V.wait_ge(DV.sem, DV.n)
        DV.inc(V.tensor_scalar_mul(rta[:, 32:48], rta[:, 32:48], -0.5))
        V.wait_ge(DV.sem, DV.n)
        DV.inc(V.tensor_scalar_add(rta[:, 32:48], rta[:, 32:48], 1.5))
        V.wait_ge(DV.sem, DV.n)
        DV.inc(V.tensor_mul(stt[:, 0:16], rta[:, 16:32], rta[:, 32:48]))
        V.wait_ge(DV.sem, DV.n)
        DV.inc(V.tensor_tensor(
            out=q1b[:].rearrange("p (h d) -> p h d", d=64),
            in0=ps_big[:, 0:512].rearrange("p (h d) -> p h d", d=64),
            in1=stt[:, 0:8][:, :, None].broadcast_to([128, 8, 64]), op=ALU.mult))
        DV.inc(V.tensor_tensor(
            out=k1b[:].rearrange("p (h d) -> p h d", d=64),
            in0=ps_big[:, 512:1024].rearrange("p (h d) -> p h d", d=64),
            in1=stt[:, 8:16][:, :, None].broadcast_to([128, 8, 64]), op=ALU.mult))
        vslot = qt % 5
        if prev_heads_pe:
            V.wait_ge(PE.sem, prev_heads_pe)
        DV.inc(V.tensor_copy(v_roll[:, vslot * 512:(vslot + 1) * 512],
                             ps_big[:, 1024:1536]))
        V.wait_ge(DV.sem, DV.n)
        ttile_q1 = tq1[:, qt * 64:(qt + 1) * 64][:, None, :].broadcast_to([128, 8, 64])
        ttile_q2 = tq2[:, qt * 64:(qt + 1) * 64][:, None, :].broadcast_to([128, 8, 64])
        ttile_k1 = tk1[:, qt * 64:(qt + 1) * 64][:, None, :].broadcast_to([128, 8, 64])
        ttile_k2 = tk2[:, qt * 64:(qt + 1) * 64][:, None, :].broadcast_to([128, 8, 64])
        DV.inc(V.tensor_tensor(out=rta[:].rearrange("p (h d) -> p h d", d=64),
                               in0=q1b[:].rearrange("p (h d) -> p h d", d=64),
                               in1=ttile_q1, op=ALU.mult))
        DV.inc(V.tensor_tensor(
            out=rtb[:].rearrange("p (h d) -> p h d", d=64),
            in0=q1b[:].rearrange("p (h two s) -> p h two s", two=2, s=32)[:, :, ::-1, :]
                .rearrange("p h two s -> p h (two s)"),
            in1=ttile_q2, op=ALU.mult)) if False else DV.inc(V.tensor_tensor(
            out=rtb[:].rearrange("p (h two s) -> p h two s", two=2, s=32),
            in0=q1b[:].rearrange("p (h two s) -> p h two s", two=2, s=32)[:, :, ::-1, :],
            in1=ttile_q2.rearrange("p h (two s) -> p h two s", two=2, s=32), op=ALU.mult))
        V.wait_ge(DV.sem, DV.n)
        DV.inc(V.tensor_add(q2b[:], rta[:], rtb[:]))
        if qt < 2:
            DV.inc(V.tensor_add(q2f[:], rta[:], rtb[:]))
        DV.inc(V.tensor_tensor(out=rta[:].rearrange("p (h d) -> p h d", d=64),
                               in0=k1b[:].rearrange("p (h d) -> p h d", d=64),
                               in1=ttile_k1, op=ALU.mult))
        DV.inc(V.tensor_tensor(
            out=rtb[:].rearrange("p (h two s) -> p h two s", two=2, s=32),
            in0=k1b[:].rearrange("p (h two s) -> p h two s", two=2, s=32)[:, :, ::-1, :],
            in1=ttile_k2.rearrange("p h (two s) -> p h two s", two=2, s=32), op=ALU.mult))
        V.wait_ge(DV.sem, DV.n)
        DV.inc(V.tensor_add(k2b[:], rta[:], rtb[:]))
        if qt < 2:
            DV.inc(V.tensor_add(k2f[:], rta[:], rtb[:]))
            DV.inc(V.tensor_copy(vf_roll[:, vslot * 512:(vslot + 1) * 512],
                                 ps_big[:, 1024:1536]))
        qk_dv = DV.n

        if qt == 0 and getattr(E, "a_dbg", False):
            V.wait_ge(DV.sem, qk_dv)
            V.wait_ge(ST.sem, ST.n)
            DV.inc(V.tensor_copy(x_out[:, 0:256], sc_mat[:]))
            DV.inc(V.tensor_copy(x_out[0:1, 256:320], qs_sb[:]))
            DV.inc(V.tensor_copy(x_out[:, 320:384], tq1[:, 0:64]))
            GP.wait_ge(DV.sem, DV.n)
            ST.inc(GP.dma_start(out=E.out[15 * 128:16 * 128, 0:384], in_=x_out[:, 0:384]), 16)
            V.wait_ge(ST.sem, ST.n)
            for bi, buf in enumerate((q1b, k1b, q2b, k2b)):
                DV.inc(V.tensor_copy(x_out[:, 0:512], buf[:]))
                GP.wait_ge(DV.sem, DV.n)
                ST.inc(GP.dma_start(out=E.out[(16 + bi) * 128:(17 + bi) * 128, 0:512],
                                    in_=x_out[:, 0:512]), 16)
                V.wait_ge(ST.sem, ST.n)
        TE.wait_ge(DV.sem, qk_dv)
        TE.wait_ge(DV.sem, evac["ps_b"]) if "ps_b" in evac else None
        for ft in range(4):
            TE.transpose(ps_b[:, ft * 128:(ft + 1) * 128],
                         q2b[:, ft * 128:(ft + 1) * 128], E.ident_sb[:])
        for ft in range(4):
            mm = TE.transpose(ps_b[:, 512 + ft * 128: 512 + (ft + 1) * 128],
                              k2b[:, ft * 128:(ft + 1) * 128], E.ident_sb[:])
            if ft == 3:
                PE.inc(mm)
        qkT_pe = PE.n
        V.wait_ge(PE.sem, qkT_pe)
        if prev_heads_pe:
            V.wait_ge(PE.sem, prev_heads_pe)
        DV.inc(V.tensor_copy(q_fmc[:], ps_b[:, 0:512]))
        DV.inc(V.tensor_copy(kfm_roll[:, vslot * 512:(vslot + 1) * 512],
                             ps_b[:, 512:1024]))
        evac["ps_b"] = DV.n
        if qt < 2:
            if ("tf",) in evac:
                TE.wait_ge(DV.sem, evac[("tf",)])
            for ft in range(4):
                mm = TE.transpose(ps_tf[:, ft * 128:(ft + 1) * 128],
                                  q2f[:, ft * 128:(ft + 1) * 128], ident_f[:])
                if ft == 3:
                    PE.inc(mm)
            V.wait_ge(PE.sem, PE.n)
            DV.inc(V.tensor_copy(qfm_f[:], ps_tf[:]))
            TE.wait_ge(DV.sem, DV.n)
            for ft in range(4):
                mm = TE.transpose(ps_tf[:, ft * 128:(ft + 1) * 128],
                                  k2f[:, ft * 128:(ft + 1) * 128], ident_f[:])
                if ft == 3:
                    PE.inc(mm)
            V.wait_ge(PE.sem, PE.n)
            DV.inc(V.tensor_copy(kf_roll[:, qt * 512:(qt + 1) * 512], ps_tf[:]))
            evac[("tf",)] = DV.n
        qkT_dv = DV.n

        sim_off = [0, 1024]
        dv_after_stats = qkT_dv
        exp_acs = {}
        for h in range(HB if qt < 2 else 0):
            hb = (h % 2) * 64
            hf = (h // 2) * 128
            s = h % 2
            so = sim_off[s]
            fp32_tile = qt < 2
            TE.wait_ge(DV.sem, dv_after_stats)
            if h == 0:
                TE.wait_ge(AC.sem, sq_ac)
            if h >= 2:
                TE.wait_ge(AC.sem, exp_acs[h - 2])
            for ki, ktile in enumerate(kts):
                ks_ = ktile % 5
                if fp32_tile:
                    mm = TE.matmul(ps_pool[:, so + ki * 128: so + (ki + 1) * 128],
                                   qfm_f[hb:hb + 64, hf:hf + 128],
                                   kf_roll[hb:hb + 64, ktile * 512 + hf: ktile * 512 + hf + 128],
                                   start=True, stop=True)
                else:
                    mm = TE.matmul(ps_pool[:, so + ki * 128: so + (ki + 1) * 128],
                                   q_fmc[hb:hb + 64, hf:hf + 128],
                                   kfm_roll[hb:hb + 64, ks_ * 512 + hf: ks_ * 512 + hf + 128],
                                   start=True, stop=True)
                if ki == nkt - 1:
                    PE.inc(mm)
            sim_pe = PE.n
            V.wait_ge(PE.sem, sim_pe)
            if qt >= 4:
                DV.inc(V.tensor_tensor(
                    out=ps_pool[:, so:so + 640].rearrange("p (b j) -> p b j", j=128)[:, ::4, :],
                    in0=ps_pool[:, so:so + 640].rearrange("p (b j) -> p b j", j=128)[:, ::4, :],
                    in1=mcat[:].rearrange("p (b j) -> p b j", j=128),
                    op=ALU.add))
            else:
                DV.inc(V.tensor_add(ps_pool[:, so + (nkt - 1) * 128: so + nkt * 128],
                                    ps_pool[:, so + (nkt - 1) * 128: so + nkt * 128],
                                    E.mle_sb[:]))
            DV.inc(V.reduce_max(out=stt[:, 16 + s:17 + s], in_=ps_pool[:, so:so + w],
                                axis=AX.X, negate=True))
            mask_dv = DV.n
            SC.wait_ge(DV.sem, mask_dv)
            if ("attnf", s) in rel:
                SC.wait_ge(DV.sem, rel[("attnf", s)])
            AC.inc(SC.activation(out=attn_f[:, s * 640: s * 640 + w],
                                 in_=ps_pool[:, so:so + w], func=AF.Exp,
                                 bias=stt[:, 16 + s:17 + s],
                                 accum_out=stt[:, 18 + s:19 + s]))
            exp_acs[h] = AC.n
            exp_ac = AC.n
            V.wait_ge(AC.sem, exp_ac)
            DV.inc(V.reciprocal(stt[:, 20 + s:21 + s], stt[:, 18 + s:19 + s]))
            rec_dv = DV.n
            V.wait_ge(DV.sem, rec_dv)
            if ("tsp", s) in rel:
                V.wait_ge(PE.sem, rel[("tsp", s)])
            if fp32_tile and ("tsp", 1 - s) in rel:
                V.wait_ge(PE.sem, rel[("tsp", 1 - s)])
            DV.inc(V.tensor_scalar_mul(
                attn_sb[:, s * 640: s * 640 + w] if not fp32_tile
                else attn_nf[:, 0:w],
                attn_f[:, s * 640: s * 640 + w], stt[:, 20 + s:21 + s]))
            tsp_po = DV.n
            rel[("attnf", s)] = tsp_po
            TE.wait_ge(DV.sem, tsp_po)
            if fp32_tile:
                if ("tf",) in evac:
                    TE.wait_ge(DV.sem, evac[("tf",)])
                for ki in range(nkt):
                    mm = TE.transpose(ps_tf[:, ki * 128:(ki + 1) * 128],
                                      attn_nf[:, ki * 128:(ki + 1) * 128], ident_f[:])
                    if ki == nkt - 1:
                        PE.inc(mm)
                rel[("tsp", s)] = PE.n
                V.wait_ge(PE.sem, PE.n)
                DV.inc(V.tensor_copy(afm_f[:, 0:w], ps_tf[:, 0:w]))
                evac[("tf",)] = DV.n
                afm_dv = DV.n
            else:
                if ("ps_b2",) in evac:
                    TE.wait_ge(DV.sem, evac[("ps_b2",)])
                for ki in range(nkt):
                    mm = TE.transpose(ps_b[:, ki * 128:(ki + 1) * 128],
                                      attn_sb[:, s * 640 + ki * 128: s * 640 + (ki + 1) * 128],
                                      E.ident_sb[:])
                    if ki == nkt - 1:
                        PE.inc(mm)
                rel[("tsp", s)] = PE.n
                V.wait_ge(PE.sem, PE.n)
                DV.inc(V.tensor_copy(attn_fm[:, s * 640: s * 640 + w], ps_b[:, 0:w]))
                evac[("ps_b2",)] = DV.n
                evac["ps_b"] = DV.n
                afm_dv = DV.n
            TE.wait_ge(DV.sem, afm_dv)
            if h == 0 and ("oev",) in evac:
                TE.wait_ge(DV.sem, evac[("oev",)])
            for ki, ktile in enumerate(kts):
                ks_ = ktile % 5
                if fp32_tile:
                    mm = TE.matmul(ps_o[0:64, h * 128:(h + 1) * 128],
                                   vf_roll[:, ktile * 512 + h * 64: ktile * 512 + h * 64 + 64],
                                   afm_f[:, ki * 128:(ki + 1) * 128],
                                   start=(ki == 0 and h % 4 == 0), stop=(ki == nkt - 1))
                else:
                    mm = TE.matmul(ps_o[0:64, h * 128:(h + 1) * 128],
                                   v_roll[:, ks_ * 512 + h * 64: ks_ * 512 + h * 64 + 64],
                                   attn_fm[:, s * 640 + ki * 128: s * 640 + (ki + 1) * 128],
                                   start=(ki == 0 and h % 4 == 0), stop=(ki == nkt - 1))
                if ki == nkt - 1:
                    PE.inc(mm)
        if qt >= 2:
            sim_pe = {}
            av_pe = {}
            for g in range(HB + 1):
                if g < HB:
                    h = g
                    hb = (h % 2) * 64
                    hf = (h // 2) * 128
                    s = h % 2
                    so = sim_off[s]
                    if g < 2:
                        TE.wait_ge(DV.sem, dv_after_stats)
                    else:
                        TE.wait_ge(AC.sem, exp_acs[g - 2])
                    for ki, ktile in enumerate(kts):
                        ks_ = ktile % 5
                        mm = TE.matmul(
                            ps_pool[:, so + ki * 128: so + (ki + 1) * 128],
                            q_fmc[hb:hb + 64, hf:hf + 128],
                            kfm_roll[hb:hb + 64, ks_ * 512 + hf: ks_ * 512 + hf + 128],
                            start=True, stop=True)
                        if ki == nkt - 1:
                            PE.inc(mm)
                    sim_pe[g] = PE.n
                if g >= 1:
                    j = g - 1
                    sj = j % 2
                    soj = sim_off[sj]
                    V.wait_ge(AC.sem, exp_acs[j])
                    DV.inc(V.reciprocal(stt[:, 20 + sj:21 + sj],
                                        stt[:, 18 + sj:19 + sj]))
                    V.wait_ge(DV.sem, DV.n)
                    if ("tsp", sj) in rel:
                        V.wait_ge(PE.sem, rel[("tsp", sj)])
                    DV.inc(V.tensor_scalar_mul(
                        attn_sb[:, sj * 640: sj * 640 + w],
                        attn_f[:, sj * 640: sj * 640 + w],
                        stt[:, 20 + sj:21 + sj]))
                    rel[("attnf", sj)] = DV.n
                    tsp_dv = DV.n
                if g < HB:
                    V.wait_ge(PE.sem, sim_pe[g])
                    if qt >= 4:
                        DV.inc(V.tensor_tensor(
                            out=ps_pool[:, so:so + 640]
                                .rearrange("p (b j) -> p b j", j=128)[:, ::4, :],
                            in0=ps_pool[:, so:so + 640]
                                .rearrange("p (b j) -> p b j", j=128)[:, ::4, :],
                            in1=mcat[:].rearrange("p (b j) -> p b j", j=128),
                            op=ALU.add))
                    else:
                        DV.inc(V.tensor_add(
                            ps_pool[:, so + (nkt - 1) * 128: so + nkt * 128],
                            ps_pool[:, so + (nkt - 1) * 128: so + nkt * 128],
                            E.mle_sb[:]))
                    DV.inc(V.reduce_max(out=stt[:, 16 + s:17 + s],
                                        in_=ps_pool[:, so:so + w],
                                        axis=AX.X, negate=True))
                    mask_dv = DV.n
                    SC.wait_ge(DV.sem, mask_dv)
                    if ("attnf", s) in rel:
                        SC.wait_ge(DV.sem, rel[("attnf", s)])
                    AC.inc(SC.activation(out=attn_f[:, s * 640: s * 640 + w],
                                         in_=ps_pool[:, so:so + w], func=AF.Exp,
                                         bias=stt[:, 16 + s:17 + s],
                                         accum_out=stt[:, 18 + s:19 + s]))
                    exp_acs[g] = AC.n
                if g >= 1:
                    j = g - 1
                    sj = j % 2
                    TE.wait_ge(DV.sem, tsp_dv)
                    if ("ps_b2",) in evac:
                        TE.wait_ge(DV.sem, evac[("ps_b2",)])
                    for ki in range(nkt):
                        mm = TE.transpose(
                            ps_b[:, ki * 128:(ki + 1) * 128],
                            attn_sb[:, sj * 640 + ki * 128: sj * 640 + (ki + 1) * 128],
                            E.ident_sb[:])
                        if ki == nkt - 1:
                            PE.inc(mm)
                    rel[("tsp", sj)] = PE.n
                    V.wait_ge(PE.sem, PE.n)
                    if ("afm", sj) in rel:
                        V.wait_ge(PE.sem, rel[("afm", sj)])
                    DV.inc(V.tensor_copy(attn_fm[:, sj * 640: sj * 640 + w],
                                         ps_b[:, 0:w]))
                    evac[("ps_b2",)] = DV.n
                    evac["ps_b"] = DV.n
                    afm_dv = DV.n
                    TE.wait_ge(DV.sem, afm_dv)
                    if j == 0 and ("oev",) in evac:
                        TE.wait_ge(DV.sem, evac[("oev",)])
                    for ki, ktile in enumerate(kts):
                        ks_ = ktile % 5
                        mm = TE.matmul(
                            ps_o[0:64, j * 128:(j + 1) * 128],
                            v_roll[:, ks_ * 512 + j * 64: ks_ * 512 + j * 64 + 64],
                            attn_fm[:, sj * 640 + ki * 128: sj * 640 + (ki + 1) * 128],
                            start=(ki == 0 and j % 4 == 0), stop=(ki == nkt - 1))
                        if ki == nkt - 1:
                            PE.inc(mm)
                    av_pe[j] = PE.n
                    rel[("afm", sj)] = PE.n
        V.wait_ge(PE.sem, PE.n)
        if ("ofm",) in rel:
            V.wait_ge(PE.sem, rel[("ofm",)])
        DV.inc(V.tensor_copy(
            o_fm[0:64, :].rearrange("p (g d) -> p g d", d=128),
            ps_o[0:64, :].rearrange("p (h two d) -> p h two d",
                                    two=2, d=128)[:, :, 0, :]))
        DV.inc(V.tensor_copy(
            o_fm[64:128, :].rearrange("p (g d) -> p g d", d=128),
            ps_o[0:64, :].rearrange("p (h two d) -> p h two d",
                                    two=2, d=128)[:, :, 1, :]))
        evac[("oev",)] = DV.n
        o_dv = DV.n

        prev_heads_pe = PE.n
        TE.wait_ge(DV.sem, o_dv)
        TE.wait_ge(AC.sem, exp_acs[HB - 1])
        for ot in range(4):
            for nh in range(2):
                mm = TE.matmul(ps_big[:, nh * 512:(nh + 1) * 512],
                               o_fm[:, ot * 128:(ot + 1) * 128],
                               wo_bf[:, ot * 1024 + nh * 512: ot * 1024 + (nh + 1) * 512],
                               start=(ot == 0), stop=(ot == 3))
                if ot == 3 and nh == 1:
                    PE.inc(mm)
        wo_pe = PE.n
        rel[("ofm",)] = wo_pe
        V.wait_ge(PE.sem, wo_pe)
        if ("xout", slot) in rel:
            V.wait_ge(ST.sem, rel[("xout", slot)])
        DV.inc(V.tensor_add(x_out[:, slot * D:slot * D + D],
                            x_tm[:, slot * D:slot * D + D], ps_big[:, 0:1024]))
        evac[("big",)] = DV.n
        rel[("xtm", slot)] = DV.n
        evac[("exp_last",)] = exp_acs[HB - 1]
        xo_dv = DV.n
        GP.wait_ge(DV.sem, xo_dv)
        ST.inc(GP.dma_start(out=E.xA[qt * 128:(qt + 1) * 128, :],
                            in_=x_out[:, slot * D:slot * D + D]), 16)
        rel[("xout", slot)] = ST.n
    pa.close()


def _emit_B(E, li):
    nc = E.nc
    V, SC, TE, SY, GP = E.V, E.SC, E.TE, E.SY, E.GP
    LD, ST, PE, DV, AC, PO = E.LD, E.ST, E.PE, E.DV, E.AC, E.PO
    H2 = 2 * HID

    pb = ExitStack()

    def sb(name, shape, dtype):
        return pb.enter_context(nc.sbuf_tensor("b%d_%s" % (li, name), shape, dtype))

    def ps(name, shape, dtype=F32):
        return pb.enter_context(nc.psum_tensor("b%d_%s" % (li, name), shape, dtype))

    _pass_barrier(E)

    w1g = sb("w1g", [128, 8 * H2], BF16)
    w2b = sb("w2b", [128, 22 * D], BF16)
    wstage = sb("wstage", [128, 2 * 1024], F32)
    g_cols = sb("gcols", [128, 8], F32)
    b_cols = sb("bcols", [128, 8], F32)
    b_cols_bf = sb("bcolsbf", [128, 8], BF16)
    bias_a = sb("biasa", [128, 22], F32)
    bias_g = sb("biasg", [128, 22], F32)
    b1_cols = sb("b1cols", [128, 44], F32)
    ones_f = sb("ones_f", [1, 128], F32)
    x_tm = sb("x_tm", [128, 2 * D], F32)
    x_bf = sb("x_bf", [128, D], BF16)
    x_fm = sb("x_fm", [128, 8 * 513], BF16)
    prevcol = sb("prevcol", [128, 8], BF16)
    h_ln = sb("h_ln", [128, 8 * 512], BF16)
    sqscr = sb("sqscr", [128, 512], BF16)
    rowbuf = sb("rowbuf", [1, 3 * 512], F32)
    mu_b = sb("mu_b", [128, 512], F32)
    r_b = sb("r_b", [128, 512], F32)
    t1buf = sb("t1", [128, 512], F32)
    gelu_sb = sb("gelu", [128, 2 * 512], BF16)
    hc = sb("hc", [128, 22 * 512], BF16)
    x_out = sb("x_out", [128, D], F32)

    ps_za = [ps("za0", [128, 512], F32), ps("za1", [128, 512], F32)]
    ps_zg = [ps("zg0", [128, 512], F32), ps("zg1", [128, 512], F32)]
    ps_y = ps("y", [128, 1024], F32)
    ps_b = ps("trp", [128, 1024], BF16)
    ps_row = ps_y[0:1, 0:1024]
    ps_bc = ps_za[0]

    fold6 = (li == 1) and getattr(E, "fold_p6", False)
    if fold6:
        x1_in = wstage[:, 0:D]
        o6 = wstage[:, D:2 * D]
        sq6 = gelu_sb
        stt6 = sb("stt6", [128, 8], F32)
    prev_sq6 = 0
    prev_o6_ac = 0
    x1_ld = 0

    DV.inc(V.memset(ones_f[:], 1.0))
    V.memset(hc[32:64, 21 * 512:22 * 512], 0.0)
    V.memset(hc[64:128, 21 * 512:22 * 512], 0.0)
    DV.inc(V.memset(hc[64:65, 21 * 512:22 * 512], 1.0))
    hcinit_dv = DV.n

    for kt in range(8):
        LD.inc(SY.dma_start(out=g_cols[:, kt:kt + 1],
                            in_=E.f_g[li, kt * 128:(kt + 1) * 128][:, None]), 16)
        LD.inc(SY.dma_start(out=b_cols[:, kt:kt + 1],
                            in_=E.f_b[li, kt * 128:(kt + 1) * 128][:, None]), 16)
    for jc in range(22):
        wa = 128 if jc < 21 else HID - 21 * 128
        LD.inc(SY.dma_start(out=b1_cols[0:wa, jc:jc + 1],
                            in_=E.f_b1[li, jc * 128: jc * 128 + wa][:, None]), 16)
        LD.inc(SY.dma_start(out=b1_cols[0:wa, 22 + jc:23 + jc],
                            in_=E.f_b1[li, HID + jc * 128: HID + jc * 128 + wa][:, None]), 16)
    gcol_ld = LD.n

    base_po = PO.n
    chunks1 = [(q * 1024, min(1024, H2 - q * 1024)) for q in range(6)]
    i = 0
    for kt in range(8):
        for (j0, wd) in chunks1:
            slot = i % 2
            if i >= 2:
                SY.wait_ge(PO.sem, base_po + i - 1)
            elif i < 2:
                SY.wait_ge(PO.sem, base_po)
            LD.inc(SY.dma_start(out=wstage[:, slot * 1024: slot * 1024 + wd],
                                in_=E.f_w1[li, kt * 128:(kt + 1) * 128, j0:j0 + wd]), 16)
            GP.wait_ge(LD.sem, LD.n)
            if i == 0:
                GP.wait_ge(LD.sem, gcol_ld)
            PO.inc(GP.tensor_copy(w1g[:, kt * H2 + j0: kt * H2 + j0 + wd],
                                   wstage[:, slot * 1024: slot * 1024 + wd]))
            i += 1
    base_po2 = PO.n
    for ct in range(22):
        slot = ct % 2
        wr = 128 if ct < 21 else HID - 21 * 128
        if ct >= 2:
            SY.wait_ge(PO.sem, base_po2 + ct - 1)
        else:
            SY.wait_ge(PO.sem, base_po2)
        if ct == 21:
            V.wait_ge(PO.sem, PO.n)
            V.memset(wstage[32:64, slot * 1024: slot * 1024 + D], 0.0)
            DV.inc(V.memset(wstage[64:128, slot * 1024: slot * 1024 + D], 0.0))
            SY.wait_ge(DV.sem, DV.n)
            LD.inc(SY.dma_start(out=wstage[64:65, slot * 1024: slot * 1024 + D],
                                in_=E.f_b2[li][None, :]), 16)
        LD.inc(SY.dma_start(out=wstage[0:wr, slot * 1024: slot * 1024 + D],
                            in_=E.f_w2[li, ct * 128: ct * 128 + wr, :]), 16)
        GP.wait_ge(LD.sem, LD.n)
        PO.inc(GP.tensor_copy(w2b[:, ct * D:(ct + 1) * D],
                              wstage[:, slot * 1024: slot * 1024 + D]))
    w_ready = PO.n

    V.wait_ge(LD.sem, gcol_ld)
    DV.inc(V.tensor_copy(bias_a[:, 0:22], b1_cols[:, 0:22]))
    DV.inc(V.tensor_copy(bias_g[:, 0:22], b1_cols[:, 22:44]))
    bias_done = DV.n

    evac = {}
    rel = {}

    for c in range(NCH):
        slot = 0
        base = 0
        for tt in range(CT):
            i = c * CT + tt
            if ("xtm", i % 2) in rel:
                SY.wait_ge(DV.sem, rel[("xtm", i % 2)])
            LD.inc(SY.dma_start(out=x_tm[:, (i % 2) * D:(i % 2 + 1) * D],
                                in_=E.xA[i * 128:(i + 1) * 128, :]), 16)
            x_ld = LD.n
            V.wait_ge(LD.sem, x_ld)
            if ("xbf",) in rel:
                V.wait_ge(PE.sem, rel[("xbf",)])
            DV.inc(V.tensor_copy(x_bf[:], x_tm[:, (i % 2) * D:(i % 2 + 1) * D]))
            rel[("xtm", i % 2)] = DV.n
            xbf_dv = DV.n
            TE.wait_ge(DV.sem, xbf_dv)
            if c == 0 and tt == 0:
                TE.wait_ge(PO.sem, w_ready)
                TE.wait_ge(DV.sem, bias_done)
            if "ps_b" in evac:
                TE.wait_ge(DV.sem, evac["ps_b"])
            for ft in range(8):
                mm = TE.transpose(ps_b[:, ft * 128:(ft + 1) * 128],
                                  x_bf[:, ft * 128:(ft + 1) * 128], E.ident_sb[:])
                if ft == 7:
                    PE.inc(mm)
            rel[("xbf",)] = PE.n
            V.wait_ge(PE.sem, PE.n)
            if tt == 0 and ("xfm",) in rel:
                V.wait_ge(PE.sem, rel[("xfm",)])
                V.wait_ge(AC.sem, rel[("xfm_ac",)])
            DV.inc(V.tensor_copy(
                x_fm[:].rearrange("p (f t) -> p f t", t=513)
                [:, :, 1 + tt * 128: 1 + (tt + 1) * 128],
                ps_b[:].rearrange("p (f t) -> p f t", t=128)))
            evac["ps_b"] = DV.n
        if c == 0:
            DV.inc(V.memset(
                x_fm[:].rearrange("p (f t) -> p f t", t=513)[:, :, 0:1], 0.0))
        else:
            DV.inc(V.tensor_copy(
                x_fm[:].rearrange("p (f t) -> p f t", t=513)[:, :, 0:1],
                prevcol[:, :, None]))
        xfm_dv = DV.n

        def hview(kt):
            if kt < 4:
                return x_fm[:, kt * 513 + 1: kt * 513 + 513]
            return x_fm[:, kt * 513 + 0: kt * 513 + 512]

        TE.wait_ge(DV.sem, xfm_dv)
        if ("row",) in evac:
            TE.wait_ge(DV.sem, evac[("row",)])
        if ("y",) in evac:
            TE.wait_ge(DV.sem, evac[("y",)])
        for kt in range(8):
            mm = TE.matmul(ps_row[:, 0:512], E.ones_sb[:, 0:1], hview(kt),
                           start=(kt == 0), stop=(kt == 7))
            if kt == 7:
                PE.inc(mm)
        sum_pe = PE.n
        for kt in range(8):
            SC.wait_ge(DV.sem, xfm_dv)
            if kt >= 1:
                SC.wait_ge(PE.sem, PE.n)
            AC.inc(SC.activation(out=sqscr[:], in_=hview(kt), func=AF.Square))
            TE.wait_ge(AC.sem, AC.n)
            PE.inc(TE.matmul(ps_row[:, 512:1024], E.ones_sb[:, 0:1],
                             sqscr[:],
                             start=(kt == 0), stop=(kt == 7)))
        sq_pe = PE.n
        sq_ac_last = AC.n
        V.wait_ge(PE.sem, sq_pe)
        DV.inc(V.tensor_scalar_mul(rowbuf[0:1, 0:512], ps_row[:, 0:512], 1.0 / D))
        V.wait_ge(DV.sem, DV.n)
        DV.inc(V.tensor_scalar_mul(rowbuf[0:1, 512:1024], ps_row[:, 512:1024], 1.0 / D))
        evac[("row",)] = DV.n
        V.wait_ge(DV.sem, DV.n)
        DV.inc(V.tensor_mul(rowbuf[0:1, 1024:1536], rowbuf[0:1, 0:512], rowbuf[0:1, 0:512]))
        V.wait_ge(DV.sem, DV.n)
        DV.inc(V.tensor_sub(rowbuf[0:1, 512:1024], rowbuf[0:1, 512:1024],
                            rowbuf[0:1, 1024:1536]))
        V.wait_ge(DV.sem, DV.n)
        DV.inc(V.tensor_scalar_add(rowbuf[0:1, 512:1024], rowbuf[0:1, 512:1024], EPS))
        SC.wait_ge(DV.sem, DV.n)
        AC.inc(SC.activation(out=rowbuf[0:1, 1024:1536], in_=rowbuf[0:1, 512:1024],
                             func=AF.Sqrt))
        V.wait_ge(AC.sem, AC.n)
        DV.inc(V.reciprocal(rowbuf[0:1, 512:1024], rowbuf[0:1, 1024:1536]))
        V.wait_ge(DV.sem, DV.n)
        row_dv = DV.n
        TE.wait_ge(DV.sem, row_dv)
        if ("bc",) in evac:
            TE.wait_ge(DV.sem, evac[("bc",)])
        if ("za", 0) in evac:
            TE.wait_ge(DV.sem, evac[("za", 0)])
        PE.inc(TE.matmul(ps_bc[:, :], ones_f[0:1, :], rowbuf[0:1, 0:512],
                         start=True, stop=True))
        V.wait_ge(PE.sem, PE.n)
        DV.inc(V.tensor_copy(mu_b[:], ps_bc[:, :]))
        TE.wait_ge(DV.sem, DV.n)
        PE.inc(TE.matmul(ps_bc[:, :], ones_f[0:1, :], rowbuf[0:1, 512:1024],
                         start=True, stop=True))
        V.wait_ge(PE.sem, PE.n)
        DV.inc(V.tensor_copy(r_b[:], ps_bc[:, :]))
        evac[("bc",)] = DV.n
        V.wait_ge(DV.sem, DV.n)
        if ("hln",) in rel:
            V.wait_ge(PE.sem, rel[("hln",)])
        for kt in range(8):
            DV.inc(V.tensor_sub(t1buf[:], hview(kt), mu_b[:]))
            V.wait_ge(DV.sem, DV.n)
            DV.inc(V.tensor_mul(t1buf[:], t1buf[:], r_b[:]))
            V.wait_ge(DV.sem, DV.n)
            DV.inc(V.tensor_scalar(out=h_ln[:, kt * 512:(kt + 1) * 512],
                                   in0=t1buf[:],
                                   scalar1=g_cols[:, kt:kt + 1],
                                   scalar2=b_cols[:, kt:kt + 1],
                                   op0=ALU.mult, op1=ALU.add))
        DV.inc(V.tensor_copy(prevcol[:, :, None],
                             x_fm[:].rearrange("p (f t) -> p f t", t=513)[:, :, 512:513]))
        hln_dv = DV.n

        for jc in range(22):
            wa = 128 if jc < 21 else HID - 21 * 128
            zs = jc % 2
            TE.wait_ge(DV.sem, hln_dv)
            if ("za", zs) in evac:
                TE.wait_ge(DV.sem, evac[("za", zs)])
            if ("zg", zs) in evac:
                TE.wait_ge(AC.sem, evac[("zg", zs)])
            if zs == 0 and ("bc",) in evac:
                TE.wait_ge(DV.sem, evac[("bc",)])
            for kt in range(8):
                TE.matmul(ps_za[zs][0:wa, :],
                          w1g[:, kt * H2 + jc * 128: kt * H2 + jc * 128 + wa],
                          h_ln[:, kt * 512:(kt + 1) * 512],
                          start=(kt == 0), stop=(kt == 7))
                mm = TE.matmul(ps_zg[zs][0:wa, :],
                               w1g[:, kt * H2 + HID + jc * 128: kt * H2 + HID + jc * 128 + wa],
                               h_ln[:, kt * 512:(kt + 1) * 512],
                               start=(kt == 0), stop=(kt == 7))
                if kt == 7:
                    PE.inc(mm)
            z_pe = PE.n
            SC.wait_ge(PE.sem, z_pe)
            if jc >= 2:
                SC.wait_ge(DV.sem, DV.n)
            AC.inc(SC.activation(out=gelu_sb[:, zs * 512:(zs + 1) * 512][0:wa, :],
                                 in_=ps_zg[zs][0:wa, :], func=AF.Gelu,
                                 bias=bias_g[0:wa, jc:jc + 1]))
            evac[("zg", zs)] = AC.n
            V.wait_ge(AC.sem, AC.n)
            if ("hc", jc) in rel:
                V.wait_ge(PE.sem, rel[("hc", jc)])
            DV.inc(V.scalar_tensor_tensor(
                out=hc[0:wa, jc * 512:(jc + 1) * 512],
                in0=ps_za[zs][0:wa, :], scalar=bias_a[0:wa, jc:jc + 1],
                in1=gelu_sb[:, zs * 512:(zs + 1) * 512][0:wa, :],
                op0=ALU.add, op1=ALU.mult))
            evac[("za", zs)] = DV.n
        hc_dv = DV.n

        for tt in range(CT):
            i = c * CT + tt
            if ("xtm", i % 2) in rel:
                SY.wait_ge(DV.sem, rel[("xtm", i % 2)])
            LD.inc(SY.dma_start(out=x_tm[:, (i % 2) * D:(i % 2 + 1) * D],
                                in_=E.xA[i * 128:(i + 1) * 128, :]), 16)
            xr_ld = LD.n
            if fold6:
                if ("x1in",) in rel:
                    SY.wait_ge(DV.sem, rel[("x1in",)])
                if c == 0 and tt == 0:
                    SY.wait_ge(PO.sem, w_ready)
                LD.inc(SY.dma_start(out=x1_in[:],
                                    in_=E.x1k[i * 128:(i + 1) * 128, :]), 16)
                x1_ld = LD.n
            TE.wait_ge(DV.sem, hc_dv)
            if ("y",) in evac:
                TE.wait_ge(DV.sem, evac[("y",)])
            for jc in range(22):
                kr = 128 if jc < 21 else 65
                for nh in range(2):
                    mm = TE.matmul(ps_y[:, nh * 512:(nh + 1) * 512],
                                   hc[0:kr, jc * 512 + tt * 128: jc * 512 + (tt + 1) * 128],
                                   w2b[0:kr, jc * D + nh * 512: jc * D + (nh + 1) * 512],
                                   start=(jc == 0), stop=(jc == 21))
                    if jc == 21 and nh == 1:
                        PE.inc(mm)
            y_pe = PE.n
            if tt == CT - 1:
                rel[("hc", 0)] = PE.n
                for jc in range(22):
                    rel[("hc", jc)] = PE.n
            V.wait_ge(PE.sem, y_pe)
            V.wait_ge(LD.sem, xr_ld)
            if ("xout",) in rel:
                V.wait_ge(ST.sem, rel[("xout",)])
            if fold6 and prev_o6_ac:
                V.wait_ge(AC.sem, prev_o6_ac)
            DV.inc(V.tensor_add(x_out[:], x_tm[:, (i % 2) * D:(i % 2 + 1) * D], ps_y[:]))
            rel[("xtm", i % 2)] = DV.n
            evac[("y",)] = DV.n
            if not fold6:
                GP.wait_ge(DV.sem, DV.n)
                ST.inc(GP.dma_start(out=E.xA[i * 128:(i + 1) * 128, :], in_=x_out[:]), 16)
                rel[("xout",)] = ST.n
            else:
                xo_dv = DV.n
                SC.wait_ge(DV.sem, xo_dv)
                _ln_stats(E, x_out[:], stt6, sq6, D,
                          prev_guard=(DV.sem, prev_sq6) if prev_sq6 else None)
                prev_sq6 = DV.n
                SC.wait_ge(DV.sem, DV.n)
                if ("o6",) in rel:
                    SC.wait_ge(ST.sem, rel[("o6",)])
                AC.inc(SC.activation(out=o6[:], in_=x_out[:], func=AF.Identity,
                                     scale=stt6[:, 3:4], bias=stt6[:, 4:5]))
                prev_o6_ac = AC.n
                V.wait_ge(AC.sem, prev_o6_ac)
                V.wait_ge(LD.sem, x1_ld)
                DV.inc(V.tensor_add(o6[:], o6[:], x1_in[:]))
                rel[("x1in",)] = DV.n
                GP.wait_ge(DV.sem, DV.n)
                ST.inc(GP.dma_start(out=E.out[i * 128:(i + 1) * 128, :],
                                    in_=o6[:]), 16)
                rel[("o6",)] = ST.n
        rel[("hln",)] = PE.n
        rel[("xfm",)] = sq_pe
        rel[("xfm_ac",)] = sq_ac_last

    if fold6:
        GP.wait_ge(ST.sem, ST.n)

    dbg = getattr(E, "b_dbg", None)
    if dbg:
        tmp = x_tm
        row = 0
        for name in dbg:
            buf = {"hln": h_ln, "rowbuf": rowbuf, "mub": mu_b, "rb": r_b,
                   "hc": hc, "xfm": x_fm, "hc2": hc, "hc3": hc,
                   "w2b": w2b, "w2b2": w2b}[name]
            ncols = {"hln": 4096, "rowbuf": 1536, "mub": 512, "rb": 512,
                     "hc": 4096, "xfm": 4104, "hc2": 4096, "hc3": 4096,
                     "w2b": 4096, "w2b2": 4096}[name]
            off = {"hc2": 4096, "hc3": 8192, "w2b2": 18432}.get(name, 0)
            for blk in range(0, min(ncols, 4096), 1024):
                w_ = min(1024, ncols - blk)
                V.wait_ge(DV.sem, DV.n)
                V.wait_ge(ST.sem, ST.n)
                V.wait_ge(PO.sem, PO.n)
                V.wait_ge(AC.sem, AC.n)
                V.memset(tmp[:, 0:1024], 0.0)
                blk2 = blk + off
                w_ = min(1024, buf.shape[1] - blk2) if name in ("hc3",) else w_
                if w_ <= 0:
                    row += 1
                    continue
                DV.inc(V.tensor_copy(tmp[0:1, 0:w_] if name == "rowbuf" else tmp[:, 0:w_],
                                     buf[0:1, blk2:blk2 + w_] if name == "rowbuf" else buf[:, blk2:blk2 + w_]))
                GP.wait_ge(DV.sem, DV.n)
                ST.inc(GP.dma_start(out=E.out[row * 128:(row + 1) * 128, :], in_=tmp[:, 0:1024]), 16)
                GP.wait_ge(ST.sem, ST.n)
                row += 1
    pb.close()


def _emit_P6(E):
    nc = E.nc
    V, SC, TE, SY, GP = E.V, E.SC, E.TE, E.SY, E.GP
    LD, ST, PE, DV, AC, PO = E.LD, E.ST, E.PE, E.DV, E.AC, E.PO

    p6 = ExitStack()

    def sb(name, shape, dtype):
        return p6.enter_context(nc.sbuf_tensor(name, shape, dtype))

    _pass_barrier(E)
    xa_in = sb("p6_xa", [128, 2 * D], F32)
    x1_in = sb("p6_x1", [128, 2 * D], F32)
    o_sb = sb("p6_o", [128, 2 * D], F32)
    sq_scr = sb("p6_sq", [128, D], BF16)
    stt = sb("p6_stt", [128, 8], F32)

    SY.wait_ge(ST.sem, ST.n)
    free_po = {}
    st_after = {}
    prev_sq_guard = 0
    for i in range(NT):
        slot = i % 2
        if i >= 2:
            SY.wait_ge(PO.sem, free_po[i - 2])
        LD.inc(SY.dma_start(out=xa_in[:, slot * D:slot * D + D],
                            in_=E.xA[i * 128:(i + 1) * 128, :]), 16)
        LD.inc(SY.dma_start(out=x1_in[:, slot * D:slot * D + D],
                            in_=E.x1k[i * 128:(i + 1) * 128, :]), 16)
        x_ld = LD.n
        SC.wait_ge(LD.sem, x_ld)
        V.wait_ge(LD.sem, x_ld)
        _ln_stats(E, xa_in[:, slot * D:slot * D + D], stt, sq_scr, D,
                  prev_guard=(DV.sem, prev_sq_guard) if prev_sq_guard else None)
        prev_sq_guard = DV.n
        if i >= 2:
            V.wait_ge(ST.sem, st_after[i - 2])
        DV.inc(V.tensor_scalar(out=o_sb[:, slot * D:slot * D + D],
                               in0=xa_in[:, slot * D:slot * D + D],
                               scalar1=stt[:, 2:3], scalar2=stt[:, 3:4],
                               op0=ALU.subtract, op1=ALU.mult))
        GP.wait_ge(DV.sem, DV.n)
        GP.wait_ge(LD.sem, x_ld)
        PO.inc(GP.tensor_add(o_sb[:, slot * D:slot * D + D],
                             o_sb[:, slot * D:slot * D + D],
                             x1_in[:, slot * D:slot * D + D]))
        free_po[i] = PO.n
        ST.inc(GP.dma_start(out=E.out[i * 128:(i + 1) * 128, :],
                            in_=o_sb[:, slot * D:slot * D + D]), 16)
        st_after[i] = ST.n
    GP.wait_ge(ST.sem, ST.n)
    p6.close()


def kernel(L0_attn_x, L0_attn_Wq, L0_attn_Wk, L0_attn_Wv, L0_attn_Wo, L0_attn_rf,
           L0_ffn_wqkv, L0_ffn_q_scale, L0_ffn_k_scale, L0_ffn_wo, L0_ffn_ln_g,
           L0_ffn_ln_b, L0_ffn_w1, L0_ffn_b1, L0_ffn_w2, L0_ffn_b2,
           _passes=("P0", "P1", "A0", "B0", "A1", "B1"), _cores=8):
    consts = _host_consts()
    rf_dup = np.concatenate([np.asarray(L0_attn_rf), np.asarray(L0_attn_rf)], axis=0)
    rf_dup = np.ascontiguousarray(rf_dup, dtype=np.float32)

    def a(x):
        return np.ascontiguousarray(np.asarray(x), dtype=np.float32)

    common = dict(
        wq=a(L0_attn_Wq), wk=a(L0_attn_Wk), wv=a(L0_attn_Wv), wo_a=a(L0_attn_Wo),
        rf_dup=rf_dup,
        f_wqkv=a(L0_ffn_wqkv), f_qs=a(L0_ffn_q_scale), f_ks=a(L0_ffn_k_scale),
        f_wo=a(L0_ffn_wo), f_g=a(L0_ffn_ln_g), f_b=a(L0_ffn_ln_b),
        f_w1=a(L0_ffn_w1), f_b1=a(L0_ffn_b1), f_w2=a(L0_ffn_w2), f_b2=a(L0_ffn_b2),
        ident=consts["ident"], ones_bf=consts["ones_bf"], shift_tbl=consts["shift_tbl"],
        t1q=consts["t1q"], t2q=consts["t2q"], t1k=consts["t1k"], t2k=consts["t2k"],
        m_ge=consts["m_ge"], m_le=consts["m_le"],
    )
    xfull = np.asarray(L0_attn_x)
    in_maps = []
    for c in range(_cores):
        m = dict(common)
        m["x0"] = np.ascontiguousarray(xfull[c], dtype=np.float32)
        in_maps.append(m)

    nc = build_nc(passes=_passes)
    res = run_bass_kernel_spmd(nc, in_maps, core_ids=list(range(_cores)))
    return np.stack([r["out"] for r in res.results], axis=0)


# version 09
# speedup vs baseline: 1.3391x; 1.0416x over previous
import numpy as np
import ml_dtypes
from contextlib import ExitStack

import concourse.bass as bass
import concourse.mybir as mybir
from concourse.bass_utils import run_bass_kernel_spmd

F32 = mybir.dt.float32
BF16 = mybir.dt.bfloat16
AF = mybir.ActivationFunctionType
ALU = mybir.AluOpType
AX = mybir.AxisListType

S = 4096
D = 1024
NT = 32
NCH = 8
CT = 4
HA = 16
FA = 128
HB = 8
DB = 64
INNER = 512
HID = 2730
EPS = 1e-5
MAGIC = float(3 * 2.0 ** 22)
INV2PI = float(1.0 / (2 * np.pi))
NEG2PI = float(-2 * np.pi)
ZCLIP = float(1e-6 * FA)
NEG_BIG = -1e30
LNEPS = 1e-30


def _host_consts():
    c = {}
    c["ident"] = np.eye(128, dtype=np.float32).astype(ml_dtypes.bfloat16)
    st = np.zeros((128, 1), np.float32)
    st[0:64, 0] = 0.25
    c["shift_tbl"] = st
    t = np.arange(S, dtype=np.float64)
    half = np.arange(0, DB, 2, dtype=np.float64)
    inv_freq = 1.0 / (10000.0 ** (half / DB))
    freqs = t[:, None] * inv_freq[None, :]
    freqs = np.concatenate([freqs, freqs], axis=1)
    base = (half + 0.4 * DB) / (1.4 * DB)
    power = (t - S // 2) / float(512 // 2)
    scale = base[None, :] ** power[:, None]
    scale = np.concatenate([scale, scale], axis=1)
    cos, sin = np.cos(freqs), np.sin(freqs)
    sgn = np.ones((1, DB)); sgn[0, 0:32] = -1.0
    c["t1q"] = (cos * scale * 8.0).astype(np.float32)
    c["t2q"] = (sin * scale * 8.0 * sgn).astype(np.float32)
    c["t1k"] = (cos / scale).astype(np.float32)
    c["t2k"] = (sin / scale * sgn).astype(np.float32)
    ii = np.arange(128)[:, None]
    jj = np.arange(128)[None, :]
    c["m_ge"] = np.where(jj >= ii, 0.0, NEG_BIG).astype(np.float32)
    c["m_le"] = np.where(jj <= ii, 0.0, NEG_BIG).astype(np.float32)
    c["ones_bf"] = np.ones((128, 128), dtype=ml_dtypes.bfloat16)
    return c


class Ctr:
    def __init__(self, es, nc, name):
        self.sem = es.enter_context(nc.semaphore(name))
        self.n = 0

    def inc(self, instr, k=1):
        assert k > 0
        instr.then_inc(self.sem, k)
        self.n += k
        return self.n


class Env:
    pass


def _ln_stats_paranoid(E, src_ap, stt, sq_scr, nfeat, prev_guard=None):
    V, SC = E.V, E.SC
    DV, AC = E.DV, E.AC
    if prev_guard is not None:
        SC.wait_ge(prev_guard[0], prev_guard[1])
    AC.inc(SC.activation(out=sq_scr[:], in_=src_ap, func=AF.Square,
                         accum_out=stt[:, 1:2]))
    sq_ac = AC.n
    V.wait_ge(AC.sem, sq_ac)
    DV.inc(V.reduce_sum(out=stt[:, 0:1], in_=src_ap, axis=AX.X))
    V.wait_ge(DV.sem, DV.n)
    DV.inc(V.tensor_scalar_mul(stt[:, 2:3], stt[:, 0:1], 1.0 / nfeat))
    V.wait_ge(DV.sem, DV.n)
    DV.inc(V.tensor_mul(stt[:, 5:6], stt[:, 2:3], stt[:, 2:3]))
    V.wait_ge(DV.sem, DV.n)
    DV.inc(V.tensor_scalar_mul(stt[:, 6:7], stt[:, 1:2], 1.0 / nfeat))
    V.wait_ge(DV.sem, DV.n)
    DV.inc(V.tensor_scalar_add(stt[:, 6:7], stt[:, 6:7], EPS))
    V.wait_ge(DV.sem, DV.n)
    DV.inc(E.V.tensor_sub(stt[:, 6:7], stt[:, 6:7], stt[:, 5:6]))
    SC.wait_ge(DV.sem, DV.n)
    AC.inc(SC.activation(out=stt[:, 7:8], in_=stt[:, 6:7], func=AF.Sqrt))
    V.wait_ge(AC.sem, AC.n)
    DV.inc(V.reciprocal(stt[:, 3:4], stt[:, 7:8]))
    V.wait_ge(DV.sem, DV.n)
    DV.inc(V.tensor_mul(stt[:, 4:5], stt[:, 2:3], stt[:, 3:4]))
    V.wait_ge(DV.sem, DV.n)
    DV.inc(V.tensor_scalar_mul(stt[:, 4:5], stt[:, 4:5], -1.0))
    V.wait_ge(DV.sem, DV.n)
    return DV.n


def _ln_stats(E, src_ap, stt, sq_scr, nfeat, prev_guard=None):
    if True:
        return _ln_stats_paranoid(E, src_ap, stt, sq_scr, nfeat, prev_guard)
    """Emit LN stats for one [128, nfeat] tile (src may be PSUM or SBUF).
    Produces stt[:,2:3]=mu, stt[:,3:4]=rstd, stt[:,4:5]=-mu*rstd.
    prev_guard: (sem, val) the ACT Square must wait on (WAR for stt[:,1:2])."""
    V, SC = E.V, E.SC
    DV, AC = E.DV, E.AC
    if prev_guard is not None:
        SC.wait_ge(prev_guard[0], prev_guard[1])
    AC.inc(SC.activation(out=sq_scr[:], in_=src_ap, func=AF.Square,
                         accum_out=stt[:, 1:2]))
    sq_ac = AC.n
    V.wait_ge(AC.sem, sq_ac)
    V.reduce_sum(out=stt[:, 0:1], in_=src_ap, axis=AX.X)
    V.tensor_scalar_mul(stt[:, 2:3], stt[:, 0:1], 1.0 / nfeat)
    V.tensor_mul(stt[:, 5:6], stt[:, 2:3], stt[:, 2:3])
    V.tensor_scalar_mul(stt[:, 6:7], stt[:, 1:2], 1.0 / nfeat)
    V.tensor_scalar_add(stt[:, 6:7], stt[:, 6:7], EPS)
    DV.inc(E.V.tensor_sub(stt[:, 6:7], stt[:, 6:7], stt[:, 5:6]))
    SC.wait_ge(DV.sem, DV.n)
    AC.inc(SC.activation(out=stt[:, 7:8], in_=stt[:, 6:7], func=AF.Sqrt))
    V.wait_ge(AC.sem, AC.n)
    V.reciprocal(stt[:, 3:4], stt[:, 7:8])
    V.tensor_mul(stt[:, 4:5], stt[:, 2:3], stt[:, 3:4])
    DV.inc(V.tensor_scalar_mul(stt[:, 4:5], stt[:, 4:5], -1.0))
    return DV.n


def _pass_barrier(E):
    sems = [(E.PE, E.PE.n), (E.DV, E.DV.n), (E.AC, E.AC.n), (E.PO, E.PO.n),
            (E.LD, E.LD.n), (E.ST, E.ST.n)]
    for eng in (E.SY, E.GP, E.V, E.SC, E.TE):
        for ctr, val in sems:
            if val:
                eng.wait_ge(ctr.sem, val)


def _load_weights_bf16(E, pairs, stage, stage_cols=D):
    SY, GP, LD, PO = E.SY, E.GP, E.LD, E.PO
    base_po = PO.n
    for i, (src, dst) in enumerate(pairs):
        slot = i % 2
        if i >= 2:
            SY.wait_ge(PO.sem, base_po + i - 1)
        else:
            SY.wait_ge(PO.sem, base_po)
        LD.inc(SY.dma_start(out=stage[:, slot * stage_cols:(slot + 1) * stage_cols],
                            in_=src), 16)
        GP.wait_ge(LD.sem, LD.n)
        PO.inc(GP.tensor_copy(dst, stage[:, slot * stage_cols:(slot + 1) * stage_cols]))
    return PO.n


def build_nc(passes=("P0", "P1", "A0", "B0", "A1", "B1")):
    nc = bass.Bass("TRN2", target_bir_lowering=False)
    E = Env()
    E.nc = nc
    E.passes = passes
    E.fold_p6 = ("P6" not in passes) and ("B1" in passes)

    E.x0 = nc.dram_tensor("x0", [S, D], F32, kind="ExternalInput").ap()
    E.wq_d = nc.dram_tensor("wq", [D, D], F32, kind="ExternalInput").ap()
    E.wk_d = nc.dram_tensor("wk", [D, D], F32, kind="ExternalInput").ap()
    E.wv_d = nc.dram_tensor("wv", [D, D], F32, kind="ExternalInput").ap()
    E.wo_d = nc.dram_tensor("wo_a", [D, D], F32, kind="ExternalInput").ap()
    E.rf_d = nc.dram_tensor("rf_dup", [128, FA], F32, kind="ExternalInput").ap()
    E.f_wqkv = nc.dram_tensor("f_wqkv", [2, D, 3 * INNER], F32, kind="ExternalInput").ap()
    E.f_qs = nc.dram_tensor("f_qs", [2, DB], F32, kind="ExternalInput").ap()
    E.f_ks = nc.dram_tensor("f_ks", [2, DB], F32, kind="ExternalInput").ap()
    E.f_wo = nc.dram_tensor("f_wo", [2, INNER, D], F32, kind="ExternalInput").ap()
    E.f_g = nc.dram_tensor("f_g", [2, D], F32, kind="ExternalInput").ap()
    E.f_b = nc.dram_tensor("f_b", [2, D], F32, kind="ExternalInput").ap()
    E.f_w1 = nc.dram_tensor("f_w1", [2, D, 2 * HID], F32, kind="ExternalInput").ap()
    E.f_b1 = nc.dram_tensor("f_b1", [2, 2 * HID], F32, kind="ExternalInput").ap()
    E.f_w2 = nc.dram_tensor("f_w2", [2, HID, D], F32, kind="ExternalInput").ap()
    E.f_b2 = nc.dram_tensor("f_b2", [2, D], F32, kind="ExternalInput").ap()
    E.ident_d = nc.dram_tensor("ident", [128, 128], BF16, kind="ExternalInput").ap()
    E.ones_d = nc.dram_tensor("ones_bf", [128, 128], BF16, kind="ExternalInput").ap()
    E.shift_d = nc.dram_tensor("shift_tbl", [128, 1], F32, kind="ExternalInput").ap()
    E.t1q_d = nc.dram_tensor("t1q", [S, DB], F32, kind="ExternalInput").ap()
    E.t2q_d = nc.dram_tensor("t2q", [S, DB], F32, kind="ExternalInput").ap()
    E.t1k_d = nc.dram_tensor("t1k", [S, DB], F32, kind="ExternalInput").ap()
    E.t2k_d = nc.dram_tensor("t2k", [S, DB], F32, kind="ExternalInput").ap()
    E.mge_d = nc.dram_tensor("m_ge", [128, 128], F32, kind="ExternalInput").ap()
    E.mle_d = nc.dram_tensor("m_le", [128, 128], F32, kind="ExternalInput").ap()

    E.out = nc.dram_tensor("out", [S, D], F32, kind="ExternalOutput").ap()
    E.xA = nc.dram_tensor("xA", [S, D], F32, kind="Internal").ap()
    E.x1k = nc.dram_tensor("x1k", [S, D], F32, kind="Internal").ap()

    es = ExitStack()
    E.es = es

    def sbuf(name, shape, dtype):
        return es.enter_context(nc.sbuf_tensor(name, shape, dtype))
    E.sbuf = sbuf

    E.ident_sb = sbuf("ident_sb", [128, 128], BF16)
    E.ones_sb = sbuf("ones_sb", [128, 128], BF16)
    E.shift_sb = sbuf("shift_sb", [128, 1], F32)
    E.mge_sb = sbuf("mge_sb", [128, 128], F32)
    E.mle_sb = sbuf("mle_sb", [128, 128], F32)
    E.cns = sbuf("cns", [128, 4], F32)
    nc.const_aps.register(F32, INV2PI, E.cns[:, 0:1])
    nc.const_aps.register(F32, NEG2PI, E.cns[:, 1:2])
    nc.const_aps.register(F32, -0.5, E.cns[:, 2:3])
    nc.const_aps.register(F32, LNEPS, E.cns[:, 3:4])

    E.LD = Ctr(es, nc, "LD")
    E.ST = Ctr(es, nc, "ST")
    E.PE = Ctr(es, nc, "PE")
    E.DV = Ctr(es, nc, "DV")
    E.AC = Ctr(es, nc, "AC")
    E.PO = Ctr(es, nc, "PO")

    E.V, E.SC, E.TE, E.SY, E.GP = nc.vector, nc.scalar, nc.tensor, nc.sync, nc.gpsimd

    with nc.Block():
        E.LD.inc(E.SY.dma_start(out=E.ident_sb[:], in_=E.ident_d[:]), 16)
        E.LD.inc(E.SY.dma_start(out=E.ones_sb[:], in_=E.ones_d[:]), 16)
        E.LD.inc(E.SY.dma_start(out=E.shift_sb[:], in_=E.shift_d[:]), 16)
        E.LD.inc(E.SY.dma_start(out=E.mge_sb[:], in_=E.mge_d[:]), 16)
        E.LD.inc(E.SY.dma_start(out=E.mle_sb[:], in_=E.mle_d[:]), 16)
        E.V.memset(E.cns[:, 0:1], INV2PI)
        E.V.memset(E.cns[:, 2:3], -0.5)
        E.V.memset(E.cns[:, 3:4], LNEPS)
        E.DV.inc(E.V.memset(E.cns[:, 1:2], NEG2PI))
        E.cns_dv = E.DV.n
        E.consts_ld = E.LD.n

        E.stageA_es = ExitStack()
        E.ln1_fm = E.stageA_es.enter_context(nc.sbuf_tensor("ln1_fm", [128, 8 * S], BF16))
        E.kv_sb = E.stageA_es.enter_context(nc.sbuf_tensor("kv_sb", [128, HA * 65], BF16))
        E.kv_done = 0

        if "P0" in passes:
            with nc.named_scope("P0"):
                _emit_P0(E)
        p1lev = [p for p in passes if p.startswith("P1")]
        if p1lev:
            E.p1_level = int(p1lev[0][3:]) if "@" in p1lev[0] else 99
            with nc.named_scope("P1"):
                _emit_P1(E)
        if any(("A%d" % li) in passes or ("B%d" % li) in passes for li in (0, 1)):
            E.stageA_es.close()
            E.stageA_closed = True
        for li in (0, 1):
            if ("A%d" % li) in passes:
                with nc.named_scope("A%d" % li):
                    _emit_A(E, li)
            if ("B%d" % li) in passes:
                with nc.named_scope("B%d" % li):
                    _emit_B(E, li)
        if "P6" in passes:
            with nc.named_scope("P6"):
                _emit_P6(E)
        if "DBGKV" in passes:
            V, GP, SY, LD, ST, PO, DV = E.V, E.GP, E.SY, E.LD, E.ST, E.PO, E.DV
            tmp = E.stageA_es.enter_context(nc.sbuf_tensor("dbgkv_tmp", [128, 1040], F32))
            V.wait_ge(DV.sem, E.kv_done)
            DV.inc(V.tensor_copy(tmp[:], E.kv_sb[:]))
            GP.wait_ge(DV.sem, E.DV.n)
            ST.inc(GP.dma_start(out=E.out[0:128, 0:1024], in_=tmp[:, 0:1024]), 16)
            ST.inc(GP.dma_start(out=E.out[128:256, 0:16], in_=tmp[:, 1024:1040]), 16)
            GP.wait_ge(ST.sem, ST.n)
        if "DBGX1" in passes or "DBGXA" in passes:
            srcT = E.x1k if "DBGX1" in passes else E.xA
            V, GP, SY, LD, ST, PO = E.V, E.GP, E.SY, E.LD, E.ST, E.PO
            SY.wait_ge(ST.sem, ST.n)
            tmp = (sbuf("dbg_tmp", [128, D], F32)
                   if getattr(E, "stageA_closed", False)
                   else E.stageA_es.enter_context(nc.sbuf_tensor("dbg_tmp", [128, D], F32)))
            for i in range(NT):
                LD.inc(SY.dma_start(out=tmp[:], in_=srcT[i * 128:(i + 1) * 128, :]), 16)
                GP.wait_ge(LD.sem, LD.n)
                ST.inc(GP.dma_start(out=E.out[i * 128:(i + 1) * 128, :], in_=tmp[:]), 16)
                SY.wait_ge(ST.sem, ST.n)
            GP.wait_ge(ST.sem, ST.n)

    if not getattr(E, "stageA_closed", False):
        E.stageA_es.close()
    es.close()
    return nc


def _emit_P0(E):
    nc = E.nc
    V, SC, TE, SY, GP = E.V, E.SC, E.TE, E.SY, E.GP
    LD, ST, PE, DV, AC, PO = E.LD, E.ST, E.PE, E.DV, E.AC, E.PO

    p0 = ExitStack()

    def sb(name, shape, dtype):
        return p0.enter_context(nc.sbuf_tensor(name, shape, dtype))

    def ps(name, shape, dtype=F32):
        return p0.enter_context(nc.psum_tensor(name, shape, dtype))

    wk_bf = sb("wk_bf", [128, 8 * D], BF16)
    wv_bf = sb("wv_bf", [128, 8 * D], BF16)
    wstage = sb("p0_wstage", [128, 2 * D], F32)
    rf_sb = sb("rf_sb", [128, FA], F32)
    rf_bf = sb("rf_bf", [128, FA], BF16)

    x_in = sb("p0_x_in", [128, 2 * D], F32)
    ln1_tm = sb("p0_ln1_tm", [128, D], BF16)
    sq_scr = sb("p0_sq_scr", [128, D], BF16)
    stt = sb("p0_stt", [128, 8], F32)
    kfm = sb("p0_kfm", [128, 8 * 512], BF16)
    kp_fm = sb("p0_kp_fm", [128, 2 * 512], BF16)
    kp_tm = sb("p0_kp_tm", [128, HA * 512], BF16)
    v_tm = sb("p0_v_tm", [128, 4 * 1040], BF16)
    u0b = sb("p0_u0", [128, 2 * 512], F32)
    tb = sb("p0_t", [128, 2 * 512], F32)
    rb = sb("p0_r", [128, 2 * 512], F32)

    ps_a0 = ps("ps_a0", [128, 1024], F32)
    ps_a1 = ps("ps_a1", [128, 1024], F32)
    ps_b = ps("ps_b", [128, 1024], BF16)
    ps_kv = [ps("ps_kv0", [128, 455], F32), ps("ps_kv1", [128, 455], F32),
             ps("ps_kv2", [128, 130], F32)]

    def kv_slice(h):
        return ps_kv[h // 7][:, (h % 7) * 65:(h % 7) * 65 + 65]

    if not getattr(E, "p0_skip_weights", False):
        LD.inc(SY.dma_start(out=rf_sb[:], in_=rf_d_ap(E)), 16)
        GP.wait_ge(LD.sem, LD.n)
        PO.inc(GP.tensor_copy(rf_bf[:], rf_sb[:]))
        pairs = [(E.wk_d[i * 128:(i + 1) * 128, :], wk_bf[:, i * D:(i + 1) * D]) for i in range(8)]
        pairs += [(E.wv_d[i * 128:(i + 1) * 128, :], wv_bf[:, i * D:(i + 1) * D]) for i in range(8)]
        w_ready = _load_weights_bf16(E, pairs, wstage)
    else:
        w_ready = PO.n

    if not getattr(E, "p0_skip_memset", False):
        for tt in range(4):
            V.memset(v_tm[:, tt * 1040:(tt + 1) * 1040]
                     .rearrange("p (h s) -> p h s", s=65)[:, :, 64:65], 1.0)

    x_free_dv = {}
    evac = {}
    rel = {}
    kv_started = [False, False, False]
    prev_norm_guard = 0
    prev_sin_guard = 0
    prev_sq_guard = 0

    p0_nch = getattr(E, "p0_nch", NCH)
    p0_tiles = getattr(E, "p0_tiles", CT)
    for c in range(p0_nch):
        c0 = c * 512
        for j in range(p0_tiles):
            i = c * CT + j
            slot = i % 2
            if i >= 2:
                SY.wait_ge(DV.sem, x_free_dv[i - 2])
            LD.inc(SY.dma_start(out=x_in[:, slot * D:slot * D + D],
                                in_=E.x0[i * 128:(i + 1) * 128, :]), 16)
            x_ld = LD.n

            src = x_in[:, slot * D:slot * D + D]
            SC.wait_ge(LD.sem, x_ld)
            V.wait_ge(LD.sem, x_ld)
            _ln_stats(E, src, stt, sq_scr, D,
                      prev_guard=(DV.sem, prev_sq_guard) if prev_sq_guard else None)
            if prev_norm_guard:
                V.wait_ge(PE.sem, prev_norm_guard)
            DV.inc(V.tensor_scalar(out=ln1_tm[:], in0=src,
                                   scalar1=stt[:, 2:3], scalar2=stt[:, 3:4],
                                   op0=ALU.subtract, op1=ALU.mult))
            x_free_dv[i] = DV.n
            prev_sq_guard = DV.n
            norm_dv = DV.n

            TE.wait_ge(DV.sem, norm_dv)
            TE.wait_ge(LD.sem, E.consts_ld)
            if "ps_b" in evac:
                TE.wait_ge(DV.sem, evac["ps_b"])
            for ft in range(8):
                mm = TE.transpose(ps_b[:, ft * 128:(ft + 1) * 128],
                                  ln1_tm[:, ft * 128:(ft + 1) * 128], E.ident_sb[:])
                if ft == 7:
                    PE.inc(mm)
            prev_norm_guard = PE.n
            V.wait_ge(PE.sem, PE.n)
            DV.inc(V.tensor_copy(
                E.ln1_fm[:].rearrange("p (f t) -> p f t", t=S)[:, :, i * 128:(i + 1) * 128],
                ps_b[:].rearrange("p (f t) -> p f t", t=128)))
            evac["ps_b"] = DV.n

        ln1_ready = DV.n
        if getattr(E, "p0_skip_rest", False):
            continue

        TE.wait_ge(DV.sem, ln1_ready)
        if c == 0:
            TE.wait_ge(PO.sem, w_ready)
        for ft in range(8):
            half = [ps_a0, ps_a1][ft % 2][:, 0:512]
            key = ("ak", ft % 2)
            if key in evac:
                TE.wait_ge(DV.sem, evac[key])
            if ft % 2 == 0 and ("v",) in evac:
                TE.wait_ge(DV.sem, evac[("v",)])
            if ft == 1 and ("projk", 1) in evac:
                TE.wait_ge(AC.sem, evac[("projk", 1)])
            for kt in range(8):
                mm = TE.matmul(half,
                               wk_bf[:, kt * D + ft * 128: kt * D + (ft + 1) * 128],
                               E.ln1_fm[:, kt * S + c0: kt * S + c0 + 512],
                               start=(kt == 0), stop=(kt == 7))
                if kt == 7:
                    PE.inc(mm)
            V.wait_ge(PE.sem, PE.n)
            DV.inc(V.tensor_copy(kfm[:, ft * 512:(ft + 1) * 512], half))
            evac[key] = DV.n
        kfm_ready = DV.n

        for tt in range(CT):
            TE.wait_ge(DV.sem, evac[("ak", 0)])
            if ("v",) in evac:
                TE.wait_ge(DV.sem, evac[("v",)])
            for kt in range(8):
                for nh in range(2):
                    mm = TE.matmul(ps_a0[:, nh * 512:(nh + 1) * 512],
                                   E.ln1_fm[:, kt * S + c0 + tt * 128: kt * S + c0 + (tt + 1) * 128],
                                   wv_bf[:, kt * D + nh * 512: kt * D + (nh + 1) * 512],
                                   start=(kt == 0), stop=(kt == 7))
                    if kt == 7 and nh == 1:
                        PE.inc(mm)
            V.wait_ge(PE.sem, PE.n)
            DV.inc(V.tensor_copy(
                v_tm[:, tt * 1040:(tt + 1) * 1040]
                .rearrange("p (h s) -> p h s", s=65)[:, :, 0:64],
                ps_a0[:].rearrange("p (h d) -> p h d", d=64)))
            evac[("v",)] = DV.n

        proj_slots = [ps_a1[:, 512:1024], ps_a1[:, 0:512]]
        sin_acs = {}
        for g in range(HA + 1):
            if g < HA:
                h = g
                base = (h % 2) * 64
                sp = g % 2
                proj = proj_slots[sp]
                if g < 2:
                    TE.wait_ge(DV.sem, kfm_ready)
                if ("projk", sp) in evac:
                    TE.wait_ge(AC.sem, evac[("projk", sp)])
                PE.inc(TE.matmul(proj, rf_bf[base:base + 64, :],
                                 kfm[base:base + 64, (h // 2) * 512:(h // 2) * 512 + 512],
                                 start=True, stop=True))
                SC.wait_ge(PE.sem, PE.n)
                if c == 0 and g == 0:
                    SC.wait_ge(DV.sem, max(E.cns_dv, 0))
                    SC.wait_ge(LD.sem, E.consts_ld)
                if ("p0u0", sp) in rel:
                    SC.wait_ge(DV.sem, rel[("p0u0", sp)])
                AC.inc(SC.activation(out=u0b[:, sp * 512:(sp + 1) * 512], in_=proj,
                                     func=AF.Identity,
                                     scale=INV2PI, bias=E.shift_sb[:, 0:1]))
                evac[("projk", sp)] = AC.n
                V.wait_ge(AC.sem, AC.n)
                DV.inc(V.tensor_scalar(tb[:, sp * 512:(sp + 1) * 512],
                                       u0b[:, sp * 512:(sp + 1) * 512],
                                       MAGIC, None, ALU.add))
                V.wait_ge(DV.sem, DV.n)
                if ("p0rb", sp) in rel:
                    V.wait_ge(AC.sem, rel[("p0rb", sp)])
                DV.inc(V.scalar_tensor_tensor(out=rb[:, sp * 512:(sp + 1) * 512],
                                              in0=tb[:, sp * 512:(sp + 1) * 512],
                                              scalar=MAGIC,
                                              in1=u0b[:, sp * 512:(sp + 1) * 512],
                                              op0=ALU.subtract,
                                              op1=ALU.subtract))
                rel[("p0u0", sp)] = DV.n
                SC.wait_ge(DV.sem, DV.n)
                if ("p0kpf", sp) in rel:
                    SC.wait_ge(PE.sem, rel[("p0kpf", sp)])
                AC.inc(SC.activation(out=kp_fm[:, sp * 512:(sp + 1) * 512],
                                     in_=rb[:, sp * 512:(sp + 1) * 512],
                                     func=AF.Sin, scale=NEG2PI))
                rel[("p0rb", sp)] = AC.n
                sin_acs[g] = AC.n
            if g >= 1:
                j = g - 1
                sj = j % 2
                TE.wait_ge(AC.sem, sin_acs[j])
                TE.wait_ge(DV.sem, evac["ps_b"])
                for tt in range(CT):
                    mm = TE.transpose(ps_b[:, tt * 128:(tt + 1) * 128],
                                      kp_fm[:, sj * 512 + tt * 128: sj * 512 + (tt + 1) * 128],
                                      E.ident_sb[:])
                    if tt == 3:
                        PE.inc(mm)
                rel[("p0kpf", sj)] = PE.n
                V.wait_ge(PE.sem, PE.n)
                DV.inc(V.tensor_copy(kp_tm[:, j * 512:(j + 1) * 512], ps_b[:, 0:512]))
                evac["ps_b"] = DV.n
                kpt_dv = DV.n

                TE.wait_ge(DV.sem, kpt_dv)
                bank = j // 7
                for tt in range(CT):
                    st_flag = not kv_started[bank]
                    kv_started[bank] = True
                    mm = TE.matmul(kv_slice(j),
                                   kp_tm[:, j * 512 + tt * 128: j * 512 + (tt + 1) * 128],
                                   v_tm[:, tt * 1040 + j * 65: tt * 1040 + j * 65 + 65],
                                   start=st_flag,
                                   stop=(c == NCH - 1 and tt == CT - 1))
                    if tt == CT - 1:
                        PE.inc(mm)
                prev_sin_guard = PE.n

    V.wait_ge(PE.sem, PE.n)
    DV.inc(V.tensor_copy(E.kv_sb[:, 0:455], ps_kv[0][:]))
    DV.inc(V.tensor_copy(E.kv_sb[:, 455:910], ps_kv[1][:]))
    DV.inc(V.tensor_copy(E.kv_sb[:, 910:1040], ps_kv[2][:]))
    E.kv_done = DV.n
    dbg = getattr(E, "p0_dbg", None)
    if dbg:
        tmp = sb("p0dbg_tmp", [128, 1024], F32)
        row = 0
        for name in dbg:
            buf = {"ln1": E.ln1_fm, "kfm": kfm, "kp": kp_tm, "v": v_tm,
                   "lntm": ln1_tm, "stt": stt, "xin": x_in}[name]
            ncols = {"ln1": 4096, "kfm": 4096, "kp": 4096, "v": 4160,
                     "lntm": 1024, "stt": 8, "xin": 2048}[name]
            for blk in range(0, min(ncols, 4096), 1024):
                w = min(1024, ncols - blk)
                V.wait_ge(DV.sem, DV.n)
                V.wait_ge(ST.sem, ST.n)
                V.memset(tmp[:], 0.0)
                DV.inc(V.tensor_copy(tmp[:, 0:w], buf[:, blk:blk + w]))
                GP.wait_ge(DV.sem, DV.n)
                ST.inc(GP.dma_start(out=E.out[row * 128:(row + 1) * 128, :], in_=tmp[:]), 16)
                GP.wait_ge(ST.sem, ST.n)
                row += 1
    p0.close()


def rf_d_ap(E):
    return E.rf_d[:]


def _emit_P1(E):
    nc = E.nc
    V, SC, TE, SY, GP = E.V, E.SC, E.TE, E.SY, E.GP
    LD, ST, PE, DV, AC, PO = E.LD, E.ST, E.PE, E.DV, E.AC, E.PO

    p1 = ExitStack()

    def sb(name, shape, dtype):
        return p1.enter_context(nc.sbuf_tensor(name, shape, dtype))

    def ps(name, shape, dtype=F32):
        return p1.enter_context(nc.psum_tensor(name, shape, dtype))

    _pass_barrier(E)
    wq_bf = sb("wq_bf", [128, 8 * D], BF16)
    wo_bf = sb("wo_bf", [128, 8 * D], BF16)
    wstage = sb("p1_wstage", [128, 2 * D], F32)
    rf_sb = sb("p1_rf_sb", [128, FA], F32)
    rf_bf = sb("p1_rf_bf", [128, FA], BF16)

    qfm = sb("p1_qfm", [128, 8 * 512], BF16)
    qp_all = sb("p1_qp", [128, HA * 512], BF16)
    u0b = sb("p1_u0", [128, 2 * 512], F32)
    tb = sb("p1_t", [128, 2 * 512], F32)
    rb = sb("p1_r", [128, 2 * 512], F32)
    zinv = sb("p1_zinv", [128, 16], F32)
    attn_tm = sb("p1_attn_tm", [128, D], BF16)
    attn_fm = sb("p1_attn_fm", [128, D], BF16)
    x0_in = sb("p1_x0_in", [128, 2 * D], F32)
    ln2_sb = sb("p1_ln2", [128, D], F32)
    x1_sb = sb("p1_x1", [128, D], F32)
    xa_sb = sb("p1_xa", [128, D], F32)
    sq_scr = sb("p1_sq_scr", [128, D], BF16)
    stt = sb("p1_stt", [128, 8], F32)

    ps_q = [ps("ps_q0", [128, 512], F32), ps("ps_q1", [128, 512], F32)]
    ps_qkv = ps("ps_qkv", [128, 1024], F32)
    ps_z = ps("ps_z", [128, 16], F32)
    ps_b = ps("p1_ps_b", [128, 1024], BF16)
    ps_wo = ps("ps_wo", [128, 1024], F32)

    LD.inc(SY.dma_start(out=rf_sb[:], in_=E.rf_d[:]), 16)
    GP.wait_ge(LD.sem, LD.n)
    PO.inc(GP.tensor_copy(rf_bf[:], rf_sb[:]))
    pairs = [(E.wq_d[i * 128:(i + 1) * 128, :], wq_bf[:, i * D:(i + 1) * D]) for i in range(8)]
    pairs += [(E.wo_d[i * 128:(i + 1) * 128, :], wo_bf[:, i * D:(i + 1) * D]) for i in range(8)]
    w_ready = _load_weights_bf16(E, pairs, wstage)

    evac = {}
    rel = {}
    x0_free = {}
    prev_qkv_pe = 0
    prev_sq_guard = 0
    prev_store_guard = 0
    prev_xa_ac = 0
    prev_xa_st = 0

    lev = getattr(E, "p1_level", 99)
    for c in range(NCH):
        c0 = c * 512
        if lev < 10:
            break
        if c == 0:
            TE.wait_ge(PO.sem, w_ready)
            if E.kv_done:
                TE.wait_ge(DV.sem, E.kv_done)
        for ft in range(8):
            half = ps_q[ft % 2][:]
            key = ("q", ft % 2)
            if key in evac:
                TE.wait_ge(DV.sem, evac[key])
            if ft == 0 and ("proj", 0) in evac:
                TE.wait_ge(AC.sem, evac[("proj", 0)])
            if ft == 0 and ("proj", 1) in evac:
                TE.wait_ge(AC.sem, evac[("proj", 1)])
            for kt in range(8):
                mm = TE.matmul(half,
                               wq_bf[:, kt * D + ft * 128: kt * D + (ft + 1) * 128],
                               E.ln1_fm[:, kt * S + c0: kt * S + c0 + 512],
                               start=(kt == 0), stop=(kt == 7))
                if kt == 7:
                    PE.inc(mm)
            V.wait_ge(PE.sem, PE.n)
            DV.inc(V.tensor_copy(qfm[:, ft * 512:(ft + 1) * 512], half))
            evac[key] = DV.n
        qfm_ready = DV.n

        for h in range(HA if lev >= 20 else 0):
            base = (h % 2) * 64
            sp = h % 2
            if h == 0:
                TE.wait_ge(DV.sem, qfm_ready)
                TE.wait_ge(DV.sem, evac[("q", 0)])
            if ("proj", sp) in evac:
                TE.wait_ge(AC.sem, evac[("proj", sp)])
            PE.inc(TE.matmul(ps_q[sp][:], rf_bf[base:base + 64, :],
                             qfm[base:base + 64, (h // 2) * 512:(h // 2) * 512 + 512],
                             start=True, stop=True))
            SC.wait_ge(PE.sem, PE.n)
            if c == 0 and h == 0:
                SC.wait_ge(DV.sem, max(E.cns_dv, 0))
                SC.wait_ge(LD.sem, E.consts_ld)
            if ("p1u0", sp) in rel:
                SC.wait_ge(DV.sem, rel[("p1u0", sp)])
            AC.inc(SC.activation(out=u0b[:, sp * 512:(sp + 1) * 512],
                                 in_=ps_q[sp][:], func=AF.Identity,
                                 scale=INV2PI, bias=E.shift_sb[:, 0:1]))
            evac[("proj", sp)] = AC.n
            V.wait_ge(AC.sem, AC.n)
            DV.inc(V.tensor_scalar(tb[:, sp * 512:(sp + 1) * 512],
                                   u0b[:, sp * 512:(sp + 1) * 512],
                                   MAGIC, None, ALU.add))
            V.wait_ge(DV.sem, DV.n)
            if ("p1rb", sp) in rel:
                V.wait_ge(AC.sem, rel[("p1rb", sp)])
            DV.inc(V.scalar_tensor_tensor(out=rb[:, sp * 512:(sp + 1) * 512],
                                          in0=tb[:, sp * 512:(sp + 1) * 512],
                                          scalar=MAGIC,
                                          in1=u0b[:, sp * 512:(sp + 1) * 512],
                                          op0=ALU.subtract,
                                          op1=ALU.subtract))
            rel[("p1u0", sp)] = DV.n
            SC.wait_ge(DV.sem, DV.n)
            if prev_qkv_pe and h < 2:
                SC.wait_ge(PE.sem, prev_qkv_pe)
            AC.inc(SC.activation(out=qp_all[:, h * 512:(h + 1) * 512],
                                 in_=rb[:, sp * 512:(sp + 1) * 512],
                                 func=AF.Sin, scale=NEG2PI))
            rel[("p1rb", sp)] = AC.n
        qp_ready = AC.n

        for tt in range(CT if lev >= 30 else 0):
            i = c * CT + tt
            slot = i % 2
            if i >= 2 and (i - 2) in x0_free:
                SY.wait_ge(DV.sem, x0_free[i - 2])
            LD.inc(SY.dma_start(out=x0_in[:, slot * D:slot * D + D],
                                in_=E.x0[i * 128:(i + 1) * 128, :]), 16)
            x_ld = LD.n

            TE.wait_ge(AC.sem, qp_ready)
            if ("qkv",) in evac:
                TE.wait_ge(DV.sem, evac[("qkv",)])
            for h in range(HA):
                mm = None
                if lev != 32:
                    mm = TE.matmul(ps_qkv[:, h * 64:(h + 1) * 64],
                                   qp_all[:, h * 512 + tt * 128: h * 512 + (tt + 1) * 128],
                                   E.kv_sb[:, h * 65: h * 65 + 64],
                                   start=True, stop=True)
                if lev != 31:
                    mm = TE.matmul(ps_z[:, h:h + 1],
                                   qp_all[:, h * 512 + tt * 128: h * 512 + (tt + 1) * 128],
                                   E.kv_sb[:, h * 65 + 64: h * 65 + 65],
                                   start=True, stop=True)
                if h == HA - 1:
                    PE.inc(mm)
            qkv_pe = PE.n
            prev_qkv_pe = qkv_pe
            V.wait_ge(PE.sem, qkv_pe)
            if lev >= 35:
                DV.inc(V.tensor_scalar_max(zinv[:], ps_z[:], ZCLIP))
                V.wait_ge(DV.sem, DV.n)
                DV.inc(V.reciprocal(zinv[:], zinv[:]))
                V.wait_ge(DV.sem, DV.n)
                DV.inc(V.tensor_tensor(
                    out=attn_tm[:].rearrange("p (h d) -> p h d", d=64),
                    in0=ps_qkv[:].rearrange("p (h d) -> p h d", d=64),
                    in1=zinv[:, :, None].broadcast_to([128, 16, 64]),
                    op=ALU.mult))
            else:
                DV.inc(V.tensor_copy(attn_tm[:], ps_qkv[:]))
            evac[("qkv",)] = DV.n
            attn_dv = DV.n

            if lev < 40:
                continue
            TE.wait_ge(DV.sem, attn_dv)
            TE.wait_ge(LD.sem, E.consts_ld)
            if ("ps_b",) in evac:
                TE.wait_ge(DV.sem, evac[("ps_b",)])
            for ft in range(8):
                mm = TE.transpose(ps_b[:, ft * 128:(ft + 1) * 128],
                                  attn_tm[:, ft * 128:(ft + 1) * 128], E.ident_sb[:])
                if ft == 7:
                    PE.inc(mm)
            V.wait_ge(PE.sem, PE.n)
            DV.inc(V.tensor_copy(attn_fm[:], ps_b[:]))
            evac[("ps_b",)] = DV.n
            afm_dv = DV.n

            if lev < 50:
                continue
            TE.wait_ge(DV.sem, afm_dv)
            if ("wo",) in evac:
                TE.wait_ge(AC.sem, evac[("wo",)])
            for kt in range(8):
                for nh in range(2):
                    mm = TE.matmul(ps_wo[:, nh * 512:(nh + 1) * 512],
                                   attn_fm[:, kt * 128:(kt + 1) * 128],
                                   wo_bf[:, kt * D + nh * 512: kt * D + (nh + 1) * 512],
                                   start=(kt == 0), stop=(kt == 7))
                    if kt == 7 and nh == 1:
                        PE.inc(mm)
            wo_pe = PE.n

            if lev < 61:
                continue
            SC.wait_ge(PE.sem, wo_pe)
            V.wait_ge(PE.sem, wo_pe)
            _ln_stats(E, ps_wo[:], stt, sq_scr, D,
                      prev_guard=(DV.sem, prev_sq_guard) if prev_sq_guard else None)
            stat_dv = DV.n
            prev_sq_guard = DV.n
            if lev < 62:
                evac[("wo",)] = 0
                continue
            SC.wait_ge(DV.sem, stat_dv)
            AC.inc(SC.activation(out=ln2_sb[:], in_=ps_wo[:], func=AF.Identity,
                                 scale=stt[:, 3:4], bias=stt[:, 4:5]))
            evac[("wo",)] = AC.n
            ln2_ac = AC.n
            if lev < 63:
                continue

            V.wait_ge(AC.sem, ln2_ac)
            V.wait_ge(LD.sem, x_ld)
            if prev_store_guard:
                V.wait_ge(ST.sem, prev_store_guard)
            if prev_xa_ac:
                V.wait_ge(AC.sem, prev_xa_ac)
            DV.inc(V.tensor_add(x1_sb[:], x0_in[:, slot * D:slot * D + D], ln2_sb[:]))
            x1_dv = DV.n
            x0_free[i] = DV.n
            GP.wait_ge(DV.sem, x1_dv)
            ST.inc(GP.dma_start(out=E.x1k[i * 128:(i + 1) * 128, :], in_=x1_sb[:]), 16)
            prev_store_guard = ST.n
            if lev < 64:
                continue

            SC.wait_ge(DV.sem, x1_dv)
            _ln_stats(E, x1_sb[:], stt, sq_scr, D, prev_guard=(DV.sem, stat_dv))
            prev_sq_guard = DV.n
            stat2_dv = DV.n
            SC.wait_ge(DV.sem, stat2_dv)
            if prev_xa_st:
                SC.wait_ge(ST.sem, prev_xa_st)
            AC.inc(SC.activation(out=xa_sb[:], in_=x1_sb[:], func=AF.Identity,
                                 scale=stt[:, 3:4], bias=stt[:, 4:5]))
            prev_xa_ac = AC.n
            GP.wait_ge(AC.sem, prev_xa_ac)
            ST.inc(GP.dma_start(out=E.xA[i * 128:(i + 1) * 128, :], in_=xa_sb[:]), 16)
            prev_store_guard = ST.n
            prev_xa_st = ST.n

    dbg = getattr(E, "p1_dbg", None)
    if dbg:
        tmp = sb("p1dbg_tmp", [128, 1024], F32)
        row = 0
        for name in dbg:
            buf = {"qfm": qfm, "qp": qp_all, "attn_tm": attn_tm, "attn_fm": attn_fm,
                   "zinv": zinv, "stt": stt, "ln2": ln2_sb, "x1": x1_sb, "xa": xa_sb}[name]
            ncols = {"qfm": 4096, "qp": 4096, "attn_tm": 1024, "attn_fm": 1024,
                     "zinv": 16, "stt": 8, "ln2": 1024, "x1": 1024, "xa": 1024}[name]
            for blk in range(0, min(ncols, 4096), 1024):
                w = min(1024, ncols - blk)
                V.wait_ge(DV.sem, DV.n)
                V.wait_ge(ST.sem, ST.n)
                V.wait_ge(PO.sem, PO.n)
                V.wait_ge(AC.sem, AC.n)
                V.memset(tmp[:], 0.0)
                DV.inc(V.tensor_copy(tmp[:, 0:w], buf[:, blk:blk + w]))
                GP.wait_ge(DV.sem, DV.n)
                ST.inc(GP.dma_start(out=E.out[row * 128:(row + 1) * 128, :], in_=tmp[:]), 16)
                GP.wait_ge(ST.sem, ST.n)
                row += 1
    p1.close()


def _emit_A(E, li):
    nc = E.nc
    V, SC, TE, SY, GP = E.V, E.SC, E.TE, E.SY, E.GP
    LD, ST, PE, DV, AC, PO = E.LD, E.ST, E.PE, E.DV, E.AC, E.PO

    pa = ExitStack()

    def sb(name, shape, dtype):
        return pa.enter_context(nc.sbuf_tensor("a%d_%s" % (li, name), shape, dtype))

    def ps(name, shape, dtype=F32):
        return pa.enter_context(nc.psum_tensor("a%d_%s" % (li, name), shape, dtype))

    _pass_barrier(E)

    wqkv_bf = sb("wqkv", [128, 8 * 1536], BF16)
    wqkv_lo = sb("wqkv_lo", [128, 8 * 1536], BF16)
    wo_bf = sb("wo", [128, 4 * 1024], BF16)
    wstage = sb("wstage", [128, 2 * 1536], F32)
    tq1 = sb("tq1", [128, 32 * 64], F32)
    tq2 = sb("tq2", [128, 32 * 64], F32)
    tk1 = sb("tk1", [128, 32 * 64], F32)
    tk2 = sb("tk2", [128, 32 * 64], F32)
    qs_sb = sb("qs", [1, 64], F32)
    ones_f = sb("ones_f", [1, 128], F32)
    ks_sb = sb("ks", [1, 64], F32)
    sc_mat = sb("sc_mat", [128, 4 * 64], F32)
    kfm_roll = sb("kroll", [128, 5 * 512], BF16)
    v_roll = sb("vroll", [128, 5 * 512], BF16)
    x_tm = sb("x_tm", [128, 2 * D], F32)
    x_bf = sb("x_bf", [128, 2 * D], BF16)
    x_fm = sb("x_fm", [128, 2 * 8 * 129], BF16)
    q1b = sb("q1", [128, 512], F32)
    k1b = sb("k1", [128, 512], F32)
    rta = sb("rta", [128, 512], F32)
    rtb = sb("rtb", [128, 512], F32)
    q2b = sb("q2", [128, 512], BF16)
    k2b = sb("k2", [128, 512], BF16)
    q_fmc = sb("qfmc", [128, 512], BF16)
    attn_f = sb("attnf", [128, 2 * 640], F32)
    attn_sb = sb("attn", [128, 2 * 640], BF16)
    attn_fm = sb("attnfm", [128, 2 * 640], BF16)
    mcat = sb("mcat", [128, 256], F32)
    o_fm = sb("ofm", [128, 512], BF16)
    x_out = sb("x_out", [128, 2 * D], F32)
    stt = sb("stt", [128, 24], F32)
    ident_f = sb("ident_f", [128, 128], F32)
    q2f = sb("q2f", [128, 512], F32)
    k2f = sb("k2f", [128, 512], F32)
    qfm_f = sb("qfm_f", [128, 512], F32)
    kf_roll = sb("kf_roll", [128, 2 * 512], F32)
    vf_roll = sb("vf_roll", [128, 2 * 512], F32)
    attn_nf = sb("attn_nf", [128, 256], F32)
    afm_f = sb("afm_f", [128, 256], F32)
    xlo_bf = sb("xlo_bf", [128, D], BF16)
    xlo_f = sb("xlo_f", [128, D], F32)
    x_fm_lo = sb("x_fm_lo", [128, 2 * 8 * 129], BF16)

    ps_pool = ps("pool", [128, 2048], F32)
    ps_big = ps_pool[:, 0:1536]
    ps_b = ps("trp", [128, 1024], BF16)
    ps_o = ps("o", [64, 1024], F32)
    ps_f = ps("f32aux", [128, 512], F32)
    ps_qm = ps_f[:, 0:64]
    ps_tf = ps_f

    base_po = PO.n
    for i in range(8):
        slot = i % 2
        if i >= 2:
            SY.wait_ge(PO.sem, base_po + 2 * (i - 1))
        else:
            SY.wait_ge(PO.sem, base_po)
        LD.inc(SY.dma_start(out=wstage[:, slot * 1536:(slot + 1) * 1536],
                            in_=E.f_wqkv[li, i * 128:(i + 1) * 128, :]), 16)
        GP.wait_ge(LD.sem, LD.n)
        PO.inc(GP.tensor_copy(wqkv_bf[:, i * 1536:(i + 1) * 1536],
                              wstage[:, slot * 1536:(slot + 1) * 1536]))
        GP.wait_ge(PO.sem, PO.n)
        PO.inc(GP.tensor_tensor(out=wqkv_lo[:, i * 1536:(i + 1) * 1536],
                                in0=wstage[:, slot * 1536:(slot + 1) * 1536],
                                in1=wqkv_bf[:, i * 1536:(i + 1) * 1536],
                                op=ALU.subtract))
    w_ready = PO.n
    pairs = [(E.f_wo[li, i * 128:(i + 1) * 128, :], wo_bf[:, i * 1024:(i + 1) * 1024])
             for i in range(4)]
    w_ready = _load_weights_bf16(E, pairs, wstage, stage_cols=1024)

    for tbl_d, tbl in ((E.t1q_d, tq1), (E.t2q_d, tq2), (E.t1k_d, tk1), (E.t2k_d, tk2)):
        LD.inc(SY.dma_start(out=tbl[:].rearrange("p (c d) -> p c d", d=64),
                            in_=tbl_d.rearrange("(c p) d -> p c d", p=128)), 16)
    LD.inc(SY.dma_start(out=qs_sb[:], in_=E.f_qs[li:li + 1, :]), 16)
    LD.inc(SY.dma_start(out=ks_sb[:], in_=E.f_ks[li:li + 1, :]), 16)
    tbl_ld = LD.n
    V.wait_ge(LD.sem, E.consts_ld)
    V.memset(ones_f[:], 1.0)
    V.tensor_copy(mcat[:, 0:128], E.mge_sb[:])
    V.tensor_copy(mcat[:, 128:256], E.mle_sb[:])
    DV.inc(V.tensor_copy(ident_f[:], E.ident_sb[:]))
    ones_dv = DV.n
    TE.wait_ge(LD.sem, tbl_ld)
    TE.wait_ge(DV.sem, ones_dv)
    for idx, (row, swap) in enumerate(((qs_sb, False), (qs_sb, True),
                                      (ks_sb, False), (ks_sb, True))):
        rhs = row[0:1, :]
        if swap:
            rhs = row[0:1, :].rearrange("p (two d) -> p two d", two=2)[:, ::-1, :]
        if idx > 0:
            TE.wait_ge(DV.sem, DV.n)
        PE.inc(TE.matmul(ps_qm[:], ones_f[0:1, :], rhs, start=True, stop=True))
        V.wait_ge(PE.sem, PE.n)
        DV.inc(V.tensor_copy(sc_mat[:, idx * 64:(idx + 1) * 64], ps_qm[:]))
    V.wait_ge(DV.sem, DV.n)
    for tbl, mi in ((tq1, 0), (tq2, 1), (tk1, 2), (tk2, 3)):
        DV.inc(V.tensor_tensor(
            out=tbl[:].rearrange("p (c d) -> p c d", d=64),
            in0=tbl[:].rearrange("p (c d) -> p c d", d=64),
            in1=sc_mat[:, mi * 64:(mi + 1) * 64][:, None, :].broadcast_to([128, 32, 64]),
            op=ALU.mult))
    tbl_dv = DV.n

    evac = {}
    rel = {}
    first_in_bank = {"o0": True, "o1": True}
    prev_heads_pe = 0

    for qt in range(NT):
        slot = qt % 2
        kts = list(range(max(0, qt - 4), qt + 1))
        nkt = len(kts)
        w = nkt * 128

        if ("xtm", slot) in rel:
            SY.wait_ge(DV.sem, rel[("xtm", slot)])
        LD.inc(SY.dma_start(out=x_tm[:, slot * D:slot * D + D],
                            in_=E.xA[qt * 128:(qt + 1) * 128, :]), 16)
        x_ld = LD.n

        V.wait_ge(LD.sem, x_ld)
        if ("xbf", slot) in rel:
            V.wait_ge(PE.sem, rel[("xbf", slot)])
        DV.inc(V.tensor_copy(x_bf[:, slot * D:slot * D + D],
                             x_tm[:, slot * D:slot * D + D]))
        xbf_dv = DV.n

        if qt < 2:
            V.wait_ge(DV.sem, xbf_dv)
            DV.inc(V.tensor_sub(xlo_f[:], x_tm[:, slot * D:slot * D + D],
                                x_bf[:, slot * D:slot * D + D]))
            V.wait_ge(DV.sem, DV.n)
            DV.inc(V.tensor_copy(xlo_bf[:], xlo_f[:]))
            xbf_dv = DV.n
        TE.wait_ge(DV.sem, xbf_dv)
        if qt == 0:
            TE.wait_ge(PO.sem, w_ready)
            TE.wait_ge(DV.sem, tbl_dv)
        if "ps_b" in evac:
            TE.wait_ge(DV.sem, evac["ps_b"])
        for ft in range(8):
            mm = TE.transpose(ps_b[:, ft * 128:(ft + 1) * 128],
                              x_bf[:, slot * D + ft * 128: slot * D + (ft + 1) * 128],
                              E.ident_sb[:])
            if ft == 7:
                PE.inc(mm)
        rel[("xbf", slot)] = PE.n
        V.wait_ge(PE.sem, PE.n)
        if ("xfm", slot) in rel:
            V.wait_ge(PE.sem, rel[("xfm", slot)])
        base = slot * 1032
        pbase = (1 - slot) * 1032
        DV.inc(V.tensor_copy(
            x_fm[:, base:base + 1032].rearrange("p (f t) -> p f t", t=129)[:, :, 1:129],
            ps_b[:].rearrange("p (f t) -> p f t", t=128)))
        if qt == 0:
            V.memset(x_fm[:, base:base + 1032]
                     .rearrange("p (f t) -> p f t", t=129)[:, :, 0:1], 0.0)
            DV.inc(V.tensor_copy(stt[:, 23:24], stt[:, 23:24]))
        else:
            DV.inc(V.tensor_copy(
                x_fm[:, base:base + 1032].rearrange("p (f t) -> p f t", t=129)[:, :, 0:1],
                x_fm[:, pbase:pbase + 1032].rearrange("p (f t) -> p f t", t=129)[:, :, 128:129]))
        if qt < 2:
            TE.wait_ge(DV.sem, DV.n)
            for ft in range(8):
                mm = TE.transpose(ps_b[:, ft * 128:(ft + 1) * 128],
                                  xlo_bf[:, ft * 128:(ft + 1) * 128], E.ident_sb[:])
                if ft == 7:
                    PE.inc(mm)
            V.wait_ge(PE.sem, PE.n)
            DV.inc(V.tensor_copy(
                x_fm_lo[:, base:base + 1032].rearrange("p (f t) -> p f t", t=129)[:, :, 1:129],
                ps_b[:].rearrange("p (f t) -> p f t", t=128)))
            evac["ps_b"] = DV.n
            if qt == 0:
                DV.inc(V.memset(x_fm_lo[:, base:base + 1032]
                                .rearrange("p (f t) -> p f t", t=129)[:, :, 0:1], 0.0))
            else:
                DV.inc(V.tensor_copy(
                    x_fm_lo[:, base:base + 1032].rearrange("p (f t) -> p f t", t=129)[:, :, 0:1],
                    x_fm_lo[:, pbase:pbase + 1032].rearrange("p (f t) -> p f t", t=129)[:, :, 128:129]))
        xfm_dv = DV.n

        TE.wait_ge(DV.sem, xfm_dv)
        if ("big",) in evac:
            TE.wait_ge(DV.sem, evac[("big",)])
        if ("exp_last",) in evac:
            TE.wait_ge(AC.sem, evac[("exp_last",)])
        import os
        hl = int(os.environ.get("HILO", "3"))
        qkv_terms = [(x_fm, wqkv_bf)]
        if qt < 2:
            if hl & 1:
                qkv_terms += [(x_fm, wqkv_lo)]
            if hl & 2:
                qkv_terms += [(x_fm_lo, wqkv_bf)]
        nterm = len(qkv_terms)
        for ti, (xsrc, wsrc) in enumerate(qkv_terms):
            for kt in range(8):
                if kt < 4:
                    lhsT = xsrc[:, base + kt * 129 + 1: base + kt * 129 + 129]
                else:
                    lhsT = xsrc[:, base + kt * 129 + 0: base + kt * 129 + 128]
                for nh in range(3):
                    mm = TE.matmul(ps_big[:, nh * 512:(nh + 1) * 512],
                                   lhsT,
                                   wsrc[:, kt * 1536 + nh * 512: kt * 1536 + (nh + 1) * 512],
                                   start=(ti == 0 and kt == 0),
                                   stop=(ti == nterm - 1 and kt == 7))
                    if ti == nterm - 1 and kt == 7 and nh == 2:
                        PE.inc(mm)
        rel[("xfm", slot)] = PE.n
        qkv_pe = PE.n

        SC.wait_ge(PE.sem, qkv_pe)
        AC.inc(SC.activation(out=rta[:], in_=ps_big[:, 0:512], func=AF.Square))
        AC.inc(SC.activation(out=rtb[:], in_=ps_big[:, 512:1024], func=AF.Square))
        sq_ac = AC.n
        V.wait_ge(AC.sem, sq_ac)
        DV.inc(V.reduce_sum(out=stt[:, 0:8],
                            in_=rta[:].rearrange("p (h d) -> p h d", d=64), axis=AX.X))
        DV.inc(V.reduce_sum(out=stt[:, 8:16],
                            in_=rtb[:].rearrange("p (h d) -> p h d", d=64), axis=AX.X))
        V.wait_ge(DV.sem, DV.n)
        SC.wait_ge(DV.sem, DV.n)
        AC.inc(SC.activation(out=rta[:, 48:64], in_=stt[:, 0:16], func=AF.Ln,
                             bias=LNEPS))
        SC.wait_ge(AC.sem, AC.n)
        AC.inc(SC.activation(out=rta[:, 16:32], in_=rta[:, 48:64], func=AF.Exp,
                             scale=-0.5))
        V.wait_ge(AC.sem, AC.n)
        DV.inc(V.tensor_mul(rta[:, 32:48], rta[:, 16:32], rta[:, 16:32]))
        V.wait_ge(DV.sem, DV.n)
        DV.inc(V.tensor_mul(rta[:, 32:48], rta[:, 32:48], stt[:, 0:16]))
        V.wait_ge(DV.sem, DV.n)
        DV.inc(V.tensor_scalar_mul(rta[:, 32:48], rta[:, 32:48], -0.5))
        V.wait_ge(DV.sem, DV.n)
        DV.inc(V.tensor_scalar_add(rta[:, 32:48], rta[:, 32:48], 1.5))
        V.wait_ge(DV.sem, DV.n)
        DV.inc(V.tensor_mul(stt[:, 0:16], rta[:, 16:32], rta[:, 32:48]))
        V.wait_ge(DV.sem, DV.n)
        DV.inc(V.tensor_tensor(
            out=q1b[:].rearrange("p (h d) -> p h d", d=64),
            in0=ps_big[:, 0:512].rearrange("p (h d) -> p h d", d=64),
            in1=stt[:, 0:8][:, :, None].broadcast_to([128, 8, 64]), op=ALU.mult))
        DV.inc(V.tensor_tensor(
            out=k1b[:].rearrange("p (h d) -> p h d", d=64),
            in0=ps_big[:, 512:1024].rearrange("p (h d) -> p h d", d=64),
            in1=stt[:, 8:16][:, :, None].broadcast_to([128, 8, 64]), op=ALU.mult))
        vslot = qt % 5
        if prev_heads_pe:
            V.wait_ge(PE.sem, prev_heads_pe)
        DV.inc(V.tensor_copy(v_roll[:, vslot * 512:(vslot + 1) * 512],
                             ps_big[:, 1024:1536]))
        V.wait_ge(DV.sem, DV.n)
        ttile_q1 = tq1[:, qt * 64:(qt + 1) * 64][:, None, :].broadcast_to([128, 8, 64])
        ttile_q2 = tq2[:, qt * 64:(qt + 1) * 64][:, None, :].broadcast_to([128, 8, 64])
        ttile_k1 = tk1[:, qt * 64:(qt + 1) * 64][:, None, :].broadcast_to([128, 8, 64])
        ttile_k2 = tk2[:, qt * 64:(qt + 1) * 64][:, None, :].broadcast_to([128, 8, 64])
        DV.inc(V.tensor_tensor(out=rta[:].rearrange("p (h d) -> p h d", d=64),
                               in0=q1b[:].rearrange("p (h d) -> p h d", d=64),
                               in1=ttile_q1, op=ALU.mult))
        DV.inc(V.tensor_tensor(
            out=rtb[:].rearrange("p (h d) -> p h d", d=64),
            in0=q1b[:].rearrange("p (h two s) -> p h two s", two=2, s=32)[:, :, ::-1, :]
                .rearrange("p h two s -> p h (two s)"),
            in1=ttile_q2, op=ALU.mult)) if False else DV.inc(V.tensor_tensor(
            out=rtb[:].rearrange("p (h two s) -> p h two s", two=2, s=32),
            in0=q1b[:].rearrange("p (h two s) -> p h two s", two=2, s=32)[:, :, ::-1, :],
            in1=ttile_q2.rearrange("p h (two s) -> p h two s", two=2, s=32), op=ALU.mult))
        V.wait_ge(DV.sem, DV.n)
        DV.inc(V.tensor_add(q2b[:], rta[:], rtb[:]))
        if qt < 2:
            DV.inc(V.tensor_add(q2f[:], rta[:], rtb[:]))
        DV.inc(V.tensor_tensor(out=rta[:].rearrange("p (h d) -> p h d", d=64),
                               in0=k1b[:].rearrange("p (h d) -> p h d", d=64),
                               in1=ttile_k1, op=ALU.mult))
        DV.inc(V.tensor_tensor(
            out=rtb[:].rearrange("p (h two s) -> p h two s", two=2, s=32),
            in0=k1b[:].rearrange("p (h two s) -> p h two s", two=2, s=32)[:, :, ::-1, :],
            in1=ttile_k2.rearrange("p h (two s) -> p h two s", two=2, s=32), op=ALU.mult))
        V.wait_ge(DV.sem, DV.n)
        DV.inc(V.tensor_add(k2b[:], rta[:], rtb[:]))
        if qt < 2:
            DV.inc(V.tensor_add(k2f[:], rta[:], rtb[:]))
            DV.inc(V.tensor_copy(vf_roll[:, vslot * 512:(vslot + 1) * 512],
                                 ps_big[:, 1024:1536]))
        qk_dv = DV.n

        if qt == 0 and getattr(E, "a_dbg", False):
            V.wait_ge(DV.sem, qk_dv)
            V.wait_ge(ST.sem, ST.n)
            DV.inc(V.tensor_copy(x_out[:, 0:256], sc_mat[:]))
            DV.inc(V.tensor_copy(x_out[0:1, 256:320], qs_sb[:]))
            DV.inc(V.tensor_copy(x_out[:, 320:384], tq1[:, 0:64]))
            GP.wait_ge(DV.sem, DV.n)
            ST.inc(GP.dma_start(out=E.out[15 * 128:16 * 128, 0:384], in_=x_out[:, 0:384]), 16)
            V.wait_ge(ST.sem, ST.n)
            for bi, buf in enumerate((q1b, k1b, q2b, k2b)):
                DV.inc(V.tensor_copy(x_out[:, 0:512], buf[:]))
                GP.wait_ge(DV.sem, DV.n)
                ST.inc(GP.dma_start(out=E.out[(16 + bi) * 128:(17 + bi) * 128, 0:512],
                                    in_=x_out[:, 0:512]), 16)
                V.wait_ge(ST.sem, ST.n)
        TE.wait_ge(DV.sem, qk_dv)
        TE.wait_ge(DV.sem, evac["ps_b"]) if "ps_b" in evac else None
        for ft in range(4):
            TE.transpose(ps_b[:, ft * 128:(ft + 1) * 128],
                         q2b[:, ft * 128:(ft + 1) * 128], E.ident_sb[:])
        for ft in range(4):
            mm = TE.transpose(ps_b[:, 512 + ft * 128: 512 + (ft + 1) * 128],
                              k2b[:, ft * 128:(ft + 1) * 128], E.ident_sb[:])
            if ft == 3:
                PE.inc(mm)
        qkT_pe = PE.n
        V.wait_ge(PE.sem, qkT_pe)
        if prev_heads_pe:
            V.wait_ge(PE.sem, prev_heads_pe)
        DV.inc(V.tensor_copy(q_fmc[:], ps_b[:, 0:512]))
        DV.inc(V.tensor_copy(kfm_roll[:, vslot * 512:(vslot + 1) * 512],
                             ps_b[:, 512:1024]))
        evac["ps_b"] = DV.n
        if qt < 2:
            if ("tf",) in evac:
                TE.wait_ge(DV.sem, evac[("tf",)])
            for ft in range(4):
                mm = TE.transpose(ps_tf[:, ft * 128:(ft + 1) * 128],
                                  q2f[:, ft * 128:(ft + 1) * 128], ident_f[:])
                if ft == 3:
                    PE.inc(mm)
            V.wait_ge(PE.sem, PE.n)
            DV.inc(V.tensor_copy(qfm_f[:], ps_tf[:]))
            TE.wait_ge(DV.sem, DV.n)
            for ft in range(4):
                mm = TE.transpose(ps_tf[:, ft * 128:(ft + 1) * 128],
                                  k2f[:, ft * 128:(ft + 1) * 128], ident_f[:])
                if ft == 3:
                    PE.inc(mm)
            V.wait_ge(PE.sem, PE.n)
            DV.inc(V.tensor_copy(kf_roll[:, qt * 512:(qt + 1) * 512], ps_tf[:]))
            evac[("tf",)] = DV.n
        qkT_dv = DV.n

        sim_off = [0, 1024]
        dv_after_stats = qkT_dv
        exp_acs = {}
        for h in range(HB if qt < 2 else 0):
            hb = (h % 2) * 64
            hf = (h // 2) * 128
            s = h % 2
            so = sim_off[s]
            fp32_tile = qt < 2
            TE.wait_ge(DV.sem, dv_after_stats)
            if h == 0:
                TE.wait_ge(AC.sem, sq_ac)
            if h >= 2:
                TE.wait_ge(AC.sem, exp_acs[h - 2])
            for ki, ktile in enumerate(kts):
                ks_ = ktile % 5
                if fp32_tile:
                    mm = TE.matmul(ps_pool[:, so + ki * 128: so + (ki + 1) * 128],
                                   qfm_f[hb:hb + 64, hf:hf + 128],
                                   kf_roll[hb:hb + 64, ktile * 512 + hf: ktile * 512 + hf + 128],
                                   start=True, stop=True)
                else:
                    mm = TE.matmul(ps_pool[:, so + ki * 128: so + (ki + 1) * 128],
                                   q_fmc[hb:hb + 64, hf:hf + 128],
                                   kfm_roll[hb:hb + 64, ks_ * 512 + hf: ks_ * 512 + hf + 128],
                                   start=True, stop=True)
                if ki == nkt - 1:
                    PE.inc(mm)
            sim_pe = PE.n
            V.wait_ge(PE.sem, sim_pe)
            if qt >= 4:
                DV.inc(V.tensor_tensor(
                    out=ps_pool[:, so:so + 640].rearrange("p (b j) -> p b j", j=128)[:, ::4, :],
                    in0=ps_pool[:, so:so + 640].rearrange("p (b j) -> p b j", j=128)[:, ::4, :],
                    in1=mcat[:].rearrange("p (b j) -> p b j", j=128),
                    op=ALU.add))
            else:
                DV.inc(V.tensor_add(ps_pool[:, so + (nkt - 1) * 128: so + nkt * 128],
                                    ps_pool[:, so + (nkt - 1) * 128: so + nkt * 128],
                                    E.mle_sb[:]))
            DV.inc(V.reduce_max(out=stt[:, 16 + s:17 + s], in_=ps_pool[:, so:so + w],
                                axis=AX.X, negate=True))
            mask_dv = DV.n
            SC.wait_ge(DV.sem, mask_dv)
            if ("attnf", s) in rel:
                SC.wait_ge(DV.sem, rel[("attnf", s)])
            AC.inc(SC.activation(out=attn_f[:, s * 640: s * 640 + w],
                                 in_=ps_pool[:, so:so + w], func=AF.Exp,
                                 bias=stt[:, 16 + s:17 + s],
                                 accum_out=stt[:, 18 + s:19 + s]))
            exp_acs[h] = AC.n
            exp_ac = AC.n
            V.wait_ge(AC.sem, exp_ac)
            DV.inc(V.reciprocal(stt[:, 20 + s:21 + s], stt[:, 18 + s:19 + s]))
            rec_dv = DV.n
            V.wait_ge(DV.sem, rec_dv)
            if ("tsp", s) in rel:
                V.wait_ge(PE.sem, rel[("tsp", s)])
            if fp32_tile and ("tsp", 1 - s) in rel:
                V.wait_ge(PE.sem, rel[("tsp", 1 - s)])
            DV.inc(V.tensor_scalar_mul(
                attn_sb[:, s * 640: s * 640 + w] if not fp32_tile
                else attn_nf[:, 0:w],
                attn_f[:, s * 640: s * 640 + w], stt[:, 20 + s:21 + s]))
            tsp_po = DV.n
            rel[("attnf", s)] = tsp_po
            TE.wait_ge(DV.sem, tsp_po)
            if fp32_tile:
                if ("tf",) in evac:
                    TE.wait_ge(DV.sem, evac[("tf",)])
                for ki in range(nkt):
                    mm = TE.transpose(ps_tf[:, ki * 128:(ki + 1) * 128],
                                      attn_nf[:, ki * 128:(ki + 1) * 128], ident_f[:])
                    if ki == nkt - 1:
                        PE.inc(mm)
                rel[("tsp", s)] = PE.n
                V.wait_ge(PE.sem, PE.n)
                DV.inc(V.tensor_copy(afm_f[:, 0:w], ps_tf[:, 0:w]))
                evac[("tf",)] = DV.n
                afm_dv = DV.n
            else:
                if ("ps_b2",) in evac:
                    TE.wait_ge(DV.sem, evac[("ps_b2",)])
                for ki in range(nkt):
                    mm = TE.transpose(ps_b[:, ki * 128:(ki + 1) * 128],
                                      attn_sb[:, s * 640 + ki * 128: s * 640 + (ki + 1) * 128],
                                      E.ident_sb[:])
                    if ki == nkt - 1:
                        PE.inc(mm)
                rel[("tsp", s)] = PE.n
                V.wait_ge(PE.sem, PE.n)
                DV.inc(V.tensor_copy(attn_fm[:, s * 640: s * 640 + w], ps_b[:, 0:w]))
                evac[("ps_b2",)] = DV.n
                evac["ps_b"] = DV.n
                afm_dv = DV.n
            TE.wait_ge(DV.sem, afm_dv)
            if h == 0 and ("oev",) in evac:
                TE.wait_ge(DV.sem, evac[("oev",)])
            for ki, ktile in enumerate(kts):
                ks_ = ktile % 5
                if fp32_tile:
                    mm = TE.matmul(ps_o[0:64, h * 128:(h + 1) * 128],
                                   vf_roll[:, ktile * 512 + h * 64: ktile * 512 + h * 64 + 64],
                                   afm_f[:, ki * 128:(ki + 1) * 128],
                                   start=(ki == 0 and h % 4 == 0), stop=(ki == nkt - 1))
                else:
                    mm = TE.matmul(ps_o[0:64, h * 128:(h + 1) * 128],
                                   v_roll[:, ks_ * 512 + h * 64: ks_ * 512 + h * 64 + 64],
                                   attn_fm[:, s * 640 + ki * 128: s * 640 + (ki + 1) * 128],
                                   start=(ki == 0 and h % 4 == 0), stop=(ki == nkt - 1))
                if ki == nkt - 1:
                    PE.inc(mm)
        if qt >= 2:
            sim_pe = {}
            av_pe = {}
            for g in range(HB + 1):
                if g < HB:
                    h = g
                    hb = (h % 2) * 64
                    hf = (h // 2) * 128
                    s = h % 2
                    so = sim_off[s]
                    if g < 2:
                        TE.wait_ge(DV.sem, dv_after_stats)
                    else:
                        TE.wait_ge(AC.sem, exp_acs[g - 2])
                    for ki, ktile in enumerate(kts):
                        ks_ = ktile % 5
                        mm = TE.matmul(
                            ps_pool[:, so + ki * 128: so + (ki + 1) * 128],
                            q_fmc[hb:hb + 64, hf:hf + 128],
                            kfm_roll[hb:hb + 64, ks_ * 512 + hf: ks_ * 512 + hf + 128],
                            start=True, stop=True)
                        if ki == nkt - 1:
                            PE.inc(mm)
                    sim_pe[g] = PE.n
                if g >= 1:
                    j = g - 1
                    sj = j % 2
                    soj = sim_off[sj]
                    V.wait_ge(AC.sem, exp_acs[j])
                    DV.inc(V.reciprocal(stt[:, 20 + sj:21 + sj],
                                        stt[:, 18 + sj:19 + sj]))
                    V.wait_ge(DV.sem, DV.n)
                    if ("tsp", sj) in rel:
                        V.wait_ge(PE.sem, rel[("tsp", sj)])
                    DV.inc(V.tensor_scalar_mul(
                        attn_sb[:, sj * 640: sj * 640 + w],
                        attn_f[:, sj * 640: sj * 640 + w],
                        stt[:, 20 + sj:21 + sj]))
                    rel[("attnf", sj)] = DV.n
                    tsp_dv = DV.n
                if g < HB:
                    V.wait_ge(PE.sem, sim_pe[g])
                    if qt >= 4:
                        DV.inc(V.tensor_tensor(
                            out=ps_pool[:, so:so + 640]
                                .rearrange("p (b j) -> p b j", j=128)[:, ::4, :],
                            in0=ps_pool[:, so:so + 640]
                                .rearrange("p (b j) -> p b j", j=128)[:, ::4, :],
                            in1=mcat[:].rearrange("p (b j) -> p b j", j=128),
                            op=ALU.add))
                    else:
                        DV.inc(V.tensor_add(
                            ps_pool[:, so + (nkt - 1) * 128: so + nkt * 128],
                            ps_pool[:, so + (nkt - 1) * 128: so + nkt * 128],
                            E.mle_sb[:]))
                    DV.inc(V.reduce_max(out=stt[:, 16 + s:17 + s],
                                        in_=ps_pool[:, so:so + w],
                                        axis=AX.X, negate=True))
                    mask_dv = DV.n
                    SC.wait_ge(DV.sem, mask_dv)
                    if ("attnf", s) in rel:
                        SC.wait_ge(DV.sem, rel[("attnf", s)])
                    AC.inc(SC.activation(out=attn_f[:, s * 640: s * 640 + w],
                                         in_=ps_pool[:, so:so + w], func=AF.Exp,
                                         bias=stt[:, 16 + s:17 + s],
                                         accum_out=stt[:, 18 + s:19 + s]))
                    exp_acs[g] = AC.n
                if g >= 1:
                    j = g - 1
                    sj = j % 2
                    TE.wait_ge(DV.sem, tsp_dv)
                    if ("ps_b2",) in evac:
                        TE.wait_ge(DV.sem, evac[("ps_b2",)])
                    for ki in range(nkt):
                        mm = TE.transpose(
                            ps_b[:, ki * 128:(ki + 1) * 128],
                            attn_sb[:, sj * 640 + ki * 128: sj * 640 + (ki + 1) * 128],
                            E.ident_sb[:])
                        if ki == nkt - 1:
                            PE.inc(mm)
                    rel[("tsp", sj)] = PE.n
                    V.wait_ge(PE.sem, PE.n)
                    if ("afm", sj) in rel:
                        V.wait_ge(PE.sem, rel[("afm", sj)])
                    DV.inc(V.tensor_copy(attn_fm[:, sj * 640: sj * 640 + w],
                                         ps_b[:, 0:w]))
                    evac[("ps_b2",)] = DV.n
                    evac["ps_b"] = DV.n
                    afm_dv = DV.n
                    TE.wait_ge(DV.sem, afm_dv)
                    if j == 0 and ("oev",) in evac:
                        TE.wait_ge(DV.sem, evac[("oev",)])
                    for ki, ktile in enumerate(kts):
                        ks_ = ktile % 5
                        mm = TE.matmul(
                            ps_o[0:64, j * 128:(j + 1) * 128],
                            v_roll[:, ks_ * 512 + j * 64: ks_ * 512 + j * 64 + 64],
                            attn_fm[:, sj * 640 + ki * 128: sj * 640 + (ki + 1) * 128],
                            start=(ki == 0 and j % 4 == 0), stop=(ki == nkt - 1))
                        if ki == nkt - 1:
                            PE.inc(mm)
                    av_pe[j] = PE.n
                    rel[("afm", sj)] = PE.n
        V.wait_ge(PE.sem, PE.n)
        if ("ofm",) in rel:
            V.wait_ge(PE.sem, rel[("ofm",)])
        DV.inc(V.tensor_copy(
            o_fm[0:64, :].rearrange("p (g d) -> p g d", d=128),
            ps_o[0:64, :].rearrange("p (h two d) -> p h two d",
                                    two=2, d=128)[:, :, 0, :]))
        DV.inc(V.tensor_copy(
            o_fm[64:128, :].rearrange("p (g d) -> p g d", d=128),
            ps_o[0:64, :].rearrange("p (h two d) -> p h two d",
                                    two=2, d=128)[:, :, 1, :]))
        evac[("oev",)] = DV.n
        o_dv = DV.n

        prev_heads_pe = PE.n
        TE.wait_ge(DV.sem, o_dv)
        TE.wait_ge(AC.sem, exp_acs[HB - 1])
        for ot in range(4):
            for nh in range(2):
                mm = TE.matmul(ps_big[:, nh * 512:(nh + 1) * 512],
                               o_fm[:, ot * 128:(ot + 1) * 128],
                               wo_bf[:, ot * 1024 + nh * 512: ot * 1024 + (nh + 1) * 512],
                               start=(ot == 0), stop=(ot == 3))
                if ot == 3 and nh == 1:
                    PE.inc(mm)
        wo_pe = PE.n
        rel[("ofm",)] = wo_pe
        V.wait_ge(PE.sem, wo_pe)
        if ("xout", slot) in rel:
            V.wait_ge(ST.sem, rel[("xout", slot)])
        DV.inc(V.tensor_add(x_out[:, slot * D:slot * D + D],
                            x_tm[:, slot * D:slot * D + D], ps_big[:, 0:1024]))
        evac[("big",)] = DV.n
        rel[("xtm", slot)] = DV.n
        evac[("exp_last",)] = exp_acs[HB - 1]
        xo_dv = DV.n
        GP.wait_ge(DV.sem, xo_dv)
        ST.inc(GP.dma_start(out=E.xA[qt * 128:(qt + 1) * 128, :],
                            in_=x_out[:, slot * D:slot * D + D]), 16)
        rel[("xout", slot)] = ST.n
    pa.close()


def _emit_B(E, li):
    nc = E.nc
    V, SC, TE, SY, GP = E.V, E.SC, E.TE, E.SY, E.GP
    LD, ST, PE, DV, AC, PO = E.LD, E.ST, E.PE, E.DV, E.AC, E.PO
    H2 = 2 * HID

    pb = ExitStack()

    def sb(name, shape, dtype):
        return pb.enter_context(nc.sbuf_tensor("b%d_%s" % (li, name), shape, dtype))

    def ps(name, shape, dtype=F32):
        return pb.enter_context(nc.psum_tensor("b%d_%s" % (li, name), shape, dtype))

    _pass_barrier(E)

    w1g = sb("w1g", [128, 8 * H2], BF16)
    w2b = sb("w2b", [128, 22 * D], BF16)
    wstage = sb("wstage", [128, 2 * 1024], F32)
    g_cols = sb("gcols", [128, 8], F32)
    b_cols = sb("bcols", [128, 8], F32)
    b_cols_bf = sb("bcolsbf", [128, 8], BF16)
    bias_a = sb("biasa", [128, 22], F32)
    bias_g = sb("biasg", [128, 22], F32)
    b1_cols = sb("b1cols", [128, 44], F32)
    ones_f = sb("ones_f", [1, 128], F32)
    x_tm = sb("x_tm", [128, 2 * D], F32)
    x_bf = sb("x_bf", [128, D], BF16)
    x_fm = sb("x_fm", [128, 8 * 513], BF16)
    prevcol = sb("prevcol", [128, 8], BF16)
    h_ln = sb("h_ln", [128, 8 * 512], BF16)
    sqscr = sb("sqscr", [128, 512], BF16)
    rowbuf = sb("rowbuf", [1, 3 * 512], F32)
    mu_b = sb("mu_b", [128, 512], F32)
    r_b = sb("r_b", [128, 512], F32)
    t1buf = sb("t1", [128, 512], F32)
    gelu_sb = sb("gelu", [128, 2 * 512], BF16)
    hc = sb("hc", [128, 22 * 512], BF16)
    x_out = sb("x_out", [128, D], F32)

    ps_za = [ps("za0", [128, 512], F32), ps("za1", [128, 512], F32)]
    ps_zg = [ps("zg0", [128, 512], F32), ps("zg1", [128, 512], F32)]
    ps_y = ps("y", [128, 1024], F32)
    ps_b = ps("trp", [128, 1024], BF16)
    ps_row = ps_y[0:1, 0:1024]
    ps_bc = ps_za[0]

    fold6 = (li == 1) and getattr(E, "fold_p6", False)
    if fold6:
        x1_in = wstage[:, 0:D]
        o6 = wstage[:, D:2 * D]
        sq6 = gelu_sb
        stt6 = sb("stt6", [128, 8], F32)
    prev_sq6 = 0
    prev_o6_ac = 0
    x1_ld = 0

    DV.inc(V.memset(ones_f[:], 1.0))
    V.memset(hc[32:64, 21 * 512:22 * 512], 0.0)
    V.memset(hc[64:128, 21 * 512:22 * 512], 0.0)
    DV.inc(V.memset(hc[64:65, 21 * 512:22 * 512], 1.0))
    hcinit_dv = DV.n

    for kt in range(8):
        LD.inc(SY.dma_start(out=g_cols[:, kt:kt + 1],
                            in_=E.f_g[li, kt * 128:(kt + 1) * 128][:, None]), 16)
        LD.inc(SY.dma_start(out=b_cols[:, kt:kt + 1],
                            in_=E.f_b[li, kt * 128:(kt + 1) * 128][:, None]), 16)
    for jc in range(22):
        wa = 128 if jc < 21 else HID - 21 * 128
        LD.inc(SY.dma_start(out=b1_cols[0:wa, jc:jc + 1],
                            in_=E.f_b1[li, jc * 128: jc * 128 + wa][:, None]), 16)
        LD.inc(SY.dma_start(out=b1_cols[0:wa, 22 + jc:23 + jc],
                            in_=E.f_b1[li, HID + jc * 128: HID + jc * 128 + wa][:, None]), 16)
    gcol_ld = LD.n

    base_po = PO.n
    chunks1 = [(q * 1024, min(1024, H2 - q * 1024)) for q in range(6)]
    i = 0
    for kt in range(8):
        for (j0, wd) in chunks1:
            slot = i % 2
            if i >= 2:
                SY.wait_ge(PO.sem, base_po + i - 1)
            elif i < 2:
                SY.wait_ge(PO.sem, base_po)
            LD.inc(SY.dma_start(out=wstage[:, slot * 1024: slot * 1024 + wd],
                                in_=E.f_w1[li, kt * 128:(kt + 1) * 128, j0:j0 + wd]), 16)
            GP.wait_ge(LD.sem, LD.n)
            if i == 0:
                GP.wait_ge(LD.sem, gcol_ld)
            PO.inc(GP.tensor_copy(w1g[:, kt * H2 + j0: kt * H2 + j0 + wd],
                                   wstage[:, slot * 1024: slot * 1024 + wd]))
            i += 1
    base_po2 = PO.n
    for ct in range(22):
        slot = ct % 2
        wr = 128 if ct < 21 else HID - 21 * 128
        if ct >= 2:
            SY.wait_ge(PO.sem, base_po2 + ct - 1)
        else:
            SY.wait_ge(PO.sem, base_po2)
        if ct == 21:
            V.wait_ge(PO.sem, PO.n)
            V.memset(wstage[32:64, slot * 1024: slot * 1024 + D], 0.0)
            DV.inc(V.memset(wstage[64:128, slot * 1024: slot * 1024 + D], 0.0))
            SY.wait_ge(DV.sem, DV.n)
            LD.inc(SY.dma_start(out=wstage[64:65, slot * 1024: slot * 1024 + D],
                                in_=E.f_b2[li][None, :]), 16)
        LD.inc(SY.dma_start(out=wstage[0:wr, slot * 1024: slot * 1024 + D],
                            in_=E.f_w2[li, ct * 128: ct * 128 + wr, :]), 16)
        GP.wait_ge(LD.sem, LD.n)
        PO.inc(GP.tensor_copy(w2b[:, ct * D:(ct + 1) * D],
                              wstage[:, slot * 1024: slot * 1024 + D]))
    w_ready = PO.n

    V.wait_ge(LD.sem, gcol_ld)
    DV.inc(V.tensor_copy(bias_a[:, 0:22], b1_cols[:, 0:22]))
    DV.inc(V.tensor_copy(bias_g[:, 0:22], b1_cols[:, 22:44]))
    bias_done = DV.n

    evac = {}
    rel = {}

    st_hln = {}
    st_hc = {}

    def hview(kt):
        if kt < 4:
            return x_fm[:, kt * 513 + 1: kt * 513 + 513]
        return x_fm[:, kt * 513 + 0: kt * 513 + 512]

    def _build(c):
        for tt in range(CT):
            i = c * CT + tt
            if ("xtm", i % 2) in rel:
                SY.wait_ge(DV.sem, rel[("xtm", i % 2)])
            LD.inc(SY.dma_start(out=x_tm[:, (i % 2) * D:(i % 2 + 1) * D],
                                in_=E.xA[i * 128:(i + 1) * 128, :]), 16)
            x_ld = LD.n
            V.wait_ge(LD.sem, x_ld)
            if ("xbf",) in rel:
                V.wait_ge(PE.sem, rel[("xbf",)])
            DV.inc(V.tensor_copy(x_bf[:], x_tm[:, (i % 2) * D:(i % 2 + 1) * D]))
            rel[("xtm", i % 2)] = DV.n
            xbf_dv = DV.n
            TE.wait_ge(DV.sem, xbf_dv)
            if c == 0 and tt == 0:
                TE.wait_ge(PO.sem, w_ready)
                TE.wait_ge(DV.sem, bias_done)
            if "ps_b" in evac:
                TE.wait_ge(DV.sem, evac["ps_b"])
            for ft in range(8):
                mm = TE.transpose(ps_b[:, ft * 128:(ft + 1) * 128],
                                  x_bf[:, ft * 128:(ft + 1) * 128], E.ident_sb[:])
                if ft == 7:
                    PE.inc(mm)
            rel[("xbf",)] = PE.n
            V.wait_ge(PE.sem, PE.n)
            if tt == 0 and ("xfm",) in rel:
                V.wait_ge(PE.sem, rel[("xfm",)])
                V.wait_ge(AC.sem, rel[("xfm_ac",)])
            DV.inc(V.tensor_copy(
                x_fm[:].rearrange("p (f t) -> p f t", t=513)
                [:, :, 1 + tt * 128: 1 + (tt + 1) * 128],
                ps_b[:].rearrange("p (f t) -> p f t", t=128)))
            evac["ps_b"] = DV.n
        if c == 0:
            DV.inc(V.memset(
                x_fm[:].rearrange("p (f t) -> p f t", t=513)[:, :, 0:1], 0.0))
        else:
            DV.inc(V.tensor_copy(
                x_fm[:].rearrange("p (f t) -> p f t", t=513)[:, :, 0:1],
                prevcol[:, :, None]))
        xfm_dv = DV.n

        TE.wait_ge(DV.sem, xfm_dv)
        if ("row",) in evac:
            TE.wait_ge(DV.sem, evac[("row",)])
        if ("y",) in evac:
            TE.wait_ge(DV.sem, evac[("y",)])
        for kt in range(8):
            mm = TE.matmul(ps_row[:, 0:512], E.ones_sb[:, 0:1], hview(kt),
                           start=(kt == 0), stop=(kt == 7))
            if kt == 7:
                PE.inc(mm)
        for kt in range(8):
            SC.wait_ge(DV.sem, xfm_dv)
            if kt >= 1:
                SC.wait_ge(PE.sem, PE.n)
            AC.inc(SC.activation(out=sqscr[:], in_=hview(kt), func=AF.Square))
            TE.wait_ge(AC.sem, AC.n)
            PE.inc(TE.matmul(ps_row[:, 512:1024], E.ones_sb[:, 0:1],
                             sqscr[:],
                             start=(kt == 0), stop=(kt == 7)))
        sq_pe = PE.n
        rel[("xfm",)] = sq_pe
        rel[("xfm_ac",)] = AC.n
        V.wait_ge(PE.sem, sq_pe)
        DV.inc(V.tensor_scalar_mul(rowbuf[0:1, 0:512], ps_row[:, 0:512], 1.0 / D))
        V.wait_ge(DV.sem, DV.n)
        DV.inc(V.tensor_scalar_mul(rowbuf[0:1, 512:1024], ps_row[:, 512:1024], 1.0 / D))
        evac[("row",)] = DV.n
        V.wait_ge(DV.sem, DV.n)
        DV.inc(V.tensor_mul(rowbuf[0:1, 1024:1536], rowbuf[0:1, 0:512], rowbuf[0:1, 0:512]))
        V.wait_ge(DV.sem, DV.n)
        DV.inc(V.tensor_sub(rowbuf[0:1, 512:1024], rowbuf[0:1, 512:1024],
                            rowbuf[0:1, 1024:1536]))
        V.wait_ge(DV.sem, DV.n)
        DV.inc(V.tensor_scalar_add(rowbuf[0:1, 512:1024], rowbuf[0:1, 512:1024], EPS))
        SC.wait_ge(DV.sem, DV.n)
        AC.inc(SC.activation(out=rowbuf[0:1, 1024:1536], in_=rowbuf[0:1, 512:1024],
                             func=AF.Sqrt))
        V.wait_ge(AC.sem, AC.n)
        DV.inc(V.reciprocal(rowbuf[0:1, 512:1024], rowbuf[0:1, 1024:1536]))
        V.wait_ge(DV.sem, DV.n)
        row_dv = DV.n
        TE.wait_ge(DV.sem, row_dv)
        if ("bc",) in evac:
            TE.wait_ge(DV.sem, evac[("bc",)])
        if ("za", 0) in evac:
            TE.wait_ge(DV.sem, evac[("za", 0)])
        PE.inc(TE.matmul(ps_bc[:, :], ones_f[0:1, :], rowbuf[0:1, 0:512],
                         start=True, stop=True))
        V.wait_ge(PE.sem, PE.n)
        DV.inc(V.tensor_copy(mu_b[:], ps_bc[:, :]))
        TE.wait_ge(DV.sem, DV.n)
        PE.inc(TE.matmul(ps_bc[:, :], ones_f[0:1, :], rowbuf[0:1, 512:1024],
                         start=True, stop=True))
        V.wait_ge(PE.sem, PE.n)
        DV.inc(V.tensor_copy(r_b[:], ps_bc[:, :]))
        evac[("bc",)] = DV.n
        V.wait_ge(DV.sem, DV.n)
        if ("hln",) in rel:
            V.wait_ge(PE.sem, rel[("hln",)])
        for kt in range(8):
            DV.inc(V.tensor_sub(t1buf[:], hview(kt), mu_b[:]))
            V.wait_ge(DV.sem, DV.n)
            DV.inc(V.tensor_mul(t1buf[:], t1buf[:], r_b[:]))
            V.wait_ge(DV.sem, DV.n)
            DV.inc(V.tensor_scalar(out=h_ln[:, kt * 512:(kt + 1) * 512],
                                   in0=t1buf[:],
                                   scalar1=g_cols[:, kt:kt + 1],
                                   scalar2=b_cols[:, kt:kt + 1],
                                   op0=ALU.mult, op1=ALU.add))
        DV.inc(V.tensor_copy(prevcol[:, :, None],
                             x_fm[:].rearrange("p (f t) -> p f t", t=513)[:, :, 512:513]))
        st_hln[c] = DV.n

    def _z(c):
        hln_dv = st_hln[c]
        for jc in range(22):
            wa = 128 if jc < 21 else HID - 21 * 128
            zs = jc % 2
            TE.wait_ge(DV.sem, hln_dv)
            if ("za", zs) in evac:
                TE.wait_ge(DV.sem, evac[("za", zs)])
            if ("zg", zs) in evac:
                TE.wait_ge(AC.sem, evac[("zg", zs)])
            if zs == 0 and ("bc",) in evac:
                TE.wait_ge(DV.sem, evac[("bc",)])
            for kt in range(8):
                TE.matmul(ps_za[zs][0:wa, :],
                          w1g[:, kt * H2 + jc * 128: kt * H2 + jc * 128 + wa],
                          h_ln[:, kt * 512:(kt + 1) * 512],
                          start=(kt == 0), stop=(kt == 7))
                mm = TE.matmul(ps_zg[zs][0:wa, :],
                               w1g[:, kt * H2 + HID + jc * 128: kt * H2 + HID + jc * 128 + wa],
                               h_ln[:, kt * 512:(kt + 1) * 512],
                               start=(kt == 0), stop=(kt == 7))
                if kt == 7:
                    PE.inc(mm)
            z_pe = PE.n
            SC.wait_ge(PE.sem, z_pe)
            if jc >= 2:
                SC.wait_ge(DV.sem, DV.n)
            AC.inc(SC.activation(out=gelu_sb[:, zs * 512:(zs + 1) * 512][0:wa, :],
                                 in_=ps_zg[zs][0:wa, :], func=AF.Gelu,
                                 bias=bias_g[0:wa, jc:jc + 1]))
            evac[("zg", zs)] = AC.n
            V.wait_ge(AC.sem, AC.n)
            if ("hc", jc) in rel:
                V.wait_ge(PE.sem, rel[("hc", jc)])
            DV.inc(V.scalar_tensor_tensor(
                out=hc[0:wa, jc * 512:(jc + 1) * 512],
                in0=ps_za[zs][0:wa, :], scalar=bias_a[0:wa, jc:jc + 1],
                in1=gelu_sb[:, zs * 512:(zs + 1) * 512][0:wa, :],
                op0=ALU.add, op1=ALU.mult))
            evac[("za", zs)] = DV.n
        rel[("hln",)] = PE.n
        st_hc[c] = DV.n

    def _y(c):
        nonlocal prev_sq6, prev_o6_ac, x1_ld
        hc_dv = st_hc[c]
        for tt in range(CT):
            i = c * CT + tt
            if ("xtm", i % 2) in rel:
                SY.wait_ge(DV.sem, rel[("xtm", i % 2)])
            LD.inc(SY.dma_start(out=x_tm[:, (i % 2) * D:(i % 2 + 1) * D],
                                in_=E.xA[i * 128:(i + 1) * 128, :]), 16)
            xr_ld = LD.n
            if fold6:
                if ("x1in",) in rel:
                    SY.wait_ge(DV.sem, rel[("x1in",)])
                if c == 0 and tt == 0:
                    SY.wait_ge(PO.sem, w_ready)
                LD.inc(SY.dma_start(out=x1_in[:],
                                    in_=E.x1k[i * 128:(i + 1) * 128, :]), 16)
                x1_ld = LD.n
            TE.wait_ge(DV.sem, hc_dv)
            if tt == 0 and ("row",) in evac:
                TE.wait_ge(DV.sem, evac[("row",)])
            if ("y",) in evac:
                TE.wait_ge(DV.sem, evac[("y",)])
            for jc in range(22):
                kr = 128 if jc < 21 else 65
                for nh in range(2):
                    mm = TE.matmul(ps_y[:, nh * 512:(nh + 1) * 512],
                                   hc[0:kr, jc * 512 + tt * 128: jc * 512 + (tt + 1) * 128],
                                   w2b[0:kr, jc * D + nh * 512: jc * D + (nh + 1) * 512],
                                   start=(jc == 0), stop=(jc == 21))
                    if jc == 21 and nh == 1:
                        PE.inc(mm)
            y_pe = PE.n
            if tt == CT - 1:
                for jc in range(22):
                    rel[("hc", jc)] = PE.n
            V.wait_ge(PE.sem, y_pe)
            V.wait_ge(LD.sem, xr_ld)
            if ("xout",) in rel:
                V.wait_ge(ST.sem, rel[("xout",)])
            if fold6 and prev_o6_ac:
                V.wait_ge(AC.sem, prev_o6_ac)
            DV.inc(V.tensor_add(x_out[:], x_tm[:, (i % 2) * D:(i % 2 + 1) * D], ps_y[:]))
            rel[("xtm", i % 2)] = DV.n
            evac[("y",)] = DV.n
            if not fold6:
                GP.wait_ge(DV.sem, DV.n)
                ST.inc(GP.dma_start(out=E.xA[i * 128:(i + 1) * 128, :], in_=x_out[:]), 16)
                rel[("xout",)] = ST.n
            else:
                xo_dv = DV.n
                SC.wait_ge(DV.sem, xo_dv)
                _ln_stats(E, x_out[:], stt6, sq6, D,
                          prev_guard=(DV.sem, prev_sq6) if prev_sq6 else None)
                prev_sq6 = DV.n
                SC.wait_ge(DV.sem, DV.n)
                if ("o6",) in rel:
                    SC.wait_ge(ST.sem, rel[("o6",)])
                AC.inc(SC.activation(out=o6[:], in_=x_out[:], func=AF.Identity,
                                     scale=stt6[:, 3:4], bias=stt6[:, 4:5]))
                prev_o6_ac = AC.n
                V.wait_ge(AC.sem, prev_o6_ac)
                V.wait_ge(LD.sem, x1_ld)
                DV.inc(V.tensor_add(o6[:], o6[:], x1_in[:]))
                rel[("x1in",)] = DV.n
                GP.wait_ge(DV.sem, DV.n)
                ST.inc(GP.dma_start(out=E.out[i * 128:(i + 1) * 128, :],
                                    in_=o6[:]), 16)
                rel[("o6",)] = ST.n

    _build(0)
    for c in range(NCH):
        _z(c)
        if c + 1 < NCH:
            _build(c + 1)
        _y(c)

    if fold6:
        GP.wait_ge(ST.sem, ST.n)

    dbg = getattr(E, "b_dbg", None)
    if dbg:
        tmp = x_tm
        row = 0
        for name in dbg:
            buf = {"hln": h_ln, "rowbuf": rowbuf, "mub": mu_b, "rb": r_b,
                   "hc": hc, "xfm": x_fm, "hc2": hc, "hc3": hc,
                   "w2b": w2b, "w2b2": w2b}[name]
            ncols = {"hln": 4096, "rowbuf": 1536, "mub": 512, "rb": 512,
                     "hc": 4096, "xfm": 4104, "hc2": 4096, "hc3": 4096,
                     "w2b": 4096, "w2b2": 4096}[name]
            off = {"hc2": 4096, "hc3": 8192, "w2b2": 18432}.get(name, 0)
            for blk in range(0, min(ncols, 4096), 1024):
                w_ = min(1024, ncols - blk)
                V.wait_ge(DV.sem, DV.n)
                V.wait_ge(ST.sem, ST.n)
                V.wait_ge(PO.sem, PO.n)
                V.wait_ge(AC.sem, AC.n)
                V.memset(tmp[:, 0:1024], 0.0)
                blk2 = blk + off
                w_ = min(1024, buf.shape[1] - blk2) if name in ("hc3",) else w_
                if w_ <= 0:
                    row += 1
                    continue
                DV.inc(V.tensor_copy(tmp[0:1, 0:w_] if name == "rowbuf" else tmp[:, 0:w_],
                                     buf[0:1, blk2:blk2 + w_] if name == "rowbuf" else buf[:, blk2:blk2 + w_]))
                GP.wait_ge(DV.sem, DV.n)
                ST.inc(GP.dma_start(out=E.out[row * 128:(row + 1) * 128, :], in_=tmp[:, 0:1024]), 16)
                GP.wait_ge(ST.sem, ST.n)
                row += 1
    pb.close()


def _emit_P6(E):
    nc = E.nc
    V, SC, TE, SY, GP = E.V, E.SC, E.TE, E.SY, E.GP
    LD, ST, PE, DV, AC, PO = E.LD, E.ST, E.PE, E.DV, E.AC, E.PO

    p6 = ExitStack()

    def sb(name, shape, dtype):
        return p6.enter_context(nc.sbuf_tensor(name, shape, dtype))

    _pass_barrier(E)
    xa_in = sb("p6_xa", [128, 2 * D], F32)
    x1_in = sb("p6_x1", [128, 2 * D], F32)
    o_sb = sb("p6_o", [128, 2 * D], F32)
    sq_scr = sb("p6_sq", [128, D], BF16)
    stt = sb("p6_stt", [128, 8], F32)

    SY.wait_ge(ST.sem, ST.n)
    free_po = {}
    st_after = {}
    prev_sq_guard = 0
    for i in range(NT):
        slot = i % 2
        if i >= 2:
            SY.wait_ge(PO.sem, free_po[i - 2])
        LD.inc(SY.dma_start(out=xa_in[:, slot * D:slot * D + D],
                            in_=E.xA[i * 128:(i + 1) * 128, :]), 16)
        LD.inc(SY.dma_start(out=x1_in[:, slot * D:slot * D + D],
                            in_=E.x1k[i * 128:(i + 1) * 128, :]), 16)
        x_ld = LD.n
        SC.wait_ge(LD.sem, x_ld)
        V.wait_ge(LD.sem, x_ld)
        _ln_stats(E, xa_in[:, slot * D:slot * D + D], stt, sq_scr, D,
                  prev_guard=(DV.sem, prev_sq_guard) if prev_sq_guard else None)
        prev_sq_guard = DV.n
        if i >= 2:
            V.wait_ge(ST.sem, st_after[i - 2])
        DV.inc(V.tensor_scalar(out=o_sb[:, slot * D:slot * D + D],
                               in0=xa_in[:, slot * D:slot * D + D],
                               scalar1=stt[:, 2:3], scalar2=stt[:, 3:4],
                               op0=ALU.subtract, op1=ALU.mult))
        GP.wait_ge(DV.sem, DV.n)
        GP.wait_ge(LD.sem, x_ld)
        PO.inc(GP.tensor_add(o_sb[:, slot * D:slot * D + D],
                             o_sb[:, slot * D:slot * D + D],
                             x1_in[:, slot * D:slot * D + D]))
        free_po[i] = PO.n
        ST.inc(GP.dma_start(out=E.out[i * 128:(i + 1) * 128, :],
                            in_=o_sb[:, slot * D:slot * D + D]), 16)
        st_after[i] = ST.n
    GP.wait_ge(ST.sem, ST.n)
    p6.close()


def kernel(L0_attn_x, L0_attn_Wq, L0_attn_Wk, L0_attn_Wv, L0_attn_Wo, L0_attn_rf,
           L0_ffn_wqkv, L0_ffn_q_scale, L0_ffn_k_scale, L0_ffn_wo, L0_ffn_ln_g,
           L0_ffn_ln_b, L0_ffn_w1, L0_ffn_b1, L0_ffn_w2, L0_ffn_b2,
           _passes=("P0", "P1", "A0", "B0", "A1", "B1"), _cores=8):
    consts = _host_consts()
    rf_dup = np.concatenate([np.asarray(L0_attn_rf), np.asarray(L0_attn_rf)], axis=0)
    rf_dup = np.ascontiguousarray(rf_dup, dtype=np.float32)

    def a(x):
        return np.ascontiguousarray(np.asarray(x), dtype=np.float32)

    common = dict(
        wq=a(L0_attn_Wq), wk=a(L0_attn_Wk), wv=a(L0_attn_Wv), wo_a=a(L0_attn_Wo),
        rf_dup=rf_dup,
        f_wqkv=a(L0_ffn_wqkv), f_qs=a(L0_ffn_q_scale), f_ks=a(L0_ffn_k_scale),
        f_wo=a(L0_ffn_wo), f_g=a(L0_ffn_ln_g), f_b=a(L0_ffn_ln_b),
        f_w1=a(L0_ffn_w1), f_b1=a(L0_ffn_b1), f_w2=a(L0_ffn_w2), f_b2=a(L0_ffn_b2),
        ident=consts["ident"], ones_bf=consts["ones_bf"], shift_tbl=consts["shift_tbl"],
        t1q=consts["t1q"], t2q=consts["t2q"], t1k=consts["t1k"], t2k=consts["t2k"],
        m_ge=consts["m_ge"], m_le=consts["m_le"],
    )
    xfull = np.asarray(L0_attn_x)
    in_maps = []
    for c in range(_cores):
        m = dict(common)
        m["x0"] = np.ascontiguousarray(xfull[c], dtype=np.float32)
        in_maps.append(m)

    nc = build_nc(passes=_passes)
    res = run_bass_kernel_spmd(nc, in_maps, core_ids=list(range(_cores)))
    return np.stack([r["out"] for r in res.results], axis=0)
